# version 09
# baseline (speedup 1.0000x reference)
import numpy as np
import ml_dtypes
import concourse.bass as bass
import concourse.mybir as mybir
from concourse.bass_types import AP
from concourse.bass_utils import run_bass_kernel_spmd

F32 = mybir.dt.float32
BF16 = mybir.dt.bfloat16
AF = mybir.ActivationFunctionType
ALU = mybir.AluOpType
AX = mybir.AxisListType

T = 2048
C = 1024
NT = 16
NCH = 8
SLAB_COLS = 2048

ENGINES = ("pe", "act", "dve", "pool", "sp")
_BLOCK_ATTR = {"pe": "tensor", "act": "scalar", "dve": "vector", "pool": "gpsimd", "sp": "sync"}


class Bld:
    def __init__(self, nc):
        self.nc = nc
        self.ops = {e: [] for e in ENGINES}
        self.res = {}
        self.sems = {}
        self.dma_counts = {}
        self.hiwater = {e: {} for e in ENGINES}
        self.token_init = {}

    def _need(self, eng, dep):
        if dep is None:
            return False
        if dep[0] == "e":
            return dep[2] > self.hiwater[eng].get(dep[1], -1)
        return dep[2] > self.hiwater[eng].get(("s", dep[1]), -1)

    def _res(self, t):
        r = self.res.get(t)
        if r is None:
            for pfx, pdeps in self.token_init.items():
                if t.startswith(pfx):
                    r = [None, list(pdeps)]
                    break
            else:
                r = [None, []]
            self.res[t] = r
        return r

    def emit(self, eng, fn, reads=(), writes=(), dma=None, tag=""):
        deps = []
        for t in reads:
            r = self._res(t)
            if self._need(eng, r[0]):
                deps.append(r[0])
        for t in writes:
            r = self._res(t)
            if self._need(eng, r[0]):
                deps.append(r[0])
            for rd in r[1]:
                if self._need(eng, rd):
                    deps.append(rd)
        best = {}
        for d in deps:
            k = (d[0], d[1]) if d[0] != "pd" else (d[0], d[1], d[2])
            if k not in best or d[0] == "pd" or d[2] > best[k][2]:
                best[k] = d
        deps = list(best.values())
        idx = len(self.ops[eng])
        op = {"fn": fn, "waits": deps, "inc": False, "dma": None, "tag": tag}
        self._mark(eng, deps)
        self.ops[eng].append(op)
        if dma is not None:
            D = 6 if eng == "pool" else 4
            cidx = self.dma_counts.get(("chan", dma), 0)
            self.dma_counts[("chan", dma)] = cidx + 1
            semname = f"{dma}.{cidx % D}"
            if semname not in self.sems:
                self.sems[semname] = self.nc.alloc_semaphore("dsem_" + semname)
                self.dma_counts[semname] = 0
            prev = self.dma_counts[semname]
            if prev and self._need(eng, ("s", semname, prev)):
                op["waits"].append(("s", semname, prev))
                self.hiwater[eng][("s", semname)] = prev
            self.dma_counts[semname] = prev + 16
            op["dma"] = semname
            my_dep = ("s", semname, prev + 16)
        else:
            my_dep = ("e", eng, idx)
        for t in writes:
            self.res[t] = [my_dep, []]
        for t in reads:
            r = self.res.setdefault(t, [None, []])
            r[1].append(my_dep)
        return my_dep

    def _mark(self, eng, deps):
        for d in deps:
            if d[0] == "e":
                self.ops[d[1]][d[2]]["inc"] = True
                self.hiwater[eng][d[1]] = max(self.hiwater[eng].get(d[1], -1), d[2])
            else:
                self.hiwater[eng][("s", d[1])] = max(self.hiwater[eng].get(("s", d[1]), -1), d[2])

    def barrier_channel(self, eng, ch):
        for i in range(8):
            semname = f"{ch}.{i}"
            if semname in self.sems:
                self.barrier_dep(eng, ("s", semname, self.dma_counts[semname]))

    def barrier_dep(self, eng, dep):
        if self._need(eng, dep):
            if dep[0] == "e":
                self.ops[dep[1]][dep[2]]["inc"] = True
                self.hiwater[eng][dep[1]] = max(self.hiwater[eng].get(dep[1], -1), dep[2])
            else:
                self.hiwater[eng][("s", dep[1])] = max(self.hiwater[eng].get(("s", dep[1]), -1), dep[2])
            self.ops[eng].append({"fn": None, "waits": [dep], "inc": False, "dma": None, "tag": "bar"})

    def finalize(self):
        nc = self.nc
        engsem = {e: nc.alloc_semaphore("esem_" + e) for e in ENGINES if self.ops[e]}
        prefix = {}
        for e in ENGINES:
            acc, pl = 0, []
            for op in self.ops[e]:
                if op["inc"]:
                    acc += 1
                pl.append(acc)
            prefix[e] = pl
        import contextlib
        with nc.allow_low_precision("intentional bf16 intermediates (tolerance-checked)"), \
             nc.Block() as block:
            for e in ENGINES:
                if not self.ops[e]:
                    continue
                def make_body(e):
                    def body(engobj):
                        for op in self.ops[e]:
                            for d in op["waits"]:
                                if d[0] == "e":
                                    engobj.wait_ge(engsem[d[1]], prefix[d[1]][d[2]])
                                else:
                                    engobj.wait_ge(self.sems[d[1]], d[2])
                            if op["fn"] is None:
                                continue
                            inst = op["fn"](engobj)
                            if op["dma"] is not None:
                                inst.then_inc(self.sems[op["dma"]], 16)
                            elif op["inc"]:
                                inst.then_inc(engsem[e], 1)
                    return body
                getattr(block, _BLOCK_ATTR[e])(make_body(e))


class Ctx:
    def __init__(self, nc, bld, n_pool=12):
        self.nc = nc
        self.bld = bld
        self.xs = [nc.alloc_sbuf_tensor(f"XR{i}", [128, SLAB_COLS], F32) for i in range(8)]
        self.lns = [nc.alloc_sbuf_tensor(f"LNF{i}", [128, SLAB_COLS], F32) for i in range(4)]
        self.pool_slabs = [nc.alloc_sbuf_tensor(f"PS{i}", [128, SLAB_COLS], F32) for i in range(n_pool)]
        self.pool_free = list(range(n_pool))
        self.epoch = [0] * n_pool
        self.used_toks = [set() for _ in range(n_pool)]
        self.pb = [nc.alloc_psum_tensor(f"PB{i}", [128, 512], F32) for i in range(8)]
        self.smalls = {}

    def alloc_slab(self):
        i = self.pool_free.pop(0)
        deps = []
        for t in self.used_toks[i]:
            r = self.bld.res.get(t)
            if r:
                if r[0]:
                    deps.append(r[0])
                deps.extend(r[1])
        self.epoch[i] += 1
        self.used_toks[i] = set()
        self.bld.token_init[f"PS{i}e{self.epoch[i]}."] = deps
        return i

    def free_slab(self, i):
        assert i not in self.pool_free
        self.pool_free.append(i)

    def slab(self, i):
        return self.pool_slabs[i]

    def stok(self, i, suffix=""):
        t = f"PS{i}e{self.epoch[i]}.{suffix}"
        self.used_toks[i].add(t)
        return t

    def small(self, name, rows, cols, dtype=F32):
        if name not in self.smalls:
            self.smalls[name] = self.nc.alloc_sbuf_tensor("sm_" + name, [rows, cols], dtype)
        return self.smalls[name]

    def x_tile(self, i):
        s = self.xs[i // 2]
        return s[:, (i % 2) * 1024:(i % 2) * 1024 + 1024], f"XR{i // 2}"

    def ln_chunk(self, ch):
        s = self.lns[ch // 2]
        v = s[:].bitcast(BF16)[:, (ch % 2) * 2048:(ch % 2) * 2048 + 2048]
        return v, f"LNF{ch // 2}"


def bf_view(slab_handle, c0, cols):
    return slab_handle[:].bitcast(BF16)[:, c0:c0 + cols]


def f_view(slab_handle, c0, cols):
    return slab_handle[:, c0:c0 + cols]


def emit_load_x(ctx, x_dram):
    b = ctx.bld
    for i in range(NT):
        v, tok = ctx.x_tile(i)
        b.emit("sp", lambda e, v=v, i=i: e.dma_start(out=v, in_=x_dram[i * 128:(i + 1) * 128, :]),
               reads=(), writes=(tok,), dma="xin")


def emit_store_x(ctx, x_out_dram):
    b = ctx.bld
    for i in range(NT):
        v, tok = ctx.x_tile(i)
        b.emit("sp", lambda e, v=v, i=i: e.dma_start(out=x_out_dram[i * 128:(i + 1) * 128, :], in_=v),
               reads=(tok,), writes=(), dma="xout")
    b.barrier_channel("sp", "xout")


def emit_ln(ctx, name):
    b = ctx.bld
    scr = ctx.alloc_slab()
    ident = ctx.small("identbf", 128, 128, BF16)
    cols = ctx.small(f"lncol_{name}", 128, 16, F32)
    lntm = ctx.alloc_slab()
    for i in range(NT):
        xv, xtok = ctx.x_tile(i)
        p = i % 2
        ssum = cols[:, 8 * p + 0:8 * p + 1]; ssq = cols[:, 8 * p + 1:8 * p + 2]
        mean = cols[:, 8 * p + 2:8 * p + 3]; msq = cols[:, 8 * p + 3:8 * p + 4]
        m2 = cols[:, 8 * p + 4:8 * p + 5]; var = cols[:, 8 * p + 5:8 * p + 6]
        std = cols[:, 8 * p + 6:8 * p + 7]; rstd = cols[:, 8 * p + 7:8 * p + 8]
        ct = f"lncol_{name}{p}"
        scrv = f_view(ctx.slab(scr), p * 1024, 1024)
        b.emit("dve", lambda e, xv=xv, o=scrv, a=ssq:
               e.scalar_tensor_tensor(o, xv, 1.0, xv, ALU.mult, ALU.mult, accum_out=a),
               reads=(xtok,), writes=(ctx.stok(scr, str(p)), ct + "q"), tag="ln_sq")
        b.emit("act", lambda e, xv=xv, o=scrv, a=ssum:
               e.activation(o, xv, AF.Copy, accum_out=a),
               reads=(xtok,), writes=(ctx.stok(scr, str(p)), ct + "s"), tag="ln_sum")
        b.emit("dve", lambda e, o=mean, i_=ssum: e.tensor_scalar_mul(o, i_, 1.0 / C),
               reads=(ct + "s",), writes=(ct + "m",))
        b.emit("dve", lambda e, o=msq, i_=ssq: e.tensor_scalar_mul(o, i_, 1.0 / C),
               reads=(ct + "q",), writes=(ct + "2",))
        b.emit("dve", lambda e, o=m2, a=mean: e.tensor_mul(o, a, a),
               reads=(ct + "m",), writes=(ct + "4",))
        b.emit("dve", lambda e, o=var, a=msq, c=m2:
               e.tensor_scalar(o, a, c, 1e-5, ALU.subtract, ALU.add),
               reads=(ct + "2", ct + "4"), writes=(ct + "v",))
        b.emit("act", lambda e, o=std, i_=var: e.activation(o, i_, AF.Sqrt),
               reads=(ct + "v",), writes=(ct + "d",))
        b.emit("dve", lambda e, o=rstd, i_=std: e.reciprocal(o, i_),
               reads=(ct + "d",), writes=(ct + "r",))
        ltv = bf_view(ctx.slab(lntm), p * 2048, 1024)
        b.emit("dve", lambda e, o=ltv, xv=xv, m=mean, r=rstd:
               e.tensor_scalar(o, xv, m, r, ALU.subtract, ALU.mult),
               reads=(xtok, ct + "m", ct + "r"), writes=(ctx.stok(lntm, str(p)),), tag="ln_tm")
        for chs in range(NCH):
            pbi = chs
            pbv = ctx.pb[pbi][:].bitcast(BF16)[:, 0:128]
            b.emit("pe", lambda e, o=pbv, i_=bf_view(ctx.slab(lntm), p * 2048 + chs * 128, 128), idm=ident[:]:
                   e.transpose(o, i_, idm),
                   reads=(ctx.stok(lntm, str(p)), "identbf"), writes=(f"PB{pbi}",), tag="ln_tr")
            lchv, lctok = ctx.ln_chunk(chs)
            dst = lchv[:, i * 128:(i + 1) * 128]
            if chs % 2 == 0:
                b.emit("dve", lambda e, o=dst, i_=pbv: e.tensor_copy(o, i_),
                       reads=(f"PB{pbi}",), writes=(lctok,), tag="ln_ev")
            else:
                b.emit("act", lambda e, o=dst, i_=pbv: e.copy(o, i_),
                       reads=(f"PB{pbi}",), writes=(lctok,), tag="ln_ev")
    ctx.free_slab(scr)
    ctx.free_slab(lntm)


def load_w_1024(ctx, dram, slabs, sem, cols=None):
    n = dram.shape[1] if cols is None else cols
    acc0 = big_wload(ctx, dram, slabs[0], sem, r0=0, nch=4, ncols=n)
    acc1 = big_wload(ctx, dram, slabs[1], sem, r0=512, nch=4, ncols=n)

    def acc(ci):
        return (acc0 if ci < 4 else acc1)(ci % 4)
    return acc


def emit_l0_attn(ctx, W):
    b = ctx.bld
    emit_ln(ctx, "ln0a")
    QT = 512
    ccol = {}
    for nm in ("w", "u", "mk", "mv", "mr"):
        t = ctx.small(f"l0a_{nm}", 128, 8, F32)
        b.emit("sp", lambda e, o=t[:], i_=W[nm]: e.dma_start(out=o, in_=i_),
               reads=(), writes=(f"l0a_{nm}",), dma="cis")
        ccol[nm] = t
    dcol = ctx.small("l0a_d", 128, 8, F32)
    eucol = ctx.small("l0a_eu", 128, 8, F32)
    tmpc = ctx.small("l0a_tc", 128, 8, F32)
    b.emit("act", lambda e: e.activation(tmpc[:], ccol["w"][:], AF.Exp),
           reads=("l0a_w",), writes=("l0a_tc",))
    b.emit("act", lambda e: e.activation(dcol[:], tmpc[:], AF.Exp, scale=-1.0),
           reads=("l0a_tc",), writes=("l0a_d",))
    b.emit("act", lambda e: e.activation(eucol[:], ccol["u"][:], AF.Exp),
           reads=("l0a_u",), writes=("l0a_eu",))
    for nm in ("mk", "mv", "mr"):
        t2 = ctx.small(f"l0a_om{nm}", 128, 8, F32)
        b.emit("dve", lambda e, o=t2[:], i_=ccol[nm][:]:
               e.tensor_scalar(o, i_, -1.0, -1.0, ALU.mult, ALU.subtract),
               reads=(f"l0a_{nm}",), writes=(f"l0a_om{nm}",))
    pc = ctx.small("l0a_Pc", 128, 8, F32)
    qc = ctx.small("l0a_Qc", 128, 8, F32)
    b.emit("dve", lambda e: e.memset(pc[:], 0.0), reads=(), writes=("l0a_Pc",))
    b.emit("dve", lambda e: e.memset(qc[:], 0.0), reads=(), writes=("l0a_Qc",))

    def emit_mix(dst_slab, key, prefix, t0, q):
        sl = ctx.slab(dst_slab); stok = ctx.stok(dst_slab)
        for ch in range(NCH):
            lch, lctok = ctx.ln_chunk(ch)
            dfull = bf_view(sl, ch * 512, 512)
            mc = ccol_all[key][:, ch:ch + 1]
            oc = ctx.smalls[f"{prefix}_om{key}"][:, ch:ch + 1]
            if q == 0:
                b.emit("dve", lambda e, o=bf_view(sl, ch * 512 + 1, 511), i_=lch[:, t0:t0 + 511], s=oc:
                       e.tensor_scalar(o, i_, s, None, ALU.mult),
                       reads=(lctok, f"{prefix}_om{key}"), writes=(stok,))
                b.emit("dve", lambda e, o=bf_view(sl, ch * 512, 1): e.memset(o, 0.0),
                       reads=(), writes=(stok,))
            else:
                b.emit("dve", lambda e, o=dfull, i_=lch[:, t0 - 1:t0 + 511], s=oc:
                       e.tensor_scalar(o, i_, s, None, ALU.mult),
                       reads=(lctok, f"{prefix}_om{key}"), writes=(stok,))
            b.emit("dve", lambda e, o=dfull, i_=lch[:, t0:t0 + 512], s=mc, i2=dfull:
                   e.scalar_tensor_tensor(o, i_, s, i2, ALU.mult, ALU.add),
                   reads=(lctok, f"{prefix}_{key}", stok), writes=(stok,))
    ccol_all = ccol

    for q in range(4):
        t0 = q * QT
        xmr = ctx.alloc_slab()
        emit_mix(xmr, "mr", "l0a", t0, q)
        wsl_a = [ctx.alloc_slab(), ctx.alloc_slab()]
        wr_acc = load_w_1024(ctx, W["wrT"], wsl_a, "wqa")
        sr_sl = [ctx.alloc_slab(), ctx.alloc_slab()]
        for co in range(NCH):
            pbr = ctx.pb[co % 2]; pbrt = f"PB{co % 2}"
            for ci in range(NCH):
                wv_, wtok = wr_acc(ci)
                b.emit("pe", lambda e, o=pbr[:], l=wv_[:, co * 128:co * 128 + 128],
                       r=bf_view(ctx.slab(xmr), ci * 512, 512), s=(ci == 0), p=(ci == 7):
                       e.matmul(o, l, r, start=s, stop=p),
                       reads=(wtok, ctx.stok(xmr)), writes=(pbrt,))
            srv = f_view(ctx.slab(sr_sl[co // 4]), (co % 4) * 512, 512)
            b.emit("act", lambda e, o=srv, i_=pbr[:]: e.activation(o, i_, AF.Sigmoid),
                   reads=(pbrt,), writes=(ctx.stok(sr_sl[co // 4]),))
        ctx.free_slab(xmr)
        xmk = ctx.alloc_slab(); xmv = ctx.alloc_slab()
        emit_mix(xmk, "mk", "l0a", t0, q)
        emit_mix(xmv, "mv", "l0a", t0, q)
        wsl_b = [ctx.alloc_slab(), ctx.alloc_slab()]
        wk_acc = load_w_1024(ctx, W["wkT"], wsl_a, "wqa")
        wv_acc = load_w_1024(ctx, W["wvT"], wsl_b, "wqb")
        kvsl = ctx.alloc_slab()
        wkvA = ctx.alloc_slab(); wkvB = ctx.alloc_slab()
        rwk = ctx.alloc_slab()
        for co in range(NCH):
            h = co % 2
            kview = f_view(ctx.slab(kvsl), h * 1024, 512)
            vview = f_view(ctx.slab(kvsl), h * 1024 + 512, 512)
            kvt = ctx.stok(kvsl, str(h))
            srv = f_view(ctx.slab(sr_sl[co // 4]), (co % 4) * 512, 512)
            srt = ctx.stok(sr_sl[co // 4])
            pbk, pbv = ctx.pb[h], ctx.pb[2 + h]
            pbkt, pbvt = f"PB{h}", f"PB{2 + h}"
            for ci in range(NCH):
                wv_, wtok = wk_acc(ci)
                b.emit("pe", lambda e, o=pbk[:], l=wv_[:, co * 128:co * 128 + 128],
                       r=bf_view(ctx.slab(xmk), ci * 512, 512), s=(ci == 0), p=(ci == 7):
                       e.matmul(o, l, r, start=s, stop=p),
                       reads=(wtok, ctx.stok(xmk)), writes=(pbkt,))
            for ci in range(NCH):
                wv_, wtok = wv_acc(ci)
                b.emit("pe", lambda e, o=pbv[:], l=wv_[:, co * 128:co * 128 + 128],
                       r=bf_view(ctx.slab(xmv), ci * 512, 512), s=(ci == 0), p=(ci == 7):
                       e.matmul(o, l, r, start=s, stop=p),
                       reads=(wtok, ctx.stok(xmv)), writes=(pbvt,))
            b.emit("dve", lambda e, o=kview, i_=pbk[:]: e.tensor_copy(o, i_),
                   reads=(pbkt,), writes=(kvt,))
            b.emit("dve", lambda e, o=vview, i_=pbv[:]: e.tensor_copy(o, i_),
                   reads=(pbvt,), writes=(kvt,))
            sA = ctx.slab(wkvA)
            ek = f_view(sA, 0, 512); ekv = f_view(sA, 512, 512)
            P = f_view(sA, 1024, 512); Q = f_view(sA, 1536, 512)
            Psh = f_view(ctx.slab(wkvB), h * 512, 512)
            at = ctx.stok(wkvA)
            sht = ctx.stok(wkvB, str(h))
            dbro = dcol[:, co:co + 1].to_broadcast([128, 512])
            eucc = eucol[:, co:co + 1]
            pcc = pc[:, co:co + 1]; qcc = qc[:, co:co + 1]
            b.emit("act", lambda e, o=ek, i_=kview: e.activation(o, i_, AF.Exp),
                   reads=(kvt,), writes=(at,), tag="wkv_ek")
            b.emit("dve", lambda e, o=ekv, a=ek, c=vview: e.tensor_mul(o, a, c),
                   reads=(at, kvt), writes=(at,))
            b.emit("dve", lambda e, o=P, d=dbro, x_=ekv, init=pcc:
                   e.tensor_tensor_scan(o, d, x_, init, ALU.mult, ALU.add),
                   reads=(at, "l0a_d", "l0a_Pc"), writes=(at,), tag="scanP")
            b.emit("dve", lambda e, o=Psh[:, 0:1], i_=pcc: e.tensor_copy(o, i_),
                   reads=("l0a_Pc",), writes=(sht,))
            b.emit("dve", lambda e, o=Psh[:, 1:512], i_=P[:, 0:511]: e.tensor_copy(o, i_),
                   reads=(at,), writes=(sht,))
            b.emit("dve", lambda e, o=pcc, i_=P[:, 511:512]: e.tensor_copy(o, i_),
                   reads=(at,), writes=("l0a_Pc",))
            b.emit("dve", lambda e, o=P, a=ek, s=eucc, c=vview:
                   e.scalar_tensor_tensor(o, a, s, c, ALU.mult, ALU.mult),
                   reads=(at, "l0a_eu", kvt), writes=(at,), tag="eukv")
            b.emit("dve", lambda e, o=P, a=P, c=Psh: e.tensor_add(o, a, c),
                   reads=(at, sht), writes=(at,), tag="num")
            b.emit("dve", lambda e, o=Q, d=dbro, x_=ek, init=qcc:
                   e.tensor_tensor_scan(o, d, x_, init, ALU.mult, ALU.add),
                   reads=(at, "l0a_d", "l0a_Qc"), writes=(at,), tag="scanQ")
            b.emit("dve", lambda e, o=Psh[:, 0:1], i_=qcc: e.tensor_copy(o, i_),
                   reads=("l0a_Qc",), writes=(sht,))
            b.emit("dve", lambda e, o=Psh[:, 1:512], i_=Q[:, 0:511]: e.tensor_copy(o, i_),
                   reads=(at,), writes=(sht,))
            b.emit("dve", lambda e, o=qcc, i_=Q[:, 511:512]: e.tensor_copy(o, i_),
                   reads=(at,), writes=("l0a_Qc",))
            b.emit("dve", lambda e, o=Q, a=ek, s=eucc, c=Psh:
                   e.scalar_tensor_tensor(o, a, s, c, ALU.mult, ALU.add),
                   reads=(at, "l0a_eu", sht), writes=(at,), tag="den")
            b.emit("dve", lambda e, o=ekv, i_=Q: e.reciprocal(o, i_),
                   reads=(at,), writes=(at,))
            b.emit("dve", lambda e, o=ek, a=P, c=ekv: e.tensor_mul(o, a, c),
                   reads=(at,), writes=(at,), tag="wkvout")
            rw = bf_view(ctx.slab(rwk), co * 512, 512)
            b.emit("dve", lambda e, o=rw, a=ek, c=srv: e.tensor_mul(o, a, c),
                   reads=(at, srt), writes=(ctx.stok(rwk),), tag="rwkv")
        wo_acc = load_w_1024(ctx, W["woT"], wsl_b, "wqb")
        for tt in range(4):
            ti = (t0 + tt * 128) // 128
            xv, xtok = ctx.x_tile(ti)
            for half in range(2):
                pbo = ctx.pb[4 + (tt * 2 + half) % 2]
                pbot = f"PB{4 + (tt * 2 + half) % 2}"
                for co in range(NCH):
                    wv_, wtok = wo_acc(co)
                    b.emit("pe", lambda e, o=pbo[:], l=bf_view(ctx.slab(rwk), co * 512 + tt * 128, 128),
                           r=wv_[:, half * 512:half * 512 + 512], s=(co == 0), p=(co == 7):
                           e.matmul(o, l, r, start=s, stop=p),
                           reads=(wtok, ctx.stok(rwk)), writes=(pbot,))
                dst = xv[:, half * 512:half * 512 + 512]
                b.emit("dve", lambda e, o=dst, a=dst, c=pbo[:]: e.tensor_add(o, a, c),
                       reads=(pbot, xtok), writes=(xtok,), tag="resid")
        for s in (wsl_a[0], wsl_a[1], wsl_b[0], wsl_b[1], sr_sl[0], sr_sl[1],
                  xmk, xmv, kvsl, wkvA, wkvB, rwk):
            ctx.free_slab(s)


def emit_l0_ffn(ctx, W):
    b = ctx.bld
    emit_ln(ctx, "ln0f")
    QT = 512
    ccol = {}
    for nm in ("mur", "muk"):
        t = ctx.small(f"l0f_{nm}", 128, 8, F32)
        b.emit("sp", lambda e, o=t[:], i_=W[nm]: e.dma_start(out=o, in_=i_),
               reads=(), writes=(f"l0f_{nm}",), dma="cis")
        ccol[nm] = t
        t2 = ctx.small(f"l0f_om{nm}", 128, 8, F32)
        b.emit("dve", lambda e, o=t2[:], i_=t[:]:
               e.tensor_scalar(o, i_, -1.0, -1.0, ALU.mult, ALU.subtract),
               reads=(f"l0f_{nm}",), writes=(f"l0f_om{nm}",))

    def emit_mix_f(dst_slab, key, t0, q):
        sl = ctx.slab(dst_slab); stok = ctx.stok(dst_slab)
        for ch in range(NCH):
            lch, lctok = ctx.ln_chunk(ch)
            dfull = bf_view(sl, ch * 512, 512)
            mc = ccol[key][:, ch:ch + 1]
            oc = ctx.smalls[f"l0f_om{key}"][:, ch:ch + 1]
            if q == 0:
                b.emit("dve", lambda e, o=bf_view(sl, ch * 512 + 1, 511), i_=lch[:, t0:t0 + 511], s=oc:
                       e.tensor_scalar(o, i_, s, None, ALU.mult),
                       reads=(lctok, f"l0f_om{key}"), writes=(stok,))
                b.emit("dve", lambda e, o=bf_view(sl, ch * 512, 1): e.memset(o, 0.0),
                       reads=(), writes=(stok,))
            else:
                b.emit("dve", lambda e, o=dfull, i_=lch[:, t0 - 1:t0 + 511], s=oc:
                       e.tensor_scalar(o, i_, s, None, ALU.mult),
                       reads=(lctok, f"l0f_om{key}"), writes=(stok,))
            b.emit("dve", lambda e, o=dfull, i_=lch[:, t0:t0 + 512], s=mc, i2=dfull:
                   e.scalar_tensor_tensor(o, i_, s, i2, ALU.mult, ALU.add),
                   reads=(lctok, f"l0f_{key}", stok), writes=(stok,))

    for q in range(4):
        t0 = q * QT
        xr = ctx.alloc_slab(); xk = ctx.alloc_slab()
        emit_mix_f(xr, "mur", t0, q)
        emit_mix_f(xk, "muk", t0, q)
        wsl_a = [ctx.alloc_slab(), ctx.alloc_slab()]
        wk_acc = load_w_1024(ctx, W["wkT"], wsl_a, "wqa")
        kact = ctx.alloc_slab()
        ksc = ctx.alloc_slab()
        for co in range(NCH):
            pbk = ctx.pb[co % 2]; pbkt = f"PB{co % 2}"
            for ci in range(NCH):
                wv_, wtok = wk_acc(ci)
                b.emit("pe", lambda e, o=pbk[:], l=wv_[:, co * 128:co * 128 + 128],
                       r=bf_view(ctx.slab(xk), ci * 512, 512), s=(ci == 0), p=(ci == 7):
                       e.matmul(o, l, r, start=s, stop=p),
                       reads=(wtok, ctx.stok(xk)), writes=(pbkt,))
            krel = f_view(ctx.slab(ksc), (co % 2) * 512, 512)
            b.emit("act", lambda e, o=krel, i_=pbk[:]: e.activation(o, i_, AF.Relu),
                   reads=(pbkt,), writes=(ctx.stok(ksc, str(co % 2)),))
            b.emit("act", lambda e, o=bf_view(ctx.slab(kact), co * 512, 512), i_=krel:
                   e.activation(o, i_, AF.Square),
                   reads=(ctx.stok(ksc, str(co % 2)),), writes=(ctx.stok(kact),))
        ctx.free_slab(xk)
        wr_acc = load_w_1024(ctx, W["wrT"], wsl_a, "wqa")
        rtm = ctx.alloc_slab()
        for tt in range(4):
            for half in range(2):
                pbr = ctx.pb[2 + (tt * 2 + half) % 2]; pbrt = f"PB{2 + (tt * 2 + half) % 2}"
                for ci in range(NCH):
                    wv_, wtok = wr_acc(ci)
                    b.emit("pe", lambda e, o=pbr[:], l=bf_view(ctx.slab(xr), ci * 512 + tt * 128, 128),
                           r=wv_[:, half * 512:half * 512 + 512], s=(ci == 0), p=(ci == 7):
                           e.matmul(o, l, r, start=s, stop=p),
                           reads=(wtok, ctx.stok(xr)), writes=(pbrt,))
                b.emit("act", lambda e, o=bf_view(ctx.slab(rtm), tt * 1024 + half * 512, 512), i_=pbr[:]:
                       e.activation(o, i_, AF.Sigmoid),
                       reads=(pbrt,), writes=(ctx.stok(rtm),))
        ctx.free_slab(xr)
        wv_acc2 = load_w_1024(ctx, W["wvT"], wsl_a, "wqa")
        for tt in range(4):
            ti = (t0 + tt * 128) // 128
            xv, xtok = ctx.x_tile(ti)
            for half in range(2):
                pbo = ctx.pb[4 + (tt * 2 + half) % 2]; pbot = f"PB{4 + (tt * 2 + half) % 2}"
                for ci in range(NCH):
                    wv_, wtok = wv_acc2(ci)
                    b.emit("pe", lambda e, o=pbo[:], l=bf_view(ctx.slab(kact), ci * 512 + tt * 128, 128),
                           r=wv_[:, half * 512:half * 512 + 512], s=(ci == 0), p=(ci == 7):
                           e.matmul(o, l, r, start=s, stop=p),
                           reads=(wtok, ctx.stok(kact)), writes=(pbot,))
                tmp = f_view(ctx.slab(ksc), 1024 + ((tt * 2 + half) % 2) * 512, 512)
                tmpt = ctx.stok(ksc, "t" + str((tt * 2 + half) % 2))
                b.emit("dve", lambda e, o=tmp, a=pbo[:],
                       c=bf_view(ctx.slab(rtm), tt * 1024 + half * 512, 512):
                       e.tensor_mul(o, a, c),
                       reads=(pbot, ctx.stok(rtm)), writes=(tmpt,))
                dst = xv[:, half * 512:half * 512 + 512]
                b.emit("dve", lambda e, o=dst, a=dst, c=tmp: e.tensor_add(o, a, c),
                       reads=(tmpt, xtok), writes=(xtok,), tag="resid")
        for s in (kact, ksc, rtm, wsl_a[0], wsl_a[1]):
            ctx.free_slab(s)


def wload_cols_big(ctx, dram, slab, slaboff, col0, ncols, sem, tokname):
    b = ctx.bld
    t = dram.tensor
    C = t.shape[-1]
    srcap = AP(t, col0, [[C, 128], [128 * C, 8], [1, ncols]])
    dst = bf_view(ctx.slab(slab), slaboff, 8 * ncols).rearrange("p (c n) -> p c n", c=8)
    tok = ctx.stok(slab, tokname)
    b.emit("pool", lambda e, o=dst, i_=srcap: e.dma_start(out=o, in_=i_),
           reads=(), writes=(tok,), dma=sem)
    return tok


def emit_attn_planB(ctx, W, name, rope=False):
    import os
    NG = int(os.environ.get("ATTN_G", "4"))
    NH = int(os.environ.get("ATTN_H", "4"))
    NQC = int(os.environ.get("ATTN_QC", "4"))
    b = ctx.bld
    emit_ln(ctx, "ln_" + name)
    sc_col = ctx.small("attn_scale", 128, 1, F32)
    bi_col = ctx.small("attn_bias", 128, 1, F32)
    ones64 = ctx.small("ones64bf", 1, 64, BF16)
    b.emit("dve", lambda e: e.memset(sc_col[:], 0.125), reads=(), writes=("attn_scale",))
    b.emit("dve", lambda e: e.memset(bi_col[:], -16.0), reads=(), writes=("attn_bias",))
    b.emit("dve", lambda e: e.memset(ones64[:], 1.0), reads=(), writes=("ones64bf",))
    csl = ctx.alloc_slab()
    trimask = bf_view(ctx.slab(csl), 0, 2048)
    tmtok = ctx.stok(csl, "tm")
    tm = np.zeros((128, 2048), np.float32)
    for j in range(4):
        kk = np.arange(128)[:, None]
        qq = np.arange(512)[None, :]
        tm[:, j * 512:(j + 1) * 512] = (kk <= qq - 128 * j)
    tmi = ctx.nc.inline_tensor(tm.astype(ml_dtypes.bfloat16), "trimask_c_" + name)
    b.emit("sp", lambda e: e.dma_start(out=trimask, in_=tmi.ap()),
           reads=(), writes=(tmtok,), dma="cis")
    if rope:
        half = 32
        inv_freq = (10000.0 ** (-np.arange(half, dtype=np.float64) * 2.0 / 64))
        ang = np.arange(T, dtype=np.float64)[None, :] * inv_freq[:, None]
        cos64 = np.concatenate([np.cos(ang), np.cos(ang)], 0).astype(np.float32)
        sin64 = np.concatenate([np.sin(ang), np.sin(ang)], 0).astype(np.float32)
        cosb = bf_view(ctx.slab(csl), 2048, 2048)[0:64, :]
        cstk = ctx.stok(csl, "cs")
        ci_ = ctx.nc.inline_tensor(cos64.astype(ml_dtypes.bfloat16), "ropecos_c")
        si_ = ctx.nc.inline_tensor(sin64.astype(ml_dtypes.bfloat16), "ropesin_c")
        b.emit("sp", lambda e, o=cosb: e.dma_start(out=o, in_=ci_.ap()), reads=(), writes=(cstk,), dma="cis")
        rope_sl = {}

        def rview(key, c0, cols, scratch=False):
            r0 = 64 if scratch else 0
            return bf_view(ctx.slab(rope_sl[key]), c0, cols)[r0:r0 + 64, :]

        def rtok(key, scratch=False):
            return ctx.stok(rope_sl[key], "s" if scratch else "d")

    rbc = ctx.small("attn_rbc", 64, 512, F32)
    rden = ctx.small("attn_rden", 1, 512, BF16)

    for g in range(NG):
        wsl = ctx.alloc_slab()
        wvsl = ctx.alloc_slab()

        tkq = wload_cols_big(ctx, W["wqT"], wsl, 0, 256 * g, 256, "wqk", "wq")
        tkk = wload_cols_big(ctx, W["wkT"], wsl, 2048, 256 * g, 256, "wqk", "wk")

        def qk_dst(ci, off):
            return (bf_view(ctx.slab(wsl), off + ci * 256, 256), tkq if off == 0 else tkk)
        tkv = wload_cols_big(ctx, W["wvT"], wvsl, 0, 256 * g, 256, "wvo", "wv")

        def wv_dst(ci):
            return (bf_view(ctx.slab(wvsl), ci * 256, 256), tkv)
        woacc = big_wload(ctx, W["woT"], wvsl, "wvo", r0=256 * g, nch=2, ncols=1024, tokname="wo", slaboff=2048)
        tko = ctx.stok(wvsl, "wo")

        ptsl = [ctx.alloc_slab(), ctx.alloc_slab()]
        zsl = ctx.alloc_slab()
        if not rope:
            qsl = ctx.alloc_slab()
            ksl = ctx.alloc_slab()
        else:
            for kk_ in ("qE", "qO", "kE", "kO"):
                rope_sl[kk_] = ctx.alloc_slab()
            sinb = bf_view(ctx.slab(ptsl[0]), 0, 2048)[0:64, :]
            sintok = ctx.stok(ptsl[0], "sin")
            b.emit("sp", lambda e, o=sinb: e.dma_start(out=o, in_=si_.ap()),
                   reads=(), writes=(sintok,), dma="cis")
        vsl = [ctx.alloc_slab(), ctx.alloc_slab()]

        def proj_qk(which, off, dst_view_fn, dst_tok_fn):
            for p in range(2):
                for ci in range(8):
                    wv_, wtok = qk_dst(ci, off)
                    lch, lctok = ctx.ln_chunk(ci)
                    for tq in range(4):
                        b.emit("pe", lambda e, o=ctx.pb[4 + tq][:], l=wv_[:, p * 128:p * 128 + 128],
                               r=lch[:, tq * 512:(tq + 1) * 512], s=(ci == 0), pp=(ci == 7):
                               e.matmul(o, l, r, start=s, stop=pp),
                               reads=(wtok, lctok), writes=(f"PB{4 + tq}",))
                for tq in range(4):
                    dst_view_fn(which, p, tq, ctx.pb[4 + tq], f"PB{4 + tq}")

        if not rope:
            def evac_qk(which, p, tq, pbq, pbt):
                sl = qsl if which == "q" else ksl
                dst = bf_view(ctx.slab(sl), p * 2048 + tq * 512, 512)
                b.emit("dve" if tq % 2 == 0 else "act",
                       lambda e, o=dst, i_=pbq[:], tt=tq % 2: (e.tensor_copy(o, i_) if tt == 0 else e.copy(o, i_)),
                       reads=(pbt,), writes=(ctx.stok(sl, str(p)),))
            proj_qk("q", 0, evac_qk, None)
            proj_qk("k", 2048, evac_qk, None)

            def q_view(h, c0, cols):
                return bf_view(ctx.slab(qsl), (h // 2) * 2048 + c0, cols)[(h % 2) * 64:(h % 2) * 64 + 64, :], ctx.stok(qsl, str(h // 2))

            def k_view(h, c0, cols):
                return bf_view(ctx.slab(ksl), (h // 2) * 2048 + c0, cols)[(h % 2) * 64:(h % 2) * 64 + 64, :], ctx.stok(ksl, str(h // 2))
        else:
            def evac_rope(which, p, tq, pbq, pbt):
                ek, ok_ = which + "E", which + "O"
                dste = rview(ek, p * T + tq * 512, 512)
                dsto = rview(ok_, p * T + tq * 512, 512)
                b.emit("dve", lambda e, o=dste, i_=pbq[0:64, :]: e.tensor_copy(o, i_),
                       reads=(pbt,), writes=(rtok(ek),))
                b.emit("act", lambda e, o=dsto, i_=pbq[64:128, :]: e.copy(o, i_),
                       reads=(pbt,), writes=(rtok(ok_),))
            proj_qk("q", 0, evac_rope, None)
            proj_qk("k", 2048, evac_rope, None)
            for which in ("q", "k"):
                ek, ok_ = which + "E", which + "O"
                for p in range(2):
                    E = rview(ek, p * T, T)
                    O = rview(ok_, p * T, T)
                    t1 = rview(ek, 0, T, scratch=True)
                    t2 = rview(ek, T, T, scratch=True)
                    t3 = rview(ok_, 0, T, scratch=True)
                    t4 = rview(ok_, T, T, scratch=True)
                    s1t = rtok(ek, True) + "a"; s2t = rtok(ek, True) + "b"
                    s3t = rtok(ok_, True) + "a"; s4t = rtok(ok_, True) + "b"
                    b.emit("dve", lambda e, o=t1, a=E, c=cosb: e.tensor_mul(o, a, c),
                           reads=(rtok(ek), cstk), writes=(s1t,))
                    b.emit("dve", lambda e, o=t2, a=O, c=sinb: e.tensor_mul(o, a, c),
                           reads=(rtok(ok_), sintok), writes=(s2t,))
                    b.emit("dve", lambda e, o=t3, a=E, c=sinb: e.tensor_mul(o, a, c),
                           reads=(rtok(ek), sintok), writes=(s3t,))
                    b.emit("dve", lambda e, o=t4, a=O, c=cosb: e.tensor_mul(o, a, c),
                           reads=(rtok(ok_), cstk), writes=(s4t,))
                    b.emit("dve", lambda e, o=E, a=t1, c=t2: e.tensor_sub(o, a, c),
                           reads=(s1t, s2t), writes=(rtok(ek),))
                    b.emit("dve", lambda e, o=O, a=t3, c=t4: e.tensor_add(o, a, c),
                           reads=(s3t, s4t), writes=(rtok(ok_),))

            def rope_qk_views(which, h, c0, cols):
                ek, ok_ = which + "E", which + "O"
                p, r = h // 2, h % 2
                ev = rview(ek, p * T + c0, cols)[32 * r:32 * r + 32, :]
                ov = rview(ok_, p * T + c0, cols)[32 * r:32 * r + 32, :]
                return (ev, rtok(ek)), (ov, rtok(ok_))

        for ti in range(NT):
            pbv = ctx.pb[7]
            for ci in range(8):
                wv_, wtok = wv_dst(ci)
                lch, lctok = ctx.ln_chunk(ci)
                b.emit("pe", lambda e, o=pbv[:, 0:256], l=lch[:, ti * 128:(ti + 1) * 128],
                       r=wv_, s=(ci == 0), pp=(ci == 7): e.matmul(o, l, r, start=s, stop=pp),
                       reads=(wtok, lctok), writes=("PB7",))
            sl = vsl[ti // 8]
            base = (ti % 8) * 260
            vtok = ctx.stok(sl, f"v{ti % 8}")
            for hh in range(4):
                dst = bf_view(ctx.slab(sl), base + hh * 65, 64)
                b.emit("dve" if hh % 2 == 0 else "act",
                       lambda e, o=dst, i_=pbv[:, hh * 64:(hh + 1) * 64], hh=hh:
                       (e.tensor_copy(o, i_) if hh % 2 == 0 else e.copy(o, i_)),
                       reads=("PB7",), writes=(vtok,))
                b.emit("dve", lambda e, o=bf_view(ctx.slab(sl), base + hh * 65 + 64, 1): e.memset(o, 1.0),
                       reads=(), writes=(vtok,))

        def v_aug(h, kt):
            ti = kt
            sl = vsl[ti // 8]
            return bf_view(ctx.slab(sl), (ti % 8) * 260 + h * 65, 65), ctx.stok(sl, f"v{ti % 8}")

        for fc_ in range(2):
            b.emit("dve", lambda e, o=bf_view(ctx.slab(zsl), fc_ * 2048, 2048): e.memset(o, 0.0),
                   reads=(), writes=(ctx.stok(zsl, str(fc_)),))
        for h in range(NH):
            habs = 4 * g + h
            for qc in range(NQC):
                q0 = qc * 512
                nkt = 4 * (qc + 1)
                pt_toks = []
                for kt in range(nkt):
                    pbS = ctx.pb[kt % 2]
                    pbSt = f"PB{kt % 2}"
                    if not rope:
                        qv, qvt = q_view(h, q0, 512)
                        kv, kvt_ = k_view(h, kt * 128, 128)
                        b.emit("pe", lambda e, o=pbS[:], l=kv, r=qv:
                               e.matmul(o, l, r, start=True, stop=True),
                               reads=(kvt_, qvt), writes=(pbSt,))
                    else:
                        (qev, qet), (qov, qot) = rope_qk_views("q", h, q0, 512)
                        (kev, ket), (kov, kot) = rope_qk_views("k", h, kt * 128, 128)
                        b.emit("pe", lambda e, o=pbS[:], l=kev, r=qev:
                               e.matmul(o, l, r, start=True, stop=False),
                               reads=(ket, qet), writes=(pbSt,))
                        b.emit("pe", lambda e, o=pbS[:], l=kov, r=qov:
                               e.matmul(o, l, r, start=False, stop=True),
                               reads=(kot, qot), writes=(pbSt,))
                    pt = bf_view(ctx.slab(ptsl[kt // 8]), (kt % 8) * 512, 512)
                    ptt = ctx.stok(ptsl[kt // 8], f"p{kt % 8}")
                    b.emit("act", lambda e, o=pt, i_=pbS[:]:
                           e.activation(o, i_, AF.Exp, bias=bi_col[:], scale=sc_col[:]),
                           reads=(pbSt, "attn_scale", "attn_bias"), writes=(ptt,))
                    d0 = kt * 128 - q0
                    if d0 >= 0:
                        ms = trimask[:, (d0 // 128) * 512:(d0 // 128) * 512 + 512]
                        b.emit("dve", lambda e, o=pt, a=pt, c=ms: e.tensor_mul(o, a, c),
                               reads=(ptt, tmtok), writes=(ptt,))
                    pt_toks.append((pt, ptt))
                pbZ = ctx.pb[2 + qc % 2]
                pbZt = f"PB{2 + qc % 2}"
                for kt in range(nkt):
                    vv, vt = v_aug(h, kt)
                    pt, ptt = pt_toks[kt]
                    b.emit("pe", lambda e, o=pbZ[0:65, :], l=vv, r=pt, s=(kt == 0), pp=(kt == nkt - 1):
                           e.matmul(o, l, r, start=s, stop=pp),
                           reads=(vt, ptt), writes=(pbZt,))
                b.emit("dve", lambda e, o=rden[:], i_=pbZ[64:65, :]: e.reciprocal(o, i_),
                       reads=(pbZt,), writes=("attn_rden",))
                pbB = ctx.pb[4]
                b.emit("pe", lambda e, o=pbB[0:64, :], l=ones64[:], r=rden[:]:
                       e.matmul(o, l, r, start=True, stop=True),
                       reads=("ones64bf", "attn_rden"), writes=("PB4",))
                b.emit("act", lambda e, o=rbc[:], i_=pbB[0:64, :]: e.copy(o, i_),
                       reads=("PB4",), writes=("attn_rbc",))
                zv = bf_view(ctx.slab(zsl), (h // 2) * 2048 + q0, 512)[(h % 2) * 64:(h % 2) * 64 + 64, :]
                b.emit("dve", lambda e, o=zv, a=pbZ[0:64, :], c=rbc[:]: e.tensor_mul(o, a, c),
                       reads=(pbZt, "attn_rbc"), writes=(ctx.stok(zsl, str(h // 2)),))
        for ti in range(NT):
            xv, xtok = ctx.x_tile(ti)
            for fc in range(2):
                lhsT = bf_view(ctx.slab(zsl), fc * 2048 + ti * 128, 128)
                for half in range(2):
                    rhs = bf_view(ctx.slab(wvsl), 2048 + fc * 1024 + half * 512, 512)
                    b.emit("pe", lambda e, o=ctx.pb[5 + half][:], l=lhsT, r=rhs, s=(fc == 0), pp=(fc == 1):
                           e.matmul(o, l, r, start=s, stop=pp),
                           reads=(ctx.stok(zsl, str(fc)), tko), writes=(f"PB{5 + half}",))
            for half in range(2):
                dst = xv[:, half * 512:half * 512 + 512]
                b.emit("dve", lambda e, o=dst, a=dst, c=ctx.pb[5 + half][:]: e.tensor_add(o, a, c),
                       reads=(f"PB{5 + half}", xtok), writes=(xtok,), tag="resid")
        for s in [wsl, wvsl, vsl[0], vsl[1], ptsl[0], ptsl[1], zsl] + (list(rope_sl.values()) if rope else [qsl, ksl]):
            ctx.free_slab(s)
    ctx.free_slab(csl)


def emit_l2_attn(ctx, W):
    emit_attn_planB(ctx, W, "l2a", rope=False)


def emit_l3_attn(ctx, W):
    emit_attn_planB(ctx, W, "l3a", rope=True)




def big_wload(ctx, dram_ap, slab, sem, r0=0, c0=0, nch=8, ncols=1024, ee=None,
              tokname="w", slaboff=0):
    b = ctx.bld
    t = dram_ap.tensor
    shp = list(t.shape)
    C = shp[-1]
    base = r0 * C + c0
    if ee is not None:
        base += ee * shp[-2] * C
    srcap = AP(t, base, [[C, 128], [128 * C, nch], [1, ncols]])
    dst = bf_view(ctx.slab(slab), slaboff, nch * ncols).rearrange("p (c n) -> p c n", c=nch)
    tok = ctx.stok(slab, tokname)
    b.emit("pool", lambda e, o=dst, i_=srcap: e.dma_start(out=o, in_=i_),
           reads=(), writes=(tok,), dma=sem)

    def acc(ci):
        return bf_view(ctx.slab(slab), slaboff + ci * ncols, ncols), tok
    return acc


def cload(ctx, dram_ap, dst_view, dsttok, nrows, ncols, stage_slab, stage_off=0):
    b = ctx.bld
    sv = f_view(ctx.slab(stage_slab), stage_off, ncols)[0:nrows, :]
    stok = ctx.stok(stage_slab, f"cl{stage_off}")
    b.emit("sp", lambda e, o=sv, i_=dram_ap: e.dma_start(out=o, in_=i_),
           reads=(), writes=(stok,), dma="cis")
    b.emit("dve", lambda e, o=dst_view, i_=sv: e.tensor_copy(o, i_),
           reads=(stok,), writes=(dsttok,))


RSTR = 3073
PSTR = 3072


def emit_l1_attn(ctx, W):
    b = ctx.bld
    nc = ctx.nc
    emit_ln(ctx, "ln_l1a")
    rbufs = [nc.dram_tensor(f"l1a_rbuf{p}", [T * RSTR + 8], BF16, kind="Internal") for p in range(2)]
    pbufs = [nc.dram_tensor(f"l1a_pbuf{p}", [T * PSTR + 8], BF16, kind="Internal") for p in range(2)]
    sc_col = ctx.small("attn_scale", 128, 1, F32)
    bi_col = ctx.small("attn_bias", 128, 1, F32)
    b.emit("dve", lambda e: e.memset(sc_col[:], 0.125), reads=(), writes=("attn_scale",))
    b.emit("dve", lambda e: e.memset(bi_col[:], -16.0), reads=(), writes=("attn_bias",))
    padsl = ctx.alloc_slab()
    zt = bf_view(ctx.slab(padsl), 0, 1024)
    ng = bf_view(ctx.slab(padsl), 1024, 1024)
    zto = ctx.stok(padsl, "z")
    ngo = ctx.stok(padsl, "n")
    b.emit("dve", lambda e, o=zt: e.memset(o, 0.0), reads=(), writes=(zto,))
    b.emit("dve", lambda e, o=ng: e.memset(o, -1e5), reads=(), writes=(ngo,))
    for p in range(2):
        for q0 in range(0, T, 128):
            b.emit("pool", lambda e, o=AP(rbufs[p], q0 * RSTR, [[RSTR, 128], [1, 1024]]), i_=zt:
                   e.dma_start(out=o, in_=i_), reads=(zto,), writes=(f"rpad{p}",), dma="skwp")
            b.emit("pool", lambda e, o=AP(rbufs[p], q0 * RSTR + 2049, [[RSTR, 128], [1, 1024]]), i_=ng:
                   e.dma_start(out=o, in_=i_), reads=(ngo,), writes=(f"rpad{p}",), dma="skwp")
            b.emit("pool", lambda e, o=AP(pbufs[p], q0 * PSTR, [[PSTR, 128], [1, 1024]]), i_=zt:
                   e.dma_start(out=o, in_=i_), reads=(zto,), writes=(f"ppad{p}",), dma="skwp")
    ctx.free_slab(padsl)
    cstage = ctx.alloc_slab()
    l1csl = ctx.alloc_slab()
    pek_full = bf_view(ctx.slab(l1csl), 0, 2048)
    pek = pek_full[:, 0:1025]
    pktok = ctx.stok(l1csl, "pek")
    cload(ctx, W["pekT"], pek[0:64, :], pktok, 64, 1025, cstage)
    b.emit("dve", lambda e, o=pek[64:128, :], i_=pek[0:64, :]: e.tensor_copy(o, i_),
           reads=(pktok,), writes=(pktok,))
    peva = bf_view(ctx.slab(l1csl), 2048, 520)
    pvtok = ctx.stok(l1csl, "peva")
    for jc in range(8):
        cload(ctx, W["pev_aug"][jc * 128:(jc + 1) * 128, :], peva[:, jc * 65:(jc + 1) * 65],
              pvtok, 128, 65, cstage, stage_off=1025 + 80 * (jc % 4))
    pev0 = ctx.small("l1a_pev0", 1, 64, BF16)
    cload(ctx, W["pev0"], pev0[:], "l1a_pev0", 1, 64, cstage, stage_off=1500)
    ctx.free_slab(cstage)
    r0all = ctx.small("l1a_r0", 128, 16, F32)
    dencol = ctx.small("l1a_den", 128, 4, F32)

    for g in range(8):
        wsl = ctx.alloc_slab()
        qksl = ctx.alloc_slab()
        vsl_z = ctx.alloc_slab()
        ssl = [ctx.alloc_slab(), ctx.alloc_slab()]
        rbsl = ctx.alloc_slab()
        ppsl = ctx.alloc_slab()
        stsl = ctx.alloc_slab()
        tq_ = wload_cols_big(ctx, W["wqT"], wsl, 0, g * 128, 128, "wqk", "wq")
        tk_ = wload_cols_big(ctx, W["wkT"], wsl, 1024, g * 128, 128, "wqk", "wk")
        tv_ = wload_cols_big(ctx, W["wvT"], wsl, 2048, g * 128, 128, "wqk", "wv")
        big_wload(ctx, W["woT"], wsl, "wqk", r0=g * 128, nch=1, ncols=1024, tokname="wo", slaboff=3072)
        l1a_wtok = {0: tq_, 1024: tk_, 2048: tv_}
        for which, off in (("q", 0), ("k", 1024)):
            for ci in range(8):
                lch, lctok = ctx.ln_chunk(ci)
                wv_ = bf_view(ctx.slab(wsl), off + ci * 128, 128)
                for tq in range(4):
                    b.emit("pe", lambda e, o=ctx.pb[4 + tq][:], l=wv_, r=lch[:, tq * 512:(tq + 1) * 512],
                           s=(ci == 0), pp=(ci == 7): e.matmul(o, l, r, start=s, stop=pp),
                           reads=(l1a_wtok[off], lctok), writes=(f"PB{4 + tq}",))
            for tq in range(4):
                dst = bf_view(ctx.slab(qksl), (0 if which == "q" else 2048) + tq * 512, 512)
                b.emit("dve" if tq % 2 == 0 else "act",
                       lambda e, o=dst, i_=ctx.pb[4 + tq][:], m=tq % 2: (e.tensor_copy(o, i_) if m == 0 else e.copy(o, i_)),
                       reads=(f"PB{4 + tq}",), writes=(ctx.stok(qksl, which),))
        for ti in range(NT):
            pbv = ctx.pb[7]
            for ci in range(8):
                lch, lctok = ctx.ln_chunk(ci)
                wv_ = bf_view(ctx.slab(wsl), 2048 + ci * 128, 128)
                b.emit("pe", lambda e, o=pbv[:, 0:128], l=lch[:, ti * 128:(ti + 1) * 128], r=wv_,
                       s=(ci == 0), pp=(ci == 7): e.matmul(o, l, r, start=s, stop=pp),
                       reads=(l1a_wtok[2048], lctok), writes=("PB7",))
            b.emit("dve", lambda e, o=bf_view(ctx.slab(vsl_z), ti * 128, 128), i_=pbv[:, 0:128]:
                   e.tensor_copy(o, i_),
                   reads=("PB7",), writes=(ctx.stok(vsl_z, "v"),))

        for hh in range(2):
            h = 2 * g + hh
            hp = h % 2
            rbuf, pbuf = rbufs[hp], pbufs[hp]
            qv_all = bf_view(ctx.slab(qksl), 0, 2048)[hh * 64:hh * 64 + 64, :]
            kv_all = bf_view(ctx.slab(qksl), 2048, 2048)[hh * 64:hh * 64 + 64, :]
            qtok, ktok = ctx.stok(qksl, "q"), ctx.stok(qksl, "k")
            tidx = 0

            def em_transpose(dst_sv, dst_tok, src_ap, read_toks):
                nonlocal tidx
                eng = "act" if tidx % 2 == 0 else "sp"
                ch = "trp" if tidx % 2 == 0 else "trps"
                tidx += 1
                b.emit(eng, lambda e, o=dst_sv, i_=src_ap: e.dma_start_transpose(o, i_),
                       reads=read_toks, writes=(dst_tok,), dma=ch)

            def phase_R(qt):
                q0 = qt * 128
                lhsT = qv_all[:, q0:q0 + 128]
                for part, (c0, ncol, pbi) in enumerate(((0, 512, 5), (512, 512, 6), (1024, 1, 4))):
                    pbr = ctx.pb[pbi]
                    b.emit("pe", lambda e, o=pbr[:, 0:ncol], l=lhsT,
                           r=pek[hh * 64:hh * 64 + 64, c0:c0 + ncol]:
                           e.matmul(o, l, r, start=True, stop=True),
                           reads=(qtok, pktok), writes=(f"PB{pbi}",))
                r0c = r0all[:, qt:qt + 1]
                b.emit("dve", lambda e, o=r0c, i_=ctx.pb[5][:, 0:1]: e.tensor_copy(o, i_),
                       reads=("PB5",), writes=(f"l1a_r0.{qt}",))
                rpar = qt % 2
                rview = bf_view(ctx.slab(rbsl), rpar * 1026, 1025)
                rtok = ctx.stok(rbsl, f"r{rpar}")
                for part, (c0, ncol, pbi) in enumerate(((0, 512, 5), (512, 512, 6), (1024, 1, 4))):
                    b.emit("dve", lambda e, o=rview[:, c0:c0 + ncol], i_=ctx.pb[pbi][:, 0:ncol], s=r0c:
                           e.tensor_scalar(o, i_, s, None, ALU.subtract),
                           reads=(f"PB{pbi}", f"l1a_r0.{qt}"), writes=(rtok,))
                b.emit("pool", lambda e, o=AP(rbuf, q0 * RSTR + 1024, [[RSTR, 128], [1, 1025]]),
                       i_=rview:
                       e.dma_start(out=o, in_=i_), reads=(rtok,), writes=(f"rbR{hp}.{qt}",), dma="skwp")

            def phase_SP(qt):
                q0 = qt * 128
                nck = (q0 + 128 + 511) // 512
                klen = nck * 512
                bsb = bf_view(ctx.slab(rbsl), 2048, klen)
                btok = ctx.stok(rbsl, "b")
                b.emit("sp", lambda e, o=bsb, i_=AP(rbuf, q0 * PSTR + 2048, [[PSTR, 128], [1, klen]]):
                       e.dma_start(out=o, in_=i_),
                       reads=(f"rbR{hp}.{qt}", f"rpad{hp}"), writes=(btok,), dma="skw")
                spar = qt % 2
                stok_s = ctx.stok(ssl[spar], "s")
                for kc in range(nck):
                    pbS = ctx.pb[kc % 2]
                    pbSt = f"PB{kc % 2}"
                    b.emit("pe", lambda e, o=pbS[:], l=qv_all[:, q0:q0 + 128],
                           r=kv_all[:, kc * 512:(kc + 1) * 512]:
                           e.matmul(o, l, r, start=True, stop=True),
                           reads=(qtok, ktok), writes=(pbSt,))
                    sdst = f_view(ctx.slab(ssl[spar]), kc * 512, 512)
                    b.emit("dve", lambda e, o=sdst, i_=pbS[:], s=r0all[:, qt:qt + 1],
                           i2=bsb[:, kc * 512:(kc + 1) * 512]:
                           e.scalar_tensor_tensor(o, i_, s, i2, ALU.add, ALU.add),
                           reads=(pbSt, f"l1a_r0.{qt}", btok), writes=(stok_s,))
                dn = dencol[:, spar * 2:spar * 2 + 1]
                pe_sb = bf_view(ctx.slab(ppsl), 0, klen)
                b.emit("act", lambda e, o=pe_sb, i_=f_view(ctx.slab(ssl[spar]), 0, klen), a=dn:
                       e.activation(o, i_, AF.Exp, bias=bi_col[:], scale=sc_col[:], accum_out=a),
                       reads=(stok_s, "attn_scale", "attn_bias"),
                       writes=(ctx.stok(ppsl, "e"), f"l1a_den{spar}"))
                rd = dencol[:, spar * 2 + 1:spar * 2 + 2]
                b.emit("dve", lambda e, o=rd, i_=dn: e.reciprocal(o, i_),
                       reads=(f"l1a_den{spar}",), writes=(f"l1a_denr{spar}",))
                pn_sb = bf_view(ctx.slab(ppsl), 2048, klen)
                b.emit("dve", lambda e, o=pn_sb, i_=pe_sb, s=rd:
                       e.tensor_scalar(o, i_, s, None, ALU.mult),
                       reads=(ctx.stok(ppsl, "e"), f"l1a_denr{spar}"), writes=(ctx.stok(ppsl, "n"),))
                b.emit("pool", lambda e, o=AP(pbuf, q0 * PSTR + 1024, [[PSTR, 128], [1, klen]]), i_=pn_sb:
                       e.dma_start(out=o, in_=i_),
                       reads=(ctx.stok(ppsl, "n"),), writes=(f"pbR{hp}.{qt}",), dma="skwp")

            def phase_AV(qc):
                q0 = qc * 512
                nkc = 4 * (qc + 1)
                pbZ = ctx.pb[2]
                pbZt = "PB2"
                ptoks = tuple(f"pbR{hp}.{4 * qc + j}" for j in range(4)) + (f"ppad{hp}",)

                def em_pev(jc, s_, pp_):
                    j0 = 1 + jc * 128
                    sv = bf_view(ctx.slab(stsl), (jc % 4) * 512, 512)
                    stok_ = ctx.stok(stsl, f"t{jc % 4}")
                    em_transpose(sv, stok_, AP(pbuf, q0 * RSTR + j0, [[RSTR, 512], [1, 128]]), ptoks)
                    b.emit("pe", lambda e, o=pbZ[0:65, :], l=peva[:, jc * 65:(jc + 1) * 65], r=sv,
                           s=s_, pp=pp_:
                           e.matmul(o, l, r, start=s_, stop=pp_),
                           reads=(pvtok, stok_), writes=(pbZt,))
                em_pev(0, True, False)
                for kc in range(nkc):
                    sv = bf_view(ctx.slab(stsl), (kc % 4) * 512, 512)
                    stok_ = ctx.stok(stsl, f"t{kc % 4}")
                    em_transpose(sv, stok_, AP(pbuf, q0 * PSTR + 1024 + kc * 128, [[PSTR, 512], [1, 128]]), ptoks)
                    vv = bf_view(ctx.slab(vsl_z), kc * 128, 128)[:, hh * 64:hh * 64 + 64]
                    b.emit("pe", lambda e, o=pbZ[0:64, :], l=vv, r=sv:
                           e.matmul(o, l, r, start=False, stop=False),
                           reads=(ctx.stok(vsl_z, "v"), stok_), writes=(pbZt,))
                for jc in range(1, 8):
                    em_pev(jc, False, jc == 7)
                w0r = ctx.small("l1a_w0", 1, 512, BF16)
                b.emit("dve", lambda e, o=w0r[:], i_=pbZ[64:65, :]:
                       e.tensor_scalar(o, i_, -1.0, 1.0, ALU.mult, ALU.add),
                       reads=(pbZt,), writes=("l1a_w0",))
                pbB = ctx.pb[3]
                b.emit("pe", lambda e, o=pbB[0:64, :], l=pev0[:], r=w0r[:]:
                       e.matmul(o, l, r, start=True, stop=True),
                       reads=("l1a_pev0", "l1a_w0"), writes=("PB3",))
                zdst = bf_view(ctx.slab(vsl_z), 2048 + q0, 512)[hh * 64:hh * 64 + 64, :]
                ztmp = bf_view(ctx.slab(stsl), 2048, 512)[0:64, :]
                b.emit("dve", lambda e, o=ztmp, i_=pbZ[0:64, :]: e.tensor_copy(o, i_),
                       reads=(pbZt,), writes=(ctx.stok(stsl, "zt"),))
                b.emit("dve", lambda e, o=zdst, a=ztmp, c=pbB[0:64, :]: e.tensor_add(o, a, c),
                       reads=(ctx.stok(stsl, "zt"), "PB3"), writes=(ctx.stok(vsl_z, "z"),))

            for qt in range(NT):
                phase_R(qt)
                phase_SP(qt)
                if qt % 4 == 3:
                    phase_AV(qt // 4)
        for ti in range(NT):
            xv, xtok = ctx.x_tile(ti)
            lhsT = bf_view(ctx.slab(vsl_z), 2048 + ti * 128, 128)
            for half in range(2):
                b.emit("pe", lambda e, o=ctx.pb[5 + half][:], l=lhsT,
                       r=bf_view(ctx.slab(wsl), 3072 + half * 512, 512):
                       e.matmul(o, l, r, start=True, stop=True),
                       reads=(ctx.stok(vsl_z, "z"), ctx.stok(wsl, "wo")), writes=(f"PB{5 + half}",))
            for half in range(2):
                dst = xv[:, half * 512:half * 512 + 512]
                b.emit("dve", lambda e, o=dst, a=dst, c=ctx.pb[5 + half][:]: e.tensor_add(o, a, c),
                       reads=(f"PB{5 + half}", xtok), writes=(xtok,), tag="resid")
        for s in (wsl, qksl, vsl_z, ssl[0], ssl[1], rbsl, ppsl, stsl):
            ctx.free_slab(s)
    ctx.free_slab(l1csl)


def wunit_load(ctx, dram3, ee, r0, c0, nrows, ncols, sl_pair, sem):
    acc0 = big_wload(ctx, dram3, sl_pair[0], sem, r0=r0, c0=c0, nch=4, ncols=ncols, ee=ee)
    acc1 = big_wload(ctx, dram3, sl_pair[1], sem, r0=r0 + 512, c0=c0, nch=4, ncols=ncols, ee=ee)

    def acc(ci):
        return (acc0 if ci < 4 else acc1)(ci % 4)
    return acc


def emit_cem(ctx, cten, ctok, E, cem, cemtok):
    b = ctx.bld
    identf = ctx.small("identf32", 128, 128, F32)
    for ti in range(NT):
        b.emit("pe", lambda e, o=ctx.pb[7][0:E, 0:128], i_=cten[:, ti * E:(ti + 1) * E], idm=identf[:]:
               e.transpose(o, i_, idm),
               reads=(ctok, "identf32"), writes=("PB7",), tag="ctr")
        b.emit("dve", lambda e, o=cem[:, ti * 128:(ti + 1) * 128] if cem.shape[0] == E else cem[0:E, ti * 128:(ti + 1) * 128], i_=ctx.pb[7][0:E, 0:128]:
               e.tensor_copy(o, i_),
               reads=("PB7",), writes=(cemtok,))


def emit_bcast_row(ctx, src_row, src_tok, dst, dst_tok, n):
    b = ctx.bld
    ones1 = ctx.small("ones1bf", 1, 128, BF16)
    if "ones1bf" not in ctx.bld.res:
        b.emit("dve", lambda e: e.memset(ones1[:], 1.0), reads=(), writes=("ones1bf",))
    b.emit("pe", lambda e, o=ctx.pb[6][:, 0:n], l=ones1[:], r=src_row:
           e.matmul(o, l, r, start=True, stop=True),
           reads=("ones1bf", src_tok), writes=("PB6",))
    b.emit("dve", lambda e, o=dst, i_=ctx.pb[6][:, 0:n]: e.tensor_copy(o, i_),
           reads=("PB6",), writes=(dst_tok,))


def emit_l1_ffn(ctx, W):
    b = ctx.bld
    emit_ln(ctx, "ln_l1f")
    E = 16
    cstage = ctx.alloc_slab()
    fcsl = ctx.alloc_slab()
    rw = ctx.small("l1f_rw", 128, 128, BF16)
    for ci in range(8):
        cload(ctx, W["router_w"][ci * 128:(ci + 1) * 128, :], rw[:, ci * 16:(ci + 1) * 16],
              "l1f_rw", 128, 16, cstage, stage_off=16 * (ci % 4))
    rbv = ctx.small("l1f_rb", 1, 16, BF16)
    cload(ctx, W["router_b"], rbv[:], "l1f_rb", 1, 16, cstage, stage_off=80)
    rb_bc = ctx.small("l1f_rbbc", 128, 16, F32)
    emit_bcast_row(ctx, rbv[:], "l1f_rb", rb_bc[:], "l1f_rbbc", 16)
    cten = ctx.small("l1f_c", 128, NT * E, F32)
    rcol = ctx.small("l1f_rcol", 128, 8, F32)
    top8 = ctx.small("l1f_top8", 128, 8, F32)
    lsc = ctx.small("l1f_lsc", 128, 2 * E, F32)
    for ti in range(NT):
        pbr = ctx.pb[6]
        for ci in range(8):
            lch, lctok = ctx.ln_chunk(ci)
            b.emit("pe", lambda e, o=pbr[:, 0:16], l=lch[:, ti * 128:(ti + 1) * 128],
                   r=rw[:, ci * 16:(ci + 1) * 16], s=(ci == 0), pp=(ci == 7):
                   e.matmul(o, l, r, start=s, stop=pp),
                   reads=(lctok, "l1f_rw"), writes=("PB6",))
        lg = lsc[:, 0:16]; ex = lsc[:, 16:32]
        ct = "l1f_lsc"
        b.emit("dve", lambda e, o=lg, a=pbr[:, 0:16], c=rb_bc[:]: e.tensor_add(o, a, c),
               reads=("PB6", "l1f_rbbc"), writes=(ct,))
        nm = rcol[:, 0:1]; sm = rcol[:, 1:2]; rs = rcol[:, 2:3]; th = rcol[:, 3:4]
        b.emit("dve", lambda e, o=nm, i_=lg: e.reduce_max(o, i_, axis=AX.X, negate=True),
               reads=(ct,), writes=("l1f_rcol_n",))
        b.emit("act", lambda e, o=ex, i_=lg, bi=nm: e.activation(o, i_, AF.Exp, bias=bi),
               reads=(ct, "l1f_rcol_n"), writes=(ct + "e",))
        b.emit("dve", lambda e, o=top8[:], i_=ex: e.max(out=o, in_=i_),
               reads=(ct + "e",), writes=("l1f_top8",))
        b.emit("dve", lambda e, o=th, i_=top8[:, 3:4]: e.tensor_copy(o, i_),
               reads=("l1f_top8",), writes=("l1f_rcol_t",))
        cv = cten[:, ti * E:(ti + 1) * E]
        b.emit("dve", lambda e, o=cv, i_=ex, s=th: e.tensor_scalar(o, i_, s, None, ALU.is_ge),
               reads=(ct + "e", "l1f_rcol_t"), writes=("l1f_c",))
        b.emit("dve", lambda e, o=cv, a=cv, c=ex: e.tensor_mul(o, a, c),
               reads=("l1f_c", ct + "e"), writes=("l1f_c",))
        b.emit("dve", lambda e, o=sm, i_=cv: e.reduce_sum(o, i_, axis=AX.X),
               reads=("l1f_c",), writes=("l1f_rcol_s",))
        b.emit("dve", lambda e, o=rs, i_=sm: e.reciprocal(o, i_),
               reads=("l1f_rcol_s",), writes=("l1f_rcol_r",))
        b.emit("dve", lambda e, o=cv, i_=cv, s=rs: e.tensor_scalar(o, i_, s, None, ALU.mult),
               reads=("l1f_c", "l1f_rcol_r"), writes=("l1f_c",))
    cem = bf_view(ctx.slab(fcsl), 0, 2048)[0:E, :]
    cemtok = ctx.stok(fcsl, "cem")
    emit_cem(ctx, cten, "l1f_c", E, cem, cemtok)
    b1t = ctx.small("l1f_b1", 128, E * 16, F32)
    for ee in range(E):
        b.emit("sp", lambda e, o=b1t[:, ee * 16:(ee + 1) * 16], i_=W["b1p"][ee]:
               e.dma_start(out=o, in_=i_), reads=(), writes=("l1f_b1",), dma="cis")
    b2t = bf_view(ctx.slab(fcsl), 2048, 1024)[0:E, :]
    b2tok = ctx.stok(fcsl, "b2")
    cload(ctx, W["b2"], b2t, b2tok, E, 1024, cstage, stage_off=512)
    ctx.free_slab(cstage)

    HT = 1024
    usl1 = [[ctx.alloc_slab(), ctx.alloc_slab()] for _ in range(2)]
    usl2 = [[ctx.alloc_slab(), ctx.alloc_slab()] for _ in range(2)]
    hsl = [ctx.alloc_slab(), ctx.alloc_slab()]
    for th in range(2):
        t0 = th * HT
        for ee in range(E):
            for ih in range(2):
                par = (ee * 2 + ih) % 2
                w1acc = wunit_load(ctx, W["w1"], ee, 0, ih * 1024, 1024, 1024, usl1[par], "wmo")
                w2acc = wunit_load(ctx, W["w2"], ee, ih * 1024, 0, 1024, 1024, usl2[par], "wmo")
                for icp in range(4):
                    for ci in range(8):
                        lch, lctok = ctx.ln_chunk(ci)
                        for ics in range(2):
                            ic = icp * 2 + ics
                            wv_, wtok = w1acc(ci)
                            for tc2 in range(2):
                                pbm = ctx.pb[ics * 2 + tc2]
                                b.emit("pe", lambda e, o=pbm[:], l=wv_[:, ic * 128:ic * 128 + 128],
                                       r=lch[:, t0 + tc2 * 512:t0 + tc2 * 512 + 512],
                                       s=(ci == 0), pp=(ci == 7):
                                       e.matmul(o, l, r, start=s, stop=pp),
                                       reads=(wtok, lctok), writes=(f"PB{ics * 2 + tc2}",))
                    for ics in range(2):
                        ic = icp * 2 + ics
                        iglob = ih * 8 + ic
                        for tc2 in range(2):
                            hv = bf_view(ctx.slab(hsl[ic // 4]), (ic % 4) * 1024 + tc2 * 512, 512)
                            htok = ctx.stok(hsl[ic // 4], f"h{ic % 4}")
                            b.emit("act", lambda e, o=hv, i_=ctx.pb[ics * 2 + tc2][:],
                                   bi=b1t[:, ee * 16 + iglob:ee * 16 + iglob + 1]:
                                   e.activation(o, i_, AF.Gelu, bias=bi),
                                   reads=(f"PB{ics * 2 + tc2}", "l1f_b1"), writes=(htok,))
                for tt in range(8):
                    ti = (t0 + tt * 128) // 128
                    xv, xtok = ctx.x_tile(ti)
                    bb0 = 4 + (tt % 2) * 2
                    for ic in range(8):
                        wv_, wtok = w2acc(ic)
                        hv = bf_view(ctx.slab(hsl[ic // 4]), (ic % 4) * 1024 + tt * 128, 128)
                        htok = ctx.stok(hsl[ic // 4], f"h{ic % 4}")
                        for hh in range(2):
                            pbo = ctx.pb[bb0 + hh]
                            b.emit("pe", lambda e, o=pbo[:], l=hv, r=wv_[:, hh * 512:hh * 512 + 512],
                                   s=(ic == 0), pp=(ic == 7):
                                   e.matmul(o, l, r, start=s, stop=pp),
                                   reads=(wtok, htok), writes=(f"PB{bb0 + hh}",))
                    for hh in range(2):
                        ccol = cten[:, ti * E + ee:ti * E + ee + 1]
                        dst = xv[:, hh * 512:hh * 512 + 512]
                        b.emit("dve", lambda e, o=dst, i_=ctx.pb[bb0 + hh][:], s=ccol, i2=dst:
                               e.scalar_tensor_tensor(o, i_, s, i2, ALU.mult, ALU.add),
                               reads=(f"PB{bb0 + hh}", "l1f_c", xtok), writes=(xtok,), tag="moeacc")
    for ti in range(NT):
        xv, xtok = ctx.x_tile(ti)
        for hh in range(2):
            pbo = ctx.pb[4 + hh % 2]
            pbot = f"PB{4 + hh % 2}"
            b.emit("pe", lambda e, o=pbo[:], l=cem[:, ti * 128:(ti + 1) * 128],
                   r=b2t[:, hh * 512:hh * 512 + 512]:
                   e.matmul(o, l, r, start=True, stop=True),
                   reads=(cemtok, b2tok), writes=(pbot,))
            dst = xv[:, hh * 512:hh * 512 + 512]
            b.emit("dve", lambda e, o=dst, a=dst, c=pbo[:]: e.tensor_add(o, a, c),
                   reads=(pbot, xtok), writes=(xtok,), tag="resid")
    for pair in usl1 + usl2:
        ctx.free_slab(pair[0]); ctx.free_slab(pair[1])
    ctx.free_slab(hsl[0]); ctx.free_slab(hsl[1])
    ctx.free_slab(fcsl)


def emit_l2_ffn(ctx, W):
    b = ctx.bld
    emit_ln(ctx, "ln_l2f")
    E = 8
    cstage = ctx.alloc_slab()
    fcsl = ctx.alloc_slab()
    rw = ctx.small("l2f_rw", 128, 8 * 9, BF16)
    for ci in range(8):
        cload(ctx, W["wgt"][ci * 128:(ci + 1) * 128, :], rw[:, ci * 9:(ci + 1) * 9],
              "l2f_rw", 128, 9, cstage, stage_off=16 * (ci % 4))
    rbv = ctx.small("l2f_rb", 1, 9, BF16)
    cload(ctx, W["bgt"], rbv[:], "l2f_rb", 1, 9, cstage, stage_off=80)
    rb_bc = ctx.small("l2f_rbbc", 128, 9, F32)
    emit_bcast_row(ctx, rbv[:], "l2f_rb", rb_bc[:], "l2f_rbbc", 9)
    cten = ctx.small("l2f_c", 128, NT * E, F32)
    rcol = ctx.small("l2f_rcol", 128, 8, F32)
    lsc = ctx.small("l2f_lsc", 128, 32, F32)
    for ti in range(NT):
        pbr = ctx.pb[6]
        for ci in range(8):
            lch, lctok = ctx.ln_chunk(ci)
            b.emit("pe", lambda e, o=pbr[:, 0:9], l=lch[:, ti * 128:(ti + 1) * 128],
                   r=rw[:, ci * 9:(ci + 1) * 9], s=(ci == 0), pp=(ci == 7):
                   e.matmul(o, l, r, start=s, stop=pp),
                   reads=(lctok, "l2f_rw"), writes=("PB6",))
        lg = lsc[:, 0:9]; ex = lsc[:, 16:24]
        ct = "l2f_lsc"
        b.emit("dve", lambda e, o=lg, a=pbr[:, 0:9], c=rb_bc[:]: e.tensor_add(o, a, c),
               reads=("PB6", "l2f_rbbc"), writes=(ct,))
        nm = rcol[:, 0:1]; sm = rcol[:, 1:2]; rs = rcol[:, 2:3]
        thr = rcol[:, 3:4]; eq = rcol[:, 4:5]; es = rcol[:, 5:6]
        b.emit("dve", lambda e, o=nm, i_=lg[:, 0:8]: e.reduce_max(o, i_, axis=AX.X, negate=True),
               reads=(ct,), writes=(ct + "n",))
        b.emit("act", lambda e, o=ex, i_=lg[:, 0:8], bi=nm, a=es:
               e.activation(o, i_, AF.Exp, bias=bi, accum_out=a),
               reads=(ct, ct + "n"), writes=(ct + "e", ct + "s"))
        b.emit("dve", lambda e, o=rs, i_=es: e.reciprocal(o, i_),
               reads=(ct + "s",), writes=(ct + "r",))
        b.emit("act", lambda e, o=thr, i_=lg[:, 8:9]: e.activation(o, i_, AF.Sigmoid),
               reads=(ct,), writes=(ct + "t",))
        cv = cten[:, ti * E:(ti + 1) * E]
        b.emit("dve", lambda e, o=thr, i_=thr: e.tensor_scalar_mul(o, i_, 0.125),
               reads=(ct + "t",), writes=(ct + "t",))
        b.emit("dve", lambda e, o=cv, i_=ex, s=rs, s2=thr:
               e.tensor_scalar(o, i_, s, s2, ALU.mult, ALU.subtract),
               reads=(ct + "e", ct + "r", ct + "t"), writes=("l2f_c",))
        b.emit("dve", lambda e, o=cv, i_=cv: e.tensor_relu(o, i_),
               reads=("l2f_c",), writes=("l2f_c",))
        b.emit("dve", lambda e, o=sm, i_=cv: e.reduce_sum(o, i_, axis=AX.X),
               reads=("l2f_c",), writes=(ct + "m",))
        b.emit("dve", lambda e, o=eq, i_=sm: e.tensor_scalar(o, i_, 0.0, None, ALU.is_equal),
               reads=(ct + "m",), writes=(ct + "q",))
        b.emit("dve", lambda e, o=sm, a=sm, c=eq: e.tensor_add(o, a, c),
               reads=(ct + "m", ct + "q"), writes=(ct + "m",))
        b.emit("dve", lambda e, o=rs, i_=sm: e.reciprocal(o, i_),
               reads=(ct + "m",), writes=(ct + "r",))
        b.emit("dve", lambda e, o=cv, i_=cv, s=rs: e.tensor_scalar(o, i_, s, None, ALU.mult),
               reads=("l2f_c", ct + "r"), writes=("l2f_c",))
    cem = bf_view(ctx.slab(fcsl), 0, 2048)[0:E, :]
    cemtok = ctx.stok(fcsl, "cem")
    emit_cem(ctx, cten, "l2f_c", E, cem, cemtok)
    b1t = ctx.small("l2f_b1", 128, E * 32, F32)
    for ee in range(E):
        b.emit("sp", lambda e, o=b1t[:, ee * 32:(ee + 1) * 32], i_=W["b1p"][ee]:
               e.dma_start(out=o, in_=i_), reads=(), writes=("l2f_b1",), dma="cis")
    b2t = bf_view(ctx.slab(fcsl), 2048, 1024)[0:E, :]
    b2tok = ctx.stok(fcsl, "b2")
    cload(ctx, W["b2"], b2t, b2tok, E, 1024, cstage, stage_off=512)
    ctx.free_slab(cstage)

    HT = 1024
    u1 = [[ctx.alloc_slab(), ctx.alloc_slab()] for _ in range(2)]
    u2 = [[ctx.alloc_slab(), ctx.alloc_slab()] for _ in range(2)]
    hsl = [ctx.alloc_slab(), ctx.alloc_slab()]
    for th in range(2):
        t0 = th * HT
        for ee in range(E):
            for iq in range(4):
                par = (ee * 4 + iq) % 2
                w1u = wunit_load(ctx, W["w1"], ee, 0, iq * 1024, 1024, 1024, u1[par], "wmo")
                w2u = wunit_load(ctx, W["w2"], ee, iq * 1024, 0, 1024, 1024, u2[par], "wmo")
                for icp in range(4):
                    for ci in range(8):
                        lch, lctok = ctx.ln_chunk(ci)
                        for ics in range(2):
                            ic = icp * 2 + ics
                            wv_, wtok = w1u(ci)
                            for tc2 in range(2):
                                pbm = ctx.pb[ics * 2 + tc2]
                                b.emit("pe", lambda e, o=pbm[:], l=wv_[:, ic * 128:ic * 128 + 128],
                                       r=lch[:, t0 + tc2 * 512:t0 + tc2 * 512 + 512],
                                       s=(ci == 0), pp=(ci == 7):
                                       e.matmul(o, l, r, start=s, stop=pp),
                                       reads=(wtok, lctok), writes=(f"PB{ics * 2 + tc2}",))
                    for ics in range(2):
                        ic = icp * 2 + ics
                        bi_ = b1t[:, ee * 32 + iq * 8 + ic:ee * 32 + iq * 8 + ic + 1]
                        for tc2 in range(2):
                            hv = bf_view(ctx.slab(hsl[ic // 4]), (ic % 4) * 1024 + tc2 * 512, 512)
                            htok = ctx.stok(hsl[ic // 4], f"h{ic % 4}")
                            b.emit("act", lambda e, o=hv, i_=ctx.pb[ics * 2 + tc2][:], bi=bi_:
                                   e.activation(o, i_, AF.Gelu_apprx_tanh, bias=bi),
                                   reads=(f"PB{ics * 2 + tc2}", "l2f_b1"), writes=(htok,))
                for tt in range(8):
                    ti = (t0 + tt * 128) // 128
                    xv, xtok = ctx.x_tile(ti)
                    bb0 = 4 + (tt % 2) * 2
                    for ic in range(8):
                        wv_, wtok = w2u(ic)
                        hv = bf_view(ctx.slab(hsl[ic // 4]), (ic % 4) * 1024 + tt * 128, 128)
                        htok = ctx.stok(hsl[ic // 4], f"h{ic % 4}")
                        for hh in range(2):
                            pbo = ctx.pb[bb0 + hh]
                            b.emit("pe", lambda e, o=pbo[:], l=hv, r=wv_[:, hh * 512:hh * 512 + 512],
                                   s=(ic == 0), pp=(ic == 7):
                                   e.matmul(o, l, r, start=s, stop=pp),
                                   reads=(wtok, htok), writes=(f"PB{bb0 + hh}",))
                    for hh in range(2):
                        ccol = cten[:, ti * E + ee:ti * E + ee + 1]
                        dst = xv[:, hh * 512:hh * 512 + 512]
                        b.emit("dve", lambda e, o=dst, i_=ctx.pb[bb0 + hh][:], s=ccol, i2=dst:
                               e.scalar_tensor_tensor(o, i_, s, i2, ALU.mult, ALU.add),
                               reads=(f"PB{bb0 + hh}", "l2f_c", xtok), writes=(xtok,), tag="moeacc")
    for ti in range(NT):
        xv, xtok = ctx.x_tile(ti)
        for hh in range(2):
            pbo = ctx.pb[4 + hh % 2]
            pbot = f"PB{4 + hh % 2}"
            b.emit("pe", lambda e, o=pbo[:], l=cem[:, ti * 128:(ti + 1) * 128],
                   r=b2t[:, hh * 512:hh * 512 + 512]:
                   e.matmul(o, l, r, start=True, stop=True),
                   reads=(cemtok, b2tok), writes=(pbot,))
            dst = xv[:, hh * 512:hh * 512 + 512]
            b.emit("dve", lambda e, o=dst, a=dst, c=pbo[:]: e.tensor_add(o, a, c),
                   reads=(pbot, xtok), writes=(xtok,), tag="resid")
    for s in u1[0] + u1[1] + u2[0] + u2[1] + hsl + [fcsl]:
        ctx.free_slab(s)


def emit_l3_ffn(ctx, W):
    b = ctx.bld
    emit_ln(ctx, "ln_l3f")
    E = 32
    cstage = ctx.alloc_slab()
    rw = ctx.small("l3f_rw", 128, 8 * 32, BF16)
    for ci in range(8):
        cload(ctx, W["rwT"][ci * 128:(ci + 1) * 128, :], rw[:, ci * 32:(ci + 1) * 32],
              "l3f_rw", 128, 32, cstage, stage_off=32 * (ci % 4))
    rbv = ctx.small("l3f_rb", 1, 32, BF16)
    cload(ctx, W["router_bias"], rbv[:], "l3f_rb", 1, 32, cstage, stage_off=256)
    ctx.free_slab(cstage)
    rb_bc = ctx.small("l3f_rbbc", 128, 32, F32)
    emit_bcast_row(ctx, rbv[:], "l3f_rb", rb_bc[:], "l3f_rbbc", 32)
    cten = ctx.small("l3f_c", 128, NT * E, F32)
    rcol = ctx.small("l3f_rcol", 128, 8, F32)
    top8 = ctx.small("l3f_top8", 128, 8, F32)
    lsc = ctx.small("l3f_lsc", 128, 160, F32)
    for ti in range(NT):
        pbr = ctx.pb[6]
        for ci in range(8):
            lch, lctok = ctx.ln_chunk(ci)
            b.emit("pe", lambda e, o=pbr[:, 0:32], l=lch[:, ti * 128:(ti + 1) * 128],
                   r=rw[:, ci * 32:(ci + 1) * 32], s=(ci == 0), pp=(ci == 7):
                   e.matmul(o, l, r, start=s, stop=pp),
                   reads=(lctok, "l3f_rw"), writes=("PB6",))
        sc = lsc[:, 0:32]; sfc = lsc[:, 32:64]; tmp = lsc[:, 64:96]; gsx = lsc[:, 96:104]
        gm = lsc[:, 104:108]; sk = lsc[:, 108:140]
        ct = "l3f_lsc"
        b.emit("act", lambda e, o=sc, i_=pbr[:, 0:32]: e.activation(o, i_, AF.Sigmoid),
               reads=("PB6",), writes=(ct,))
        b.emit("dve", lambda e, o=sfc, a=sc, c=rb_bc[:]: e.tensor_add(o, a, c),
               reads=(ct, "l3f_rbbc"), writes=(ct + "f",))
        for gix in range(4):
            b.emit("dve", lambda e, o=top8[:], i_=sfc[:, gix * 8:(gix + 1) * 8]: e.max(out=o, in_=i_),
                   reads=(ct + "f",), writes=("l3f_top8",))
            b.emit("dve", lambda e, o=gsx[:, gix:gix + 1], a=top8[:, 0:1], c=top8[:, 1:2]:
                   e.tensor_add(o, a, c),
                   reads=("l3f_top8",), writes=(ct + "g",))
        b.emit("dve", lambda e, o=gsx[:, 4:8]: e.memset(o, -1e30), reads=(), writes=(ct + "g",))
        b.emit("dve", lambda e, o=top8[:], i_=gsx[:]: e.max(out=o, in_=i_),
               reads=(ct + "g",), writes=("l3f_top8",))
        b.emit("dve", lambda e, o=gm, i_=gsx[:, 0:4], s=top8[:, 1:2]:
               e.tensor_scalar(o, i_, s, None, ALU.is_ge),
               reads=(ct + "g", "l3f_top8"), writes=(ct + "gm",))
        for gix in range(4):
            b.emit("dve", lambda e, o=sk[:, gix * 8:(gix + 1) * 8],
                   i_=gm[:, gix:gix + 1].to_broadcast([128, 8]):
                   e.tensor_copy(o, i_),
                   reads=(ct + "gm",), writes=(ct + "sk",))
        b.emit("dve", lambda e, o=tmp, a=sfc, c=sk[:, 0:32]: e.tensor_mul(o, a, c),
               reads=(ct + "f", ct + "sk"), writes=(ct + "tm",))
        b.emit("dve", lambda e, o=sk[:, 0:32], i_=sk[:, 0:32]:
               e.tensor_scalar(o, i_, 1e30, -1e30, ALU.mult, ALU.add),
               reads=(ct + "sk",), writes=(ct + "sk",))
        b.emit("dve", lambda e, o=tmp, a=tmp, c=sk[:, 0:32]: e.tensor_add(o, a, c),
               reads=(ct + "tm", ct + "sk"), writes=(ct + "tm",))
        b.emit("dve", lambda e, o=top8[:], i_=tmp: e.max(out=o, in_=i_),
               reads=(ct + "tm",), writes=("l3f_top8",))
        cv = cten[:, ti * E:(ti + 1) * E]
        b.emit("dve", lambda e, o=cv, i_=tmp, s=top8[:, 3:4]:
               e.tensor_scalar(o, i_, s, None, ALU.is_ge),
               reads=(ct + "tm", "l3f_top8"), writes=("l3f_c",))
        b.emit("dve", lambda e, o=cv, a=cv, c=sc: e.tensor_mul(o, a, c),
               reads=("l3f_c", ct), writes=("l3f_c",))
        nm = rcol[:, 0:1]; rs = rcol[:, 1:2]
        b.emit("dve", lambda e, o=nm, i_=cv: e.reduce_sum(o, i_, axis=AX.X),
               reads=("l3f_c",), writes=(ct + "su",))
        b.emit("dve", lambda e, o=nm, i_=nm: e.tensor_scalar_add(o, i_, 1e-20),
               reads=(ct + "su",), writes=(ct + "su",))
        b.emit("dve", lambda e, o=rs, i_=nm: e.reciprocal(o, i_),
               reads=(ct + "su",), writes=(ct + "sr",))
        b.emit("dve", lambda e, o=cv, i_=cv, s=rs, s2=2.5:
               e.tensor_scalar(o, i_, s, s2, ALU.mult, ALU.mult),
               reads=("l3f_c", ct + "sr"), writes=("l3f_c",))

    ug = [ctx.alloc_slab(), ctx.alloc_slab()]
    uu = [ctx.alloc_slab(), ctx.alloc_slab()]
    ud = [ctx.alloc_slab(), ctx.alloc_slab()]
    hsl = [ctx.alloc_slab(), ctx.alloc_slab()]
    gsl = [ctx.alloc_slab(), ctx.alloc_slab()]

    def l3_unit(dram3, ee, sl, sem):
        return big_wload(ctx, dram3, sl, sem, nch=8, ncols=512, ee=ee)

    def l3_dunit(dram3, ee, sl, sem):
        return big_wload(ctx, dram3, sl, sem, nch=4, ncols=1024, ee=ee)

    for ee in range(E):
        gacc = l3_unit(W["gate_wT"], ee, ug[ee % 2], "wmo")
        uacc = l3_unit(W["up_wT"], ee, uu[ee % 2], "wmo")
        dacc = l3_dunit(W["down_wT"], ee, ud[ee % 2], "wmo")
        for ic in range(4):
            for ci in range(8):
                wv_, wtok = gacc(ci)
                lch, lctok = ctx.ln_chunk(ci)
                for tq in range(4):
                    b.emit("pe", lambda e, o=ctx.pb[tq][:], l=wv_[:, ic * 128:ic * 128 + 128],
                           r=lch[:, tq * 512:(tq + 1) * 512], s=(ci == 0), pp=(ci == 7):
                           e.matmul(o, l, r, start=s, stop=pp),
                           reads=(wtok, lctok), writes=(f"PB{tq}",))
            for ci in range(8):
                wv_, wtok = uacc(ci)
                lch, lctok = ctx.ln_chunk(ci)
                for tq in range(4):
                    b.emit("pe", lambda e, o=ctx.pb[4 + tq][:], l=wv_[:, ic * 128:ic * 128 + 128],
                           r=lch[:, tq * 512:(tq + 1) * 512], s=(ci == 0), pp=(ci == 7):
                           e.matmul(o, l, r, start=s, stop=pp),
                           reads=(wtok, lctok), writes=(f"PB{4 + tq}",))
            for tq in range(4):
                gv = bf_view(ctx.slab(gsl[ic // 2]), (ic % 2) * 2048 + tq * 512, 512)
                gtok = ctx.stok(gsl[ic // 2], f"g{ic % 2}")
                b.emit("act", lambda e, o=gv, i_=ctx.pb[tq][:]: e.activation(o, i_, AF.Silu),
                       reads=(f"PB{tq}",), writes=(gtok,))
                hv = bf_view(ctx.slab(hsl[ic // 2]), (ic % 2) * 2048 + tq * 512, 512)
                htok = ctx.stok(hsl[ic // 2], f"h{ic % 2}")
                b.emit("dve", lambda e, o=hv, a=ctx.pb[4 + tq][:], c=gv: e.tensor_mul(o, a, c),
                       reads=(f"PB{4 + tq}", gtok), writes=(htok,))
        for tt in range(NT):
            xv, xtok = ctx.x_tile(tt)
            bb0 = (tt % 2) * 2
            for ic in range(4):
                wv_, wtok = dacc(ic)
                hv = bf_view(ctx.slab(hsl[ic // 2]), (ic % 2) * 2048 + tt * 128, 128)
                htok = ctx.stok(hsl[ic // 2], f"h{ic % 2}")
                for hh in range(2):
                    b.emit("pe", lambda e, o=ctx.pb[bb0 + hh][:], l=hv, r=wv_[:, hh * 512:hh * 512 + 512],
                           s=(ic == 0), pp=(ic == 3):
                           e.matmul(o, l, r, start=s, stop=pp),
                           reads=(wtok, htok), writes=(f"PB{bb0 + hh}",))
            for hh in range(2):
                ccol = cten[:, tt * E + ee:tt * E + ee + 1]
                dst = xv[:, hh * 512:hh * 512 + 512]
                b.emit("dve", lambda e, o=dst, i_=ctx.pb[bb0 + hh][:], s=ccol, i2=dst:
                       e.scalar_tensor_tensor(o, i_, s, i2, ALU.mult, ALU.add),
                       reads=(f"PB{bb0 + hh}", "l3f_c", xtok), writes=(xtok,), tag="moeacc")
    for s_ in (ug[0], ug[1], uu[0], uu[1], ud[0], ud[1]):
        ctx.free_slab(s_)
    sg_u = [ctx.alloc_slab(), ctx.alloc_slab()]
    su_u = [ctx.alloc_slab(), ctx.alloc_slab()]
    sd_u = [ctx.alloc_slab(), ctx.alloc_slab()]
    for th in range(2):
        t0 = th * 1024
        sgacc = wunit_load(ctx, W["sgwT"], None, 0, 0, 1024, 1024, sg_u, "wmo")
        suacc = wunit_load(ctx, W["suwT"], None, 0, 0, 1024, 1024, su_u, "wmo")
        sdacc = wunit_load(ctx, W["sdwT"], None, 0, 0, 1024, 1024, sd_u, "wmo")
        for ic in range(8):
            for ci in range(8):
                wv_, wtok = sgacc(ci)
                lch, lctok = ctx.ln_chunk(ci)
                for tq in range(2):
                    b.emit("pe", lambda e, o=ctx.pb[tq][:], l=wv_[:, ic * 128:ic * 128 + 128],
                           r=lch[:, t0 + tq * 512:t0 + tq * 512 + 512], s=(ci == 0), pp=(ci == 7):
                           e.matmul(o, l, r, start=s, stop=pp),
                           reads=(wtok, lctok), writes=(f"PB{tq}",))
            for ci in range(8):
                wv_, wtok = suacc(ci)
                lch, lctok = ctx.ln_chunk(ci)
                for tq in range(2):
                    b.emit("pe", lambda e, o=ctx.pb[2 + tq][:], l=wv_[:, ic * 128:ic * 128 + 128],
                           r=lch[:, t0 + tq * 512:t0 + tq * 512 + 512], s=(ci == 0), pp=(ci == 7):
                           e.matmul(o, l, r, start=s, stop=pp),
                           reads=(wtok, lctok), writes=(f"PB{2 + tq}",))
            for tq in range(2):
                gv = bf_view(ctx.slab(gsl[ic // 4]), (ic % 4) * 1024 + tq * 512, 512)
                gtok = ctx.stok(gsl[ic // 4], f"sg{ic % 4}")
                b.emit("act", lambda e, o=gv, i_=ctx.pb[tq][:]: e.activation(o, i_, AF.Silu),
                       reads=(f"PB{tq}",), writes=(gtok,))
                hv = bf_view(ctx.slab(hsl[ic // 4]), (ic % 4) * 1024 + tq * 512, 512)
                htok = ctx.stok(hsl[ic // 4], f"sh{ic % 4}")
                b.emit("dve", lambda e, o=hv, a=ctx.pb[2 + tq][:], c=gv: e.tensor_mul(o, a, c),
                       reads=(f"PB{2 + tq}", gtok), writes=(htok,))
        for tt in range(8):
            ti = (t0 + tt * 128) // 128
            xv, xtok = ctx.x_tile(ti)
            bb0 = 4 + (tt % 2) * 2
            for ic in range(8):
                wv_, wtok = sdacc(ic)
                hv = bf_view(ctx.slab(hsl[ic // 4]), (ic % 4) * 1024 + tt * 128, 128)
                htok = ctx.stok(hsl[ic // 4], f"sh{ic % 4}")
                for hh in range(2):
                    b.emit("pe", lambda e, o=ctx.pb[bb0 + hh][:], l=hv, r=wv_[:, hh * 512:hh * 512 + 512],
                           s=(ic == 0), pp=(ic == 7):
                           e.matmul(o, l, r, start=s, stop=pp),
                           reads=(wtok, htok), writes=(f"PB{bb0 + hh}",))
            for hh in range(2):
                dst = xv[:, hh * 512:hh * 512 + 512]
                b.emit("dve", lambda e, o=dst, a=dst, c=ctx.pb[bb0 + hh][:]: e.tensor_add(o, a, c),
                       reads=(f"PB{bb0 + hh}", xtok), writes=(xtok,), tag="resid")
    for s in [hsl[0], hsl[1], gsl[0], gsl[1]] + sg_u + su_u + sd_u:
        ctx.free_slab(s)


def _colprep(v):
    return np.ascontiguousarray(np.asarray(v, dtype=np.float32).reshape(8, 128).T)


def host_prep(inputs):
    P = {}
    g = lambda k: np.ascontiguousarray(np.asarray(inputs[k], dtype=np.float32))
    tr = lambda k: np.ascontiguousarray(np.asarray(inputs[k], dtype=np.float32).T)
    P["l0a"] = {
        "w": _colprep(inputs["L0_attn_w"]), "u": _colprep(inputs["L0_attn_u"]),
        "mk": _colprep(inputs["L0_attn_mix_k"]), "mv": _colprep(inputs["L0_attn_mix_v"]),
        "mr": _colprep(inputs["L0_attn_mix_r"]),
        "wkT": tr("L0_attn_Wk"), "wvT": tr("L0_attn_Wv"),
        "wrT": tr("L0_attn_Wr"), "woT": tr("L0_attn_Wo"),
    }
    def headmaj_T(k):
        w = np.asarray(inputs[k], dtype=np.float32).reshape(1024, 1024)
        return np.ascontiguousarray(w.T)
    pek_ = np.asarray(inputs["L1_attn_pek"], np.float32)
    pev_ = np.asarray(inputs["L1_attn_pev"], np.float32)
    P["l1a"] = {
        "wqT": tr("L1_attn_Wq"), "wkT": tr("L1_attn_Wk"),
        "wvT": tr("L1_attn_Wv"), "woT": tr("L1_attn_Wo"),
        "pekT": np.ascontiguousarray(pek_[0:1025].T),
        "pev_aug": np.ascontiguousarray(np.concatenate(
            [pev_[1:1025], np.ones((1024, 1), np.float32)], axis=1)),
        "pev0": np.ascontiguousarray(pev_[0:1]),
    }
    P["l2a"] = {
        "wqT": headmaj_T("L2_attn_W_Q"), "wkT": headmaj_T("L2_attn_W_K"),
        "wvT": headmaj_T("L2_attn_W_V"), "woT": tr("L2_attn_W_O"),
    }
    perm = []
    for p_ in range(8):
        for h_ in (2 * p_, 2 * p_ + 1):
            perm.extend([64 * h_ + 2 * i_ for i_ in range(32)])
        for h_ in (2 * p_, 2 * p_ + 1):
            perm.extend([64 * h_ + 2 * i_ + 1 for i_ in range(32)])
    perm = np.asarray(perm)
    P["l3a"] = {
        "wqT": np.ascontiguousarray(np.asarray(inputs["L3_attn_Wq"], np.float32).T[:, perm]),
        "wkT": np.ascontiguousarray(np.asarray(inputs["L3_attn_Wk"], np.float32).T[:, perm]),
        "wvT": tr("L3_attn_Wv"), "woT": tr("L3_attn_Wo"),
    }
    def _b1prep(b1):
        b1 = np.asarray(b1, np.float32)
        E_, I_ = b1.shape
        return np.ascontiguousarray(b1.reshape(E_, I_ // 128, 128).transpose(0, 2, 1))
    P["l1f"] = {
        "router_w": g("L1_ffn_router_w"),
        "router_b": np.asarray(inputs["L1_ffn_router_b"], np.float32).reshape(1, 16),
        "w1": g("L1_ffn_w1"), "b1p": _b1prep(inputs["L1_ffn_b1"]),
        "w2": g("L1_ffn_w2"), "b2": g("L1_ffn_b2"),
    }
    P["l2f"] = {
        "wgt": np.ascontiguousarray(np.concatenate(
            [np.asarray(inputs["L2_ffn_Wg"], np.float32),
             np.asarray(inputs["L2_ffn_Wt"], np.float32)], axis=1)),
        "bgt": np.ascontiguousarray(np.concatenate(
            [np.asarray(inputs["L2_ffn_bg"], np.float32),
             np.asarray(inputs["L2_ffn_bt"], np.float32)]).reshape(1, 9)),
        "w1": g("L2_ffn_W1"), "b1p": _b1prep(inputs["L2_ffn_b1"]),
        "w2": g("L2_ffn_W2"), "b2": g("L2_ffn_b2"),
    }
    P["l3f"] = {
        "rwT": tr("L3_ffn_router_weight"),
        "router_bias": np.asarray(inputs["L3_ffn_router_bias"], np.float32).reshape(1, 32),
        "gate_wT": np.ascontiguousarray(np.asarray(inputs["L3_ffn_gate_w"], np.float32).transpose(0, 2, 1)),
        "up_wT": np.ascontiguousarray(np.asarray(inputs["L3_ffn_up_w"], np.float32).transpose(0, 2, 1)),
        "down_wT": np.ascontiguousarray(np.asarray(inputs["L3_ffn_down_w"], np.float32).transpose(0, 2, 1)),
        "sgwT": tr("L3_ffn_shared_gate_w"), "suwT": tr("L3_ffn_shared_up_w"),
        "sdwT": tr("L3_ffn_shared_down_w"),
    }
    P["l0f"] = {
        "mur": _colprep(inputs["L0_ffn_mu_r"]), "muk": _colprep(inputs["L0_ffn_mu_k"]),
        "wrT": tr("L0_ffn_W_r"), "wkT": tr("L0_ffn_W_k"), "wvT": tr("L0_ffn_W_v"),
    }
    return P


def declare_weights(nc, wd, prefix):
    aps, inmap = {}, {}
    for k, v in wd.items():
        t = nc.dram_tensor(f"{prefix}_{k}", list(v.shape), F32, kind="ExternalInput")
        aps[k] = t.ap()
        inmap[f"{prefix}_{k}"] = v
    return aps, inmap


POOL_N = {"l0a": 12, "l0f": 12}


def build_block(emit_fns, wds, prefixes):
    nc = bass.Bass("TRN2", target_bir_lowering=False, debug=False)
    x_in = nc.dram_tensor("x_in", [T, C], F32, kind="ExternalInput").ap()
    x_out = nc.dram_tensor("x_out", [T, C], F32, kind="ExternalOutput").ap()
    inmap_all = {}
    wapss = []
    for wd, prefix in zip(wds, prefixes):
        waps, inmap_w = declare_weights(nc, wd, prefix)
        inmap_all.update(inmap_w)
        wapss.append(waps)
    bld = Bld(nc)
    if len(prefixes) > 1:
        npool = 12
    else:
        npool = min(POOL_N.get(p, 11) for p in prefixes)
    ctx = Ctx(nc, bld, n_pool=npool)
    identbf = nc.inline_tensor(np.eye(128, dtype=np.float32).astype(ml_dtypes.bfloat16), "identbf_c")
    idt = ctx.small("identbf", 128, 128, BF16)
    bld.emit("sp", lambda e: e.dma_start(out=idt[:], in_=identbf.ap()),
             reads=(), writes=("identbf",), dma="cis")
    identfc = nc.inline_tensor(np.eye(128, dtype=np.float32), "identf32_c")
    idtf = ctx.small("identf32", 128, 128, F32)
    bld.emit("sp", lambda e: e.dma_start(out=idtf[:], in_=identfc.ap()),
             reads=(), writes=("identf32",), dma="cis")
    emit_load_x(ctx, x_in)
    for emit_fn, waps in zip(emit_fns, wapss):
        emit_fn(ctx, waps)
    emit_store_x(ctx, x_out)
    bld.finalize()
    return nc, inmap_all


BLOCKS = [
    ("l0a", emit_l0_attn),
    ("l0f", emit_l0_ffn),
    ("l1a", emit_l1_attn),
    ("l1f", emit_l1_ffn),
    ("l2a", emit_l2_attn),
    ("l2f", emit_l2_ffn),
    ("l3a", emit_l3_attn),
    ("l3f", emit_l3_ffn),
]


def run_blocks(xs, P, blocks):
    for name, emit_fn in blocks:
        nc, inmap_all = build_block([emit_fn], [P[name]], [name])
        in_maps = [dict(inmap_all, x_in=xs[c]) for c in range(8)]
        res = run_bass_kernel_spmd(nc, in_maps, core_ids=list(range(8)))
        xs = [np.asarray(res.results[c]["x_out"], dtype=np.float32) for c in range(8)]
    return xs


def kernel(**inputs):
    import os
    P = host_prep(inputs)
    x_full = np.asarray(inputs["L0_attn_x"], dtype=np.float32)
    xs = [np.ascontiguousarray(x_full[bb]) for bb in range(8)]
    if os.environ.get("UNFUSED") == "1":
        xs = run_blocks(xs, P, BLOCKS)
        return np.stack(xs, axis=0)
    names = [n for n, _ in BLOCKS]
    fns = [f for _, f in BLOCKS]
    nc, inmap_all = build_block(fns, [P[n] for n in names], names)
    in_maps = [dict(inmap_all, x_in=xs[c]) for c in range(8)]
    res = run_bass_kernel_spmd(nc, in_maps, core_ids=list(range(8)))
    return np.stack([np.asarray(res.results[c]["x_out"], dtype=np.float32) for c in range(8)], axis=0)


# version 10
# speedup vs baseline: 1.1061x; 1.1061x over previous
import numpy as np
import ml_dtypes
import concourse.bass as bass
import concourse.mybir as mybir
from concourse.bass_types import AP
from concourse.bass_utils import run_bass_kernel_spmd

F32 = mybir.dt.float32
BF16 = mybir.dt.bfloat16
AF = mybir.ActivationFunctionType
ALU = mybir.AluOpType
AX = mybir.AxisListType

T = 2048
C = 1024
NT = 16
NCH = 8
SLAB_COLS = 2048

ENGINES = ("pe", "act", "dve", "pool", "sp")
_BLOCK_ATTR = {"pe": "tensor", "act": "scalar", "dve": "vector", "pool": "gpsimd", "sp": "sync"}


class Bld:
    def __init__(self, nc):
        self.nc = nc
        self.ops = {e: [] for e in ENGINES}
        self.res = {}
        self.sems = {}
        self.dma_counts = {}
        self.hiwater = {e: {} for e in ENGINES}
        self.token_init = {}

    def _need(self, eng, dep):
        if dep is None:
            return False
        if dep[0] == "e":
            return dep[2] > self.hiwater[eng].get(dep[1], -1)
        return dep[2] > self.hiwater[eng].get(("s", dep[1]), -1)

    def _res(self, t):
        r = self.res.get(t)
        if r is None:
            for pfx, pdeps in self.token_init.items():
                if t.startswith(pfx):
                    r = [None, list(pdeps)]
                    break
            else:
                r = [None, []]
            self.res[t] = r
        return r

    def emit(self, eng, fn, reads=(), writes=(), dma=None, tag=""):
        deps = []
        for t in reads:
            r = self._res(t)
            if self._need(eng, r[0]):
                deps.append(r[0])
        for t in writes:
            r = self._res(t)
            if self._need(eng, r[0]):
                deps.append(r[0])
            for rd in r[1]:
                if self._need(eng, rd):
                    deps.append(rd)
        best = {}
        for d in deps:
            k = (d[0], d[1]) if d[0] != "pd" else (d[0], d[1], d[2])
            if k not in best or d[0] == "pd" or d[2] > best[k][2]:
                best[k] = d
        deps = list(best.values())
        idx = len(self.ops[eng])
        op = {"fn": fn, "waits": deps, "inc": False, "dma": None, "tag": tag}
        self._mark(eng, deps)
        self.ops[eng].append(op)
        if dma is not None:
            D = 6 if eng == "pool" else 4
            cidx = self.dma_counts.get(("chan", dma), 0)
            self.dma_counts[("chan", dma)] = cidx + 1
            semname = f"{dma}.{cidx % D}"
            if semname not in self.sems:
                self.sems[semname] = self.nc.alloc_semaphore("dsem_" + semname)
                self.dma_counts[semname] = 0
            prev = self.dma_counts[semname]
            if prev and self._need(eng, ("s", semname, prev)):
                op["waits"].append(("s", semname, prev))
                self.hiwater[eng][("s", semname)] = prev
            self.dma_counts[semname] = prev + 16
            op["dma"] = semname
            my_dep = ("s", semname, prev + 16)
        else:
            my_dep = ("e", eng, idx)
        for t in writes:
            self.res[t] = [my_dep, []]
        for t in reads:
            r = self.res.setdefault(t, [None, []])
            r[1].append(my_dep)
        return my_dep

    def _mark(self, eng, deps):
        for d in deps:
            if d[0] == "e":
                self.ops[d[1]][d[2]]["inc"] = True
                self.hiwater[eng][d[1]] = max(self.hiwater[eng].get(d[1], -1), d[2])
            else:
                self.hiwater[eng][("s", d[1])] = max(self.hiwater[eng].get(("s", d[1]), -1), d[2])

    def barrier_channel(self, eng, ch):
        for i in range(8):
            semname = f"{ch}.{i}"
            if semname in self.sems:
                self.barrier_dep(eng, ("s", semname, self.dma_counts[semname]))

    def barrier_dep(self, eng, dep):
        if self._need(eng, dep):
            if dep[0] == "e":
                self.ops[dep[1]][dep[2]]["inc"] = True
                self.hiwater[eng][dep[1]] = max(self.hiwater[eng].get(dep[1], -1), dep[2])
            else:
                self.hiwater[eng][("s", dep[1])] = max(self.hiwater[eng].get(("s", dep[1]), -1), dep[2])
            self.ops[eng].append({"fn": None, "waits": [dep], "inc": False, "dma": None, "tag": "bar"})

    def finalize(self):
        nc = self.nc
        engsem = {e: nc.alloc_semaphore("esem_" + e) for e in ENGINES if self.ops[e]}
        prefix = {}
        for e in ENGINES:
            acc, pl = 0, []
            for op in self.ops[e]:
                if op["inc"]:
                    acc += 1
                pl.append(acc)
            prefix[e] = pl
        import contextlib
        with nc.allow_low_precision("intentional bf16 intermediates (tolerance-checked)"), \
             nc.Block() as block:
            for e in ENGINES:
                if not self.ops[e]:
                    continue
                def make_body(e):
                    def body(engobj):
                        for op in self.ops[e]:
                            for d in op["waits"]:
                                if d[0] == "e":
                                    engobj.wait_ge(engsem[d[1]], prefix[d[1]][d[2]])
                                else:
                                    engobj.wait_ge(self.sems[d[1]], d[2])
                            if op["fn"] is None:
                                continue
                            inst = op["fn"](engobj)
                            if op["dma"] is not None:
                                inst.then_inc(self.sems[op["dma"]], 16)
                            elif op["inc"]:
                                inst.then_inc(engsem[e], 1)
                    return body
                getattr(block, _BLOCK_ATTR[e])(make_body(e))


class Ctx:
    def __init__(self, nc, bld, n_pool=12):
        self.nc = nc
        self.bld = bld
        self.xs = [nc.alloc_sbuf_tensor(f"XR{i}", [128, SLAB_COLS], F32) for i in range(8)]
        self.lns = [nc.alloc_sbuf_tensor(f"LNF{i}", [128, SLAB_COLS], F32) for i in range(4)]
        self.pool_slabs = [nc.alloc_sbuf_tensor(f"PS{i}", [128, SLAB_COLS], F32) for i in range(n_pool)]
        self.pool_free = list(range(n_pool))
        self.epoch = [0] * n_pool
        self.used_toks = [set() for _ in range(n_pool)]
        self.pb = [nc.alloc_psum_tensor(f"PB{i}", [128, 512], F32) for i in range(8)]
        self.smalls = {}

    def alloc_slab(self):
        i = self.pool_free.pop(0)
        deps = []
        for t in self.used_toks[i]:
            r = self.bld.res.get(t)
            if r:
                if r[0]:
                    deps.append(r[0])
                deps.extend(r[1])
        self.epoch[i] += 1
        self.used_toks[i] = set()
        self.bld.token_init[f"PS{i}e{self.epoch[i]}."] = deps
        return i

    def free_slab(self, i):
        assert i not in self.pool_free
        self.pool_free.append(i)

    def slab(self, i):
        return self.pool_slabs[i]

    def stok(self, i, suffix=""):
        t = f"PS{i}e{self.epoch[i]}.{suffix}"
        self.used_toks[i].add(t)
        return t

    def small(self, name, rows, cols, dtype=F32):
        if name not in self.smalls:
            self.smalls[name] = self.nc.alloc_sbuf_tensor("sm_" + name, [rows, cols], dtype)
        return self.smalls[name]

    def x_tile(self, i):
        s = self.xs[i // 2]
        return s[:, (i % 2) * 1024:(i % 2) * 1024 + 1024], f"XR{i // 2}"

    def ln_chunk(self, ch):
        s = self.lns[ch // 2]
        v = s[:].bitcast(BF16)[:, (ch % 2) * 2048:(ch % 2) * 2048 + 2048]
        return v, f"LNF{ch // 2}"


def bf_view(slab_handle, c0, cols):
    return slab_handle[:].bitcast(BF16)[:, c0:c0 + cols]


def f_view(slab_handle, c0, cols):
    return slab_handle[:, c0:c0 + cols]


def emit_load_x(ctx, x_dram):
    b = ctx.bld
    for i in range(NT):
        v, tok = ctx.x_tile(i)
        b.emit("sp", lambda e, v=v, i=i: e.dma_start(out=v, in_=x_dram[i * 128:(i + 1) * 128, :]),
               reads=(), writes=(tok,), dma="xin")


def emit_store_x(ctx, x_out_dram):
    b = ctx.bld
    for i in range(NT):
        v, tok = ctx.x_tile(i)
        b.emit("sp", lambda e, v=v, i=i: e.dma_start(out=x_out_dram[i * 128:(i + 1) * 128, :], in_=v),
               reads=(tok,), writes=(), dma="xout")
    b.barrier_channel("sp", "xout")


def emit_ln(ctx, name):
    b = ctx.bld
    scr = ctx.alloc_slab()
    ident = ctx.small("identbf", 128, 128, BF16)
    cols = ctx.small(f"lncol_{name}", 128, 16, F32)
    lntm = ctx.alloc_slab()
    for i in range(NT):
        xv, xtok = ctx.x_tile(i)
        p = i % 2
        ssum = cols[:, 8 * p + 0:8 * p + 1]; ssq = cols[:, 8 * p + 1:8 * p + 2]
        mean = cols[:, 8 * p + 2:8 * p + 3]; msq = cols[:, 8 * p + 3:8 * p + 4]
        m2 = cols[:, 8 * p + 4:8 * p + 5]; var = cols[:, 8 * p + 5:8 * p + 6]
        std = cols[:, 8 * p + 6:8 * p + 7]; rstd = cols[:, 8 * p + 7:8 * p + 8]
        ct = f"lncol_{name}{p}"
        scrv = f_view(ctx.slab(scr), p * 1024, 1024)
        b.emit("dve", lambda e, xv=xv, o=scrv, a=ssq:
               e.scalar_tensor_tensor(o, xv, 1.0, xv, ALU.mult, ALU.mult, accum_out=a),
               reads=(xtok,), writes=(ctx.stok(scr, str(p)), ct + "q"), tag="ln_sq")
        b.emit("act", lambda e, xv=xv, o=scrv, a=ssum:
               e.activation(o, xv, AF.Copy, accum_out=a),
               reads=(xtok,), writes=(ctx.stok(scr, str(p)), ct + "s"), tag="ln_sum")
        b.emit("dve", lambda e, o=mean, i_=ssum: e.tensor_scalar_mul(o, i_, 1.0 / C),
               reads=(ct + "s",), writes=(ct + "m",))
        b.emit("dve", lambda e, o=msq, i_=ssq: e.tensor_scalar_mul(o, i_, 1.0 / C),
               reads=(ct + "q",), writes=(ct + "2",))
        b.emit("dve", lambda e, o=m2, a=mean: e.tensor_mul(o, a, a),
               reads=(ct + "m",), writes=(ct + "4",))
        b.emit("dve", lambda e, o=var, a=msq, c=m2:
               e.tensor_scalar(o, a, c, 1e-5, ALU.subtract, ALU.add),
               reads=(ct + "2", ct + "4"), writes=(ct + "v",))
        b.emit("act", lambda e, o=std, i_=var: e.activation(o, i_, AF.Sqrt),
               reads=(ct + "v",), writes=(ct + "d",))
        b.emit("dve", lambda e, o=rstd, i_=std: e.reciprocal(o, i_),
               reads=(ct + "d",), writes=(ct + "r",))
        ltv = bf_view(ctx.slab(lntm), p * 2048, 1024)
        b.emit("dve", lambda e, o=ltv, xv=xv, m=mean, r=rstd:
               e.tensor_scalar(o, xv, m, r, ALU.subtract, ALU.mult),
               reads=(xtok, ct + "m", ct + "r"), writes=(ctx.stok(lntm, str(p)),), tag="ln_tm")
        for chs in range(NCH):
            pbi = chs
            pbv = ctx.pb[pbi][:].bitcast(BF16)[:, 0:128]
            b.emit("pe", lambda e, o=pbv, i_=bf_view(ctx.slab(lntm), p * 2048 + chs * 128, 128), idm=ident[:]:
                   e.transpose(o, i_, idm),
                   reads=(ctx.stok(lntm, str(p)), "identbf"), writes=(f"PB{pbi}",), tag="ln_tr")
            lchv, lctok = ctx.ln_chunk(chs)
            dst = lchv[:, i * 128:(i + 1) * 128]
            if chs % 2 == 0:
                b.emit("dve", lambda e, o=dst, i_=pbv: e.tensor_copy(o, i_),
                       reads=(f"PB{pbi}",), writes=(lctok,), tag="ln_ev")
            else:
                b.emit("act", lambda e, o=dst, i_=pbv: e.copy(o, i_),
                       reads=(f"PB{pbi}",), writes=(lctok,), tag="ln_ev")
    ctx.free_slab(scr)
    ctx.free_slab(lntm)


def load_w_1024(ctx, dram, slabs, sem, cols=None):
    n = dram.shape[1] if cols is None else cols
    acc0 = big_wload(ctx, dram, slabs[0], sem, r0=0, nch=4, ncols=n)
    acc1 = big_wload(ctx, dram, slabs[1], sem, r0=512, nch=4, ncols=n)

    def acc(ci):
        return (acc0 if ci < 4 else acc1)(ci % 4)
    return acc


def emit_l0_attn(ctx, W):
    b = ctx.bld
    emit_ln(ctx, "ln0a")
    QT = 512
    ccol = {}
    for nm in ("w", "u", "mk", "mv", "mr"):
        t = ctx.small(f"l0a_{nm}", 128, 8, F32)
        b.emit("sp", lambda e, o=t[:], i_=W[nm]: e.dma_start(out=o, in_=i_),
               reads=(), writes=(f"l0a_{nm}",), dma="cis")
        ccol[nm] = t
    dcol = ctx.small("l0a_d", 128, 8, F32)
    eucol = ctx.small("l0a_eu", 128, 8, F32)
    tmpc = ctx.small("l0a_tc", 128, 8, F32)
    b.emit("act", lambda e: e.activation(tmpc[:], ccol["w"][:], AF.Exp),
           reads=("l0a_w",), writes=("l0a_tc",))
    b.emit("act", lambda e: e.activation(dcol[:], tmpc[:], AF.Exp, scale=-1.0),
           reads=("l0a_tc",), writes=("l0a_d",))
    b.emit("act", lambda e: e.activation(eucol[:], ccol["u"][:], AF.Exp),
           reads=("l0a_u",), writes=("l0a_eu",))
    for nm in ("mk", "mv", "mr"):
        t2 = ctx.small(f"l0a_om{nm}", 128, 8, F32)
        b.emit("dve", lambda e, o=t2[:], i_=ccol[nm][:]:
               e.tensor_scalar(o, i_, -1.0, -1.0, ALU.mult, ALU.subtract),
               reads=(f"l0a_{nm}",), writes=(f"l0a_om{nm}",))
    pc = ctx.small("l0a_Pc", 128, 8, F32)
    qc = ctx.small("l0a_Qc", 128, 8, F32)
    b.emit("dve", lambda e: e.memset(pc[:], 0.0), reads=(), writes=("l0a_Pc",))
    b.emit("dve", lambda e: e.memset(qc[:], 0.0), reads=(), writes=("l0a_Qc",))

    def emit_mix(dst_slab, key, prefix, t0, q):
        sl = ctx.slab(dst_slab); stok = ctx.stok(dst_slab)
        for ch in range(NCH):
            lch, lctok = ctx.ln_chunk(ch)
            dfull = bf_view(sl, ch * 512, 512)
            mc = ccol_all[key][:, ch:ch + 1]
            oc = ctx.smalls[f"{prefix}_om{key}"][:, ch:ch + 1]
            if q == 0:
                b.emit("dve", lambda e, o=bf_view(sl, ch * 512 + 1, 511), i_=lch[:, t0:t0 + 511], s=oc:
                       e.tensor_scalar(o, i_, s, None, ALU.mult),
                       reads=(lctok, f"{prefix}_om{key}"), writes=(stok,))
                b.emit("dve", lambda e, o=bf_view(sl, ch * 512, 1): e.memset(o, 0.0),
                       reads=(), writes=(stok,))
            else:
                b.emit("dve", lambda e, o=dfull, i_=lch[:, t0 - 1:t0 + 511], s=oc:
                       e.tensor_scalar(o, i_, s, None, ALU.mult),
                       reads=(lctok, f"{prefix}_om{key}"), writes=(stok,))
            b.emit("dve", lambda e, o=dfull, i_=lch[:, t0:t0 + 512], s=mc, i2=dfull:
                   e.scalar_tensor_tensor(o, i_, s, i2, ALU.mult, ALU.add),
                   reads=(lctok, f"{prefix}_{key}", stok), writes=(stok,))
    ccol_all = ccol

    for q in range(4):
        t0 = q * QT
        xmr = ctx.alloc_slab()
        emit_mix(xmr, "mr", "l0a", t0, q)
        wsl_a = [ctx.alloc_slab(), ctx.alloc_slab()]
        wr_acc = load_w_1024(ctx, W["wrT"], wsl_a, "wqa")
        sr_sl = [ctx.alloc_slab(), ctx.alloc_slab()]
        for co in range(NCH):
            pbr = ctx.pb[co % 2]; pbrt = f"PB{co % 2}"
            for ci in range(NCH):
                wv_, wtok = wr_acc(ci)
                b.emit("pe", lambda e, o=pbr[:], l=wv_[:, co * 128:co * 128 + 128],
                       r=bf_view(ctx.slab(xmr), ci * 512, 512), s=(ci == 0), p=(ci == 7):
                       e.matmul(o, l, r, start=s, stop=p),
                       reads=(wtok, ctx.stok(xmr)), writes=(pbrt,))
            srv = f_view(ctx.slab(sr_sl[co // 4]), (co % 4) * 512, 512)
            b.emit("act", lambda e, o=srv, i_=pbr[:]: e.activation(o, i_, AF.Sigmoid),
                   reads=(pbrt,), writes=(ctx.stok(sr_sl[co // 4]),))
        ctx.free_slab(xmr)
        xmk = ctx.alloc_slab(); xmv = ctx.alloc_slab()
        emit_mix(xmk, "mk", "l0a", t0, q)
        emit_mix(xmv, "mv", "l0a", t0, q)
        wsl_b = [ctx.alloc_slab(), ctx.alloc_slab()]
        wk_acc = load_w_1024(ctx, W["wkT"], wsl_a, "wqa")
        wv_acc = load_w_1024(ctx, W["wvT"], wsl_b, "wqb")
        kvsl = ctx.alloc_slab()
        wkvA = ctx.alloc_slab(); wkvB = ctx.alloc_slab()
        rwk = ctx.alloc_slab()
        for co in range(NCH):
            h = co % 2
            kview = f_view(ctx.slab(kvsl), h * 1024, 512)
            vview = f_view(ctx.slab(kvsl), h * 1024 + 512, 512)
            kvt = ctx.stok(kvsl, str(h))
            srv = f_view(ctx.slab(sr_sl[co // 4]), (co % 4) * 512, 512)
            srt = ctx.stok(sr_sl[co // 4])
            pbk, pbv = ctx.pb[h], ctx.pb[2 + h]
            pbkt, pbvt = f"PB{h}", f"PB{2 + h}"
            for ci in range(NCH):
                wv_, wtok = wk_acc(ci)
                b.emit("pe", lambda e, o=pbk[:], l=wv_[:, co * 128:co * 128 + 128],
                       r=bf_view(ctx.slab(xmk), ci * 512, 512), s=(ci == 0), p=(ci == 7):
                       e.matmul(o, l, r, start=s, stop=p),
                       reads=(wtok, ctx.stok(xmk)), writes=(pbkt,))
            for ci in range(NCH):
                wv_, wtok = wv_acc(ci)
                b.emit("pe", lambda e, o=pbv[:], l=wv_[:, co * 128:co * 128 + 128],
                       r=bf_view(ctx.slab(xmv), ci * 512, 512), s=(ci == 0), p=(ci == 7):
                       e.matmul(o, l, r, start=s, stop=p),
                       reads=(wtok, ctx.stok(xmv)), writes=(pbvt,))
            b.emit("dve", lambda e, o=kview, i_=pbk[:]: e.tensor_copy(o, i_),
                   reads=(pbkt,), writes=(kvt,))
            b.emit("dve", lambda e, o=vview, i_=pbv[:]: e.tensor_copy(o, i_),
                   reads=(pbvt,), writes=(kvt,))
            sA = ctx.slab(wkvA)
            ek = f_view(sA, 0, 512); ekv = f_view(sA, 512, 512)
            P = f_view(sA, 1024, 512); Q = f_view(sA, 1536, 512)
            Psh = f_view(ctx.slab(wkvB), h * 512, 512)
            at = ctx.stok(wkvA)
            sht = ctx.stok(wkvB, str(h))
            dbro = dcol[:, co:co + 1].to_broadcast([128, 512])
            eucc = eucol[:, co:co + 1]
            pcc = pc[:, co:co + 1]; qcc = qc[:, co:co + 1]
            b.emit("act", lambda e, o=ek, i_=kview: e.activation(o, i_, AF.Exp),
                   reads=(kvt,), writes=(at,), tag="wkv_ek")
            b.emit("dve", lambda e, o=ekv, a=ek, c=vview: e.tensor_mul(o, a, c),
                   reads=(at, kvt), writes=(at,))
            b.emit("dve", lambda e, o=P, d=dbro, x_=ekv, init=pcc:
                   e.tensor_tensor_scan(o, d, x_, init, ALU.mult, ALU.add),
                   reads=(at, "l0a_d", "l0a_Pc"), writes=(at,), tag="scanP")
            b.emit("dve", lambda e, o=Psh[:, 0:1], i_=pcc: e.tensor_copy(o, i_),
                   reads=("l0a_Pc",), writes=(sht,))
            b.emit("dve", lambda e, o=Psh[:, 1:512], i_=P[:, 0:511]: e.tensor_copy(o, i_),
                   reads=(at,), writes=(sht,))
            b.emit("dve", lambda e, o=pcc, i_=P[:, 511:512]: e.tensor_copy(o, i_),
                   reads=(at,), writes=("l0a_Pc",))
            b.emit("dve", lambda e, o=P, a=ek, s=eucc, c=vview:
                   e.scalar_tensor_tensor(o, a, s, c, ALU.mult, ALU.mult),
                   reads=(at, "l0a_eu", kvt), writes=(at,), tag="eukv")
            b.emit("dve", lambda e, o=P, a=P, c=Psh: e.tensor_add(o, a, c),
                   reads=(at, sht), writes=(at,), tag="num")
            b.emit("dve", lambda e, o=Q, d=dbro, x_=ek, init=qcc:
                   e.tensor_tensor_scan(o, d, x_, init, ALU.mult, ALU.add),
                   reads=(at, "l0a_d", "l0a_Qc"), writes=(at,), tag="scanQ")
            b.emit("dve", lambda e, o=Psh[:, 0:1], i_=qcc: e.tensor_copy(o, i_),
                   reads=("l0a_Qc",), writes=(sht,))
            b.emit("dve", lambda e, o=Psh[:, 1:512], i_=Q[:, 0:511]: e.tensor_copy(o, i_),
                   reads=(at,), writes=(sht,))
            b.emit("dve", lambda e, o=qcc, i_=Q[:, 511:512]: e.tensor_copy(o, i_),
                   reads=(at,), writes=("l0a_Qc",))
            b.emit("dve", lambda e, o=Q, a=ek, s=eucc, c=Psh:
                   e.scalar_tensor_tensor(o, a, s, c, ALU.mult, ALU.add),
                   reads=(at, "l0a_eu", sht), writes=(at,), tag="den")
            b.emit("dve", lambda e, o=ekv, i_=Q: e.reciprocal(o, i_),
                   reads=(at,), writes=(at,))
            b.emit("dve", lambda e, o=ek, a=P, c=ekv: e.tensor_mul(o, a, c),
                   reads=(at,), writes=(at,), tag="wkvout")
            rw = bf_view(ctx.slab(rwk), co * 512, 512)
            b.emit("dve", lambda e, o=rw, a=ek, c=srv: e.tensor_mul(o, a, c),
                   reads=(at, srt), writes=(ctx.stok(rwk),), tag="rwkv")
        wo_acc = load_w_1024(ctx, W["woT"], wsl_b, "wqb")
        for tt in range(4):
            ti = (t0 + tt * 128) // 128
            xv, xtok = ctx.x_tile(ti)
            for half in range(2):
                pbo = ctx.pb[4 + (tt * 2 + half) % 2]
                pbot = f"PB{4 + (tt * 2 + half) % 2}"
                for co in range(NCH):
                    wv_, wtok = wo_acc(co)
                    b.emit("pe", lambda e, o=pbo[:], l=bf_view(ctx.slab(rwk), co * 512 + tt * 128, 128),
                           r=wv_[:, half * 512:half * 512 + 512], s=(co == 0), p=(co == 7):
                           e.matmul(o, l, r, start=s, stop=p),
                           reads=(wtok, ctx.stok(rwk)), writes=(pbot,))
                dst = xv[:, half * 512:half * 512 + 512]
                b.emit("dve", lambda e, o=dst, a=dst, c=pbo[:]: e.tensor_add(o, a, c),
                       reads=(pbot, xtok), writes=(xtok,), tag="resid")
        for s in (wsl_a[0], wsl_a[1], wsl_b[0], wsl_b[1], sr_sl[0], sr_sl[1],
                  xmk, xmv, kvsl, wkvA, wkvB, rwk):
            ctx.free_slab(s)


def emit_l0_ffn(ctx, W):
    b = ctx.bld
    emit_ln(ctx, "ln0f")
    QT = 512
    ccol = {}
    for nm in ("mur", "muk"):
        t = ctx.small(f"l0f_{nm}", 128, 8, F32)
        b.emit("sp", lambda e, o=t[:], i_=W[nm]: e.dma_start(out=o, in_=i_),
               reads=(), writes=(f"l0f_{nm}",), dma="cis")
        ccol[nm] = t
        t2 = ctx.small(f"l0f_om{nm}", 128, 8, F32)
        b.emit("dve", lambda e, o=t2[:], i_=t[:]:
               e.tensor_scalar(o, i_, -1.0, -1.0, ALU.mult, ALU.subtract),
               reads=(f"l0f_{nm}",), writes=(f"l0f_om{nm}",))

    def emit_mix_f(dst_slab, key, t0, q):
        sl = ctx.slab(dst_slab); stok = ctx.stok(dst_slab)
        for ch in range(NCH):
            lch, lctok = ctx.ln_chunk(ch)
            dfull = bf_view(sl, ch * 512, 512)
            mc = ccol[key][:, ch:ch + 1]
            oc = ctx.smalls[f"l0f_om{key}"][:, ch:ch + 1]
            if q == 0:
                b.emit("dve", lambda e, o=bf_view(sl, ch * 512 + 1, 511), i_=lch[:, t0:t0 + 511], s=oc:
                       e.tensor_scalar(o, i_, s, None, ALU.mult),
                       reads=(lctok, f"l0f_om{key}"), writes=(stok,))
                b.emit("dve", lambda e, o=bf_view(sl, ch * 512, 1): e.memset(o, 0.0),
                       reads=(), writes=(stok,))
            else:
                b.emit("dve", lambda e, o=dfull, i_=lch[:, t0 - 1:t0 + 511], s=oc:
                       e.tensor_scalar(o, i_, s, None, ALU.mult),
                       reads=(lctok, f"l0f_om{key}"), writes=(stok,))
            b.emit("dve", lambda e, o=dfull, i_=lch[:, t0:t0 + 512], s=mc, i2=dfull:
                   e.scalar_tensor_tensor(o, i_, s, i2, ALU.mult, ALU.add),
                   reads=(lctok, f"l0f_{key}", stok), writes=(stok,))

    for q in range(4):
        t0 = q * QT
        xr = ctx.alloc_slab(); xk = ctx.alloc_slab()
        emit_mix_f(xr, "mur", t0, q)
        emit_mix_f(xk, "muk", t0, q)
        wsl_a = [ctx.alloc_slab(), ctx.alloc_slab()]
        wk_acc = load_w_1024(ctx, W["wkT"], wsl_a, "wqa")
        kact = ctx.alloc_slab()
        ksc = ctx.alloc_slab()
        for co in range(NCH):
            pbk = ctx.pb[co % 2]; pbkt = f"PB{co % 2}"
            for ci in range(NCH):
                wv_, wtok = wk_acc(ci)
                b.emit("pe", lambda e, o=pbk[:], l=wv_[:, co * 128:co * 128 + 128],
                       r=bf_view(ctx.slab(xk), ci * 512, 512), s=(ci == 0), p=(ci == 7):
                       e.matmul(o, l, r, start=s, stop=p),
                       reads=(wtok, ctx.stok(xk)), writes=(pbkt,))
            krel = f_view(ctx.slab(ksc), (co % 2) * 512, 512)
            b.emit("act", lambda e, o=krel, i_=pbk[:]: e.activation(o, i_, AF.Relu),
                   reads=(pbkt,), writes=(ctx.stok(ksc, str(co % 2)),))
            b.emit("act", lambda e, o=bf_view(ctx.slab(kact), co * 512, 512), i_=krel:
                   e.activation(o, i_, AF.Square),
                   reads=(ctx.stok(ksc, str(co % 2)),), writes=(ctx.stok(kact),))
        ctx.free_slab(xk)
        wr_acc = load_w_1024(ctx, W["wrT"], wsl_a, "wqa")
        rtm = ctx.alloc_slab()
        for tt in range(4):
            for half in range(2):
                pbr = ctx.pb[2 + (tt * 2 + half) % 2]; pbrt = f"PB{2 + (tt * 2 + half) % 2}"
                for ci in range(NCH):
                    wv_, wtok = wr_acc(ci)
                    b.emit("pe", lambda e, o=pbr[:], l=bf_view(ctx.slab(xr), ci * 512 + tt * 128, 128),
                           r=wv_[:, half * 512:half * 512 + 512], s=(ci == 0), p=(ci == 7):
                           e.matmul(o, l, r, start=s, stop=p),
                           reads=(wtok, ctx.stok(xr)), writes=(pbrt,))
                b.emit("act", lambda e, o=bf_view(ctx.slab(rtm), tt * 1024 + half * 512, 512), i_=pbr[:]:
                       e.activation(o, i_, AF.Sigmoid),
                       reads=(pbrt,), writes=(ctx.stok(rtm),))
        ctx.free_slab(xr)
        wv_acc2 = load_w_1024(ctx, W["wvT"], wsl_a, "wqa")
        for tt in range(4):
            ti = (t0 + tt * 128) // 128
            xv, xtok = ctx.x_tile(ti)
            for half in range(2):
                pbo = ctx.pb[4 + (tt * 2 + half) % 2]; pbot = f"PB{4 + (tt * 2 + half) % 2}"
                for ci in range(NCH):
                    wv_, wtok = wv_acc2(ci)
                    b.emit("pe", lambda e, o=pbo[:], l=bf_view(ctx.slab(kact), ci * 512 + tt * 128, 128),
                           r=wv_[:, half * 512:half * 512 + 512], s=(ci == 0), p=(ci == 7):
                           e.matmul(o, l, r, start=s, stop=p),
                           reads=(wtok, ctx.stok(kact)), writes=(pbot,))
                tmp = f_view(ctx.slab(ksc), 1024 + ((tt * 2 + half) % 2) * 512, 512)
                tmpt = ctx.stok(ksc, "t" + str((tt * 2 + half) % 2))
                b.emit("dve", lambda e, o=tmp, a=pbo[:],
                       c=bf_view(ctx.slab(rtm), tt * 1024 + half * 512, 512):
                       e.tensor_mul(o, a, c),
                       reads=(pbot, ctx.stok(rtm)), writes=(tmpt,))
                dst = xv[:, half * 512:half * 512 + 512]
                b.emit("dve", lambda e, o=dst, a=dst, c=tmp: e.tensor_add(o, a, c),
                       reads=(tmpt, xtok), writes=(xtok,), tag="resid")
        for s in (kact, ksc, rtm, wsl_a[0], wsl_a[1]):
            ctx.free_slab(s)


def wload_cols_big(ctx, dram, slab, slaboff, col0, ncols, sem, tokname):
    b = ctx.bld
    t = dram.tensor
    C = t.shape[-1]
    srcap = AP(t, col0, [[C, 128], [128 * C, 8], [1, ncols]])
    dst = bf_view(ctx.slab(slab), slaboff, 8 * ncols).rearrange("p (c n) -> p c n", c=8)
    tok = ctx.stok(slab, tokname)
    b.emit("pool", lambda e, o=dst, i_=srcap: e.dma_start(out=o, in_=i_),
           reads=(), writes=(tok,), dma=sem)
    return tok


def emit_attn_planB(ctx, W, name, rope=False):
    import os
    NG = int(os.environ.get("ATTN_G", "4"))
    NH = int(os.environ.get("ATTN_H", "4"))
    NQC = int(os.environ.get("ATTN_QC", "4"))
    b = ctx.bld
    emit_ln(ctx, "ln_" + name)
    sc_col = ctx.small("attn_scale", 128, 1, F32)
    bi_col = ctx.small("attn_bias", 128, 1, F32)
    ones64 = ctx.small("ones64bf", 1, 64, BF16)
    b.emit("dve", lambda e: e.memset(sc_col[:], 0.125), reads=(), writes=("attn_scale",))
    b.emit("dve", lambda e: e.memset(bi_col[:], -16.0), reads=(), writes=("attn_bias",))
    b.emit("dve", lambda e: e.memset(ones64[:], 1.0), reads=(), writes=("ones64bf",))
    csl = ctx.alloc_slab()
    trimask = bf_view(ctx.slab(csl), 0, 2048)
    tmtok = ctx.stok(csl, "tm")
    tm = np.zeros((128, 2048), np.float32)
    for j in range(4):
        kk = np.arange(128)[:, None]
        qq = np.arange(512)[None, :]
        tm[:, j * 512:(j + 1) * 512] = (kk <= qq - 128 * j)
    tmi = ctx.nc.inline_tensor(tm.astype(ml_dtypes.bfloat16), "trimask_c_" + name)
    b.emit("sp", lambda e: e.dma_start(out=trimask, in_=tmi.ap()),
           reads=(), writes=(tmtok,), dma="cis")
    if rope:
        half = 32
        inv_freq = (10000.0 ** (-np.arange(half, dtype=np.float64) * 2.0 / 64))
        ang = np.arange(T, dtype=np.float64)[None, :] * inv_freq[:, None]
        cos64 = np.concatenate([np.cos(ang), np.cos(ang)], 0).astype(np.float32)
        sin64 = np.concatenate([np.sin(ang), np.sin(ang)], 0).astype(np.float32)
        cosb = bf_view(ctx.slab(csl), 2048, 2048)[0:64, :]
        cstk = ctx.stok(csl, "cs")
        ci_ = ctx.nc.inline_tensor(cos64.astype(ml_dtypes.bfloat16), "ropecos_c")
        si_ = ctx.nc.inline_tensor(sin64.astype(ml_dtypes.bfloat16), "ropesin_c")
        b.emit("sp", lambda e, o=cosb: e.dma_start(out=o, in_=ci_.ap()), reads=(), writes=(cstk,), dma="cis")
        rope_sl = {}

        def rview(key, c0, cols, scratch=False):
            r0 = 64 if scratch else 0
            return bf_view(ctx.slab(rope_sl[key]), c0, cols)[r0:r0 + 64, :]

        def rtok(key, scratch=False):
            return ctx.stok(rope_sl[key], "s" if scratch else "d")

    rbc = ctx.small("attn_rbc", 64, 512, F32)
    rden = ctx.small("attn_rden", 1, 512, BF16)

    for g in range(NG):
        wsl = ctx.alloc_slab()
        wvsl = ctx.alloc_slab()

        tkq = wload_cols_big(ctx, W["wqT"], wsl, 0, 256 * g, 256, "wqk", "wq")
        tkk = wload_cols_big(ctx, W["wkT"], wsl, 2048, 256 * g, 256, "wqk", "wk")

        def qk_dst(ci, off):
            return (bf_view(ctx.slab(wsl), off + ci * 256, 256), tkq if off == 0 else tkk)
        tkv = wload_cols_big(ctx, W["wvT"], wvsl, 0, 256 * g, 256, "wvo", "wv")

        def wv_dst(ci):
            return (bf_view(ctx.slab(wvsl), ci * 256, 256), tkv)
        woacc = big_wload(ctx, W["woT"], wvsl, "wvo", r0=256 * g, nch=2, ncols=1024, tokname="wo", slaboff=2048)
        tko = ctx.stok(wvsl, "wo")

        ptsl = [ctx.alloc_slab(), ctx.alloc_slab()]
        zsl = ctx.alloc_slab()
        if not rope:
            qsl = ctx.alloc_slab()
            ksl = ctx.alloc_slab()
        else:
            for kk_ in ("qE", "qO", "kE", "kO"):
                rope_sl[kk_] = ctx.alloc_slab()
            sinb = bf_view(ctx.slab(ptsl[0]), 0, 2048)[0:64, :]
            sintok = ctx.stok(ptsl[0], "sin")
            b.emit("sp", lambda e, o=sinb: e.dma_start(out=o, in_=si_.ap()),
                   reads=(), writes=(sintok,), dma="cis")
        vsl = [ctx.alloc_slab(), ctx.alloc_slab()]

        def proj_qk(which, off, dst_view_fn, dst_tok_fn):
            for p in range(2):
                for ci in range(8):
                    wv_, wtok = qk_dst(ci, off)
                    lch, lctok = ctx.ln_chunk(ci)
                    for tq in range(4):
                        b.emit("pe", lambda e, o=ctx.pb[4 + tq][:], l=wv_[:, p * 128:p * 128 + 128],
                               r=lch[:, tq * 512:(tq + 1) * 512], s=(ci == 0), pp=(ci == 7):
                               e.matmul(o, l, r, start=s, stop=pp),
                               reads=(wtok, lctok), writes=(f"PB{4 + tq}",))
                for tq in range(4):
                    dst_view_fn(which, p, tq, ctx.pb[4 + tq], f"PB{4 + tq}")

        if not rope:
            def evac_qk(which, p, tq, pbq, pbt):
                sl = qsl if which == "q" else ksl
                dst = bf_view(ctx.slab(sl), p * 2048 + tq * 512, 512)
                b.emit("dve" if tq % 2 == 0 else "act",
                       lambda e, o=dst, i_=pbq[:], tt=tq % 2: (e.tensor_copy(o, i_) if tt == 0 else e.copy(o, i_)),
                       reads=(pbt,), writes=(ctx.stok(sl, str(p)),))
            proj_qk("q", 0, evac_qk, None)
            proj_qk("k", 2048, evac_qk, None)

            def q_view(h, c0, cols):
                return bf_view(ctx.slab(qsl), (h // 2) * 2048 + c0, cols)[(h % 2) * 64:(h % 2) * 64 + 64, :], ctx.stok(qsl, str(h // 2))

            def k_view(h, c0, cols):
                return bf_view(ctx.slab(ksl), (h // 2) * 2048 + c0, cols)[(h % 2) * 64:(h % 2) * 64 + 64, :], ctx.stok(ksl, str(h // 2))
        else:
            def evac_rope(which, p, tq, pbq, pbt):
                ek, ok_ = which + "E", which + "O"
                dste = rview(ek, p * T + tq * 512, 512)
                dsto = rview(ok_, p * T + tq * 512, 512)
                b.emit("dve", lambda e, o=dste, i_=pbq[0:64, :]: e.tensor_copy(o, i_),
                       reads=(pbt,), writes=(rtok(ek),))
                b.emit("act", lambda e, o=dsto, i_=pbq[64:128, :]: e.copy(o, i_),
                       reads=(pbt,), writes=(rtok(ok_),))
            proj_qk("q", 0, evac_rope, None)
            proj_qk("k", 2048, evac_rope, None)
            for which in ("q", "k"):
                ek, ok_ = which + "E", which + "O"
                for p in range(2):
                    E = rview(ek, p * T, T)
                    O = rview(ok_, p * T, T)
                    t1 = rview(ek, 0, T, scratch=True)
                    t2 = rview(ek, T, T, scratch=True)
                    t3 = rview(ok_, 0, T, scratch=True)
                    t4 = rview(ok_, T, T, scratch=True)
                    s1t = rtok(ek, True) + "a"; s2t = rtok(ek, True) + "b"
                    s3t = rtok(ok_, True) + "a"; s4t = rtok(ok_, True) + "b"
                    b.emit("dve", lambda e, o=t1, a=E, c=cosb: e.tensor_mul(o, a, c),
                           reads=(rtok(ek), cstk), writes=(s1t,))
                    b.emit("dve", lambda e, o=t2, a=O, c=sinb: e.tensor_mul(o, a, c),
                           reads=(rtok(ok_), sintok), writes=(s2t,))
                    b.emit("dve", lambda e, o=t3, a=E, c=sinb: e.tensor_mul(o, a, c),
                           reads=(rtok(ek), sintok), writes=(s3t,))
                    b.emit("dve", lambda e, o=t4, a=O, c=cosb: e.tensor_mul(o, a, c),
                           reads=(rtok(ok_), cstk), writes=(s4t,))
                    b.emit("dve", lambda e, o=E, a=t1, c=t2: e.tensor_sub(o, a, c),
                           reads=(s1t, s2t), writes=(rtok(ek),))
                    b.emit("dve", lambda e, o=O, a=t3, c=t4: e.tensor_add(o, a, c),
                           reads=(s3t, s4t), writes=(rtok(ok_),))

            def rope_qk_views(which, h, c0, cols):
                ek, ok_ = which + "E", which + "O"
                p, r = h // 2, h % 2
                ev = rview(ek, p * T + c0, cols)[32 * r:32 * r + 32, :]
                ov = rview(ok_, p * T + c0, cols)[32 * r:32 * r + 32, :]
                return (ev, rtok(ek)), (ov, rtok(ok_))

        for ti in range(NT):
            pbv = ctx.pb[7]
            for ci in range(8):
                wv_, wtok = wv_dst(ci)
                lch, lctok = ctx.ln_chunk(ci)
                b.emit("pe", lambda e, o=pbv[:, 0:256], l=lch[:, ti * 128:(ti + 1) * 128],
                       r=wv_, s=(ci == 0), pp=(ci == 7): e.matmul(o, l, r, start=s, stop=pp),
                       reads=(wtok, lctok), writes=("PB7",))
            sl = vsl[ti // 8]
            base = (ti % 8) * 260
            vtok = ctx.stok(sl, f"v{ti % 8}")
            for hh in range(4):
                dst = bf_view(ctx.slab(sl), base + hh * 65, 64)
                b.emit("dve" if hh % 2 == 0 else "act",
                       lambda e, o=dst, i_=pbv[:, hh * 64:(hh + 1) * 64], hh=hh:
                       (e.tensor_copy(o, i_) if hh % 2 == 0 else e.copy(o, i_)),
                       reads=("PB7",), writes=(vtok,))
                b.emit("dve", lambda e, o=bf_view(ctx.slab(sl), base + hh * 65 + 64, 1): e.memset(o, 1.0),
                       reads=(), writes=(vtok,))

        def v_aug(h, kt):
            ti = kt
            sl = vsl[ti // 8]
            return bf_view(ctx.slab(sl), (ti % 8) * 260 + h * 65, 65), ctx.stok(sl, f"v{ti % 8}")

        for fc_ in range(2):
            b.emit("dve", lambda e, o=bf_view(ctx.slab(zsl), fc_ * 2048, 2048): e.memset(o, 0.0),
                   reads=(), writes=(ctx.stok(zsl, str(fc_)),))
        for h in range(NH):
            habs = 4 * g + h
            for qc in range(NQC):
                q0 = qc * 512
                nkt = 4 * (qc + 1)
                pt_toks = []
                for kt in range(nkt):
                    pbS = ctx.pb[kt % 2]
                    pbSt = f"PB{kt % 2}"
                    if not rope:
                        qv, qvt = q_view(h, q0, 512)
                        kv, kvt_ = k_view(h, kt * 128, 128)
                        b.emit("pe", lambda e, o=pbS[:], l=kv, r=qv:
                               e.matmul(o, l, r, start=True, stop=True),
                               reads=(kvt_, qvt), writes=(pbSt,))
                    else:
                        (qev, qet), (qov, qot) = rope_qk_views("q", h, q0, 512)
                        (kev, ket), (kov, kot) = rope_qk_views("k", h, kt * 128, 128)
                        b.emit("pe", lambda e, o=pbS[:], l=kev, r=qev:
                               e.matmul(o, l, r, start=True, stop=False),
                               reads=(ket, qet), writes=(pbSt,))
                        b.emit("pe", lambda e, o=pbS[:], l=kov, r=qov:
                               e.matmul(o, l, r, start=False, stop=True),
                               reads=(kot, qot), writes=(pbSt,))
                    pt = bf_view(ctx.slab(ptsl[kt // 8]), (kt % 8) * 512, 512)
                    ptt = ctx.stok(ptsl[kt // 8], f"p{kt % 8}")
                    b.emit("act", lambda e, o=pt, i_=pbS[:]:
                           e.activation(o, i_, AF.Exp, bias=bi_col[:], scale=sc_col[:]),
                           reads=(pbSt, "attn_scale", "attn_bias"), writes=(ptt,))
                    d0 = kt * 128 - q0
                    if d0 >= 0:
                        ms = trimask[:, (d0 // 128) * 512:(d0 // 128) * 512 + 512]
                        b.emit("dve", lambda e, o=pt, a=pt, c=ms: e.tensor_mul(o, a, c),
                               reads=(ptt, tmtok), writes=(ptt,))
                    pt_toks.append((pt, ptt))
                pbZ = ctx.pb[2 + qc % 2]
                pbZt = f"PB{2 + qc % 2}"
                for kt in range(nkt):
                    vv, vt = v_aug(h, kt)
                    pt, ptt = pt_toks[kt]
                    b.emit("pe", lambda e, o=pbZ[0:65, :], l=vv, r=pt, s=(kt == 0), pp=(kt == nkt - 1):
                           e.matmul(o, l, r, start=s, stop=pp),
                           reads=(vt, ptt), writes=(pbZt,))
                b.emit("dve", lambda e, o=rden[:], i_=pbZ[64:65, :]: e.reciprocal(o, i_),
                       reads=(pbZt,), writes=("attn_rden",))
                pbB = ctx.pb[4]
                b.emit("pe", lambda e, o=pbB[0:64, :], l=ones64[:], r=rden[:]:
                       e.matmul(o, l, r, start=True, stop=True),
                       reads=("ones64bf", "attn_rden"), writes=("PB4",))
                b.emit("act", lambda e, o=rbc[:], i_=pbB[0:64, :]: e.copy(o, i_),
                       reads=("PB4",), writes=("attn_rbc",))
                zv = bf_view(ctx.slab(zsl), (h // 2) * 2048 + q0, 512)[(h % 2) * 64:(h % 2) * 64 + 64, :]
                b.emit("dve", lambda e, o=zv, a=pbZ[0:64, :], c=rbc[:]: e.tensor_mul(o, a, c),
                       reads=(pbZt, "attn_rbc"), writes=(ctx.stok(zsl, str(h // 2)),))
        for ti in range(NT):
            xv, xtok = ctx.x_tile(ti)
            for fc in range(2):
                lhsT = bf_view(ctx.slab(zsl), fc * 2048 + ti * 128, 128)
                for half in range(2):
                    rhs = bf_view(ctx.slab(wvsl), 2048 + fc * 1024 + half * 512, 512)
                    b.emit("pe", lambda e, o=ctx.pb[5 + half][:], l=lhsT, r=rhs, s=(fc == 0), pp=(fc == 1):
                           e.matmul(o, l, r, start=s, stop=pp),
                           reads=(ctx.stok(zsl, str(fc)), tko), writes=(f"PB{5 + half}",))
            for half in range(2):
                dst = xv[:, half * 512:half * 512 + 512]
                b.emit("dve", lambda e, o=dst, a=dst, c=ctx.pb[5 + half][:]: e.tensor_add(o, a, c),
                       reads=(f"PB{5 + half}", xtok), writes=(xtok,), tag="resid")
        for s in [wsl, wvsl, vsl[0], vsl[1], ptsl[0], ptsl[1], zsl] + (list(rope_sl.values()) if rope else [qsl, ksl]):
            ctx.free_slab(s)
    ctx.free_slab(csl)


def emit_l2_attn(ctx, W):
    emit_attn_planB(ctx, W, "l2a", rope=False)


def emit_l3_attn(ctx, W):
    emit_attn_planB(ctx, W, "l3a", rope=True)




def big_wload(ctx, dram_ap, slab, sem, r0=0, c0=0, nch=8, ncols=1024, ee=None,
              tokname="w", slaboff=0):
    b = ctx.bld
    t = dram_ap.tensor
    shp = list(t.shape)
    C = shp[-1]
    base = r0 * C + c0
    if ee is not None:
        base += ee * shp[-2] * C
    srcap = AP(t, base, [[C, 128], [128 * C, nch], [1, ncols]])
    dst = bf_view(ctx.slab(slab), slaboff, nch * ncols).rearrange("p (c n) -> p c n", c=nch)
    tok = ctx.stok(slab, tokname)
    b.emit("pool", lambda e, o=dst, i_=srcap: e.dma_start(out=o, in_=i_),
           reads=(), writes=(tok,), dma=sem)

    def acc(ci):
        return bf_view(ctx.slab(slab), slaboff + ci * ncols, ncols), tok
    return acc


def cload(ctx, dram_ap, dst_view, dsttok, nrows, ncols, stage_slab, stage_off=0):
    b = ctx.bld
    sv = f_view(ctx.slab(stage_slab), stage_off, ncols)[0:nrows, :]
    stok = ctx.stok(stage_slab, f"cl{stage_off}")
    b.emit("sp", lambda e, o=sv, i_=dram_ap: e.dma_start(out=o, in_=i_),
           reads=(), writes=(stok,), dma="cis")
    b.emit("dve", lambda e, o=dst_view, i_=sv: e.tensor_copy(o, i_),
           reads=(stok,), writes=(dsttok,))


RSTR = 3073
PSTR = 3072


def emit_l1_attn(ctx, W):
    b = ctx.bld
    nc = ctx.nc
    emit_ln(ctx, "ln_l1a")
    rbufs = [nc.dram_tensor(f"l1a_rbuf{p}", [T * RSTR + 8], BF16, kind="Internal") for p in range(2)]
    pbufs = [nc.dram_tensor(f"l1a_pbuf{p}", [T * PSTR + 8], BF16, kind="Internal") for p in range(2)]
    sc_col = ctx.small("attn_scale", 128, 1, F32)
    bi_col = ctx.small("attn_bias", 128, 1, F32)
    b.emit("dve", lambda e: e.memset(sc_col[:], 0.125), reads=(), writes=("attn_scale",))
    b.emit("dve", lambda e: e.memset(bi_col[:], -16.0), reads=(), writes=("attn_bias",))
    padsl = ctx.alloc_slab()
    zt = bf_view(ctx.slab(padsl), 0, 1024)
    ng = bf_view(ctx.slab(padsl), 1024, 1024)
    zto = ctx.stok(padsl, "z")
    ngo = ctx.stok(padsl, "n")
    b.emit("dve", lambda e, o=zt: e.memset(o, 0.0), reads=(), writes=(zto,))
    b.emit("dve", lambda e, o=ng: e.memset(o, -1e5), reads=(), writes=(ngo,))
    for p in range(2):
        for q0 in range(0, T, 128):
            b.emit("pool", lambda e, o=AP(rbufs[p], q0 * RSTR, [[RSTR, 128], [1, 1024]]), i_=zt:
                   e.dma_start(out=o, in_=i_), reads=(zto,), writes=(f"rpad{p}",), dma="skwp")
            b.emit("pool", lambda e, o=AP(rbufs[p], q0 * RSTR + 2049, [[RSTR, 128], [1, 1024]]), i_=ng:
                   e.dma_start(out=o, in_=i_), reads=(ngo,), writes=(f"rpad{p}",), dma="skwp")
            b.emit("pool", lambda e, o=AP(pbufs[p], q0 * PSTR, [[PSTR, 128], [1, 1024]]), i_=zt:
                   e.dma_start(out=o, in_=i_), reads=(zto,), writes=(f"ppad{p}",), dma="skwp")
    ctx.free_slab(padsl)
    cstage = ctx.alloc_slab()
    l1csl = ctx.alloc_slab()
    pek_full = bf_view(ctx.slab(l1csl), 0, 2048)
    pek = pek_full[:, 0:1025]
    pktok = ctx.stok(l1csl, "pek")
    cload(ctx, W["pekT"], pek[0:64, :], pktok, 64, 1025, cstage)
    b.emit("dve", lambda e, o=pek[64:128, :], i_=pek[0:64, :]: e.tensor_copy(o, i_),
           reads=(pktok,), writes=(pktok,))
    peva = bf_view(ctx.slab(l1csl), 2048, 520)
    pvtok = ctx.stok(l1csl, "peva")
    for jc in range(8):
        cload(ctx, W["pev_aug"][jc * 128:(jc + 1) * 128, :], peva[:, jc * 65:(jc + 1) * 65],
              pvtok, 128, 65, cstage, stage_off=1025 + 80 * (jc % 4))
    pev0 = ctx.small("l1a_pev0", 1, 64, BF16)
    cload(ctx, W["pev0"], pev0[:], "l1a_pev0", 1, 64, cstage, stage_off=1500)
    ctx.free_slab(cstage)
    r0all = ctx.small("l1a_r0", 128, 16, F32)
    dencol = ctx.small("l1a_den", 128, 4, F32)

    for g in range(8):
        wsl = ctx.alloc_slab()
        qksl = ctx.alloc_slab()
        vsl_z = ctx.alloc_slab()
        ssl = [ctx.alloc_slab(), ctx.alloc_slab()]
        rbsl = ctx.alloc_slab()
        ppsl = [ctx.alloc_slab(), ctx.alloc_slab()]
        stsl = ctx.alloc_slab()
        tq_ = wload_cols_big(ctx, W["wqT"], wsl, 0, g * 128, 128, "wqk", "wq")
        tk_ = wload_cols_big(ctx, W["wkT"], wsl, 1024, g * 128, 128, "wqk", "wk")
        tv_ = wload_cols_big(ctx, W["wvT"], wsl, 2048, g * 128, 128, "wqk", "wv")
        big_wload(ctx, W["woT"], wsl, "wqk", r0=g * 128, nch=1, ncols=1024, tokname="wo", slaboff=3072)
        l1a_wtok = {0: tq_, 1024: tk_, 2048: tv_}
        for which, off in (("q", 0), ("k", 1024)):
            for ci in range(8):
                lch, lctok = ctx.ln_chunk(ci)
                wv_ = bf_view(ctx.slab(wsl), off + ci * 128, 128)
                for tq in range(4):
                    b.emit("pe", lambda e, o=ctx.pb[4 + tq][:], l=wv_, r=lch[:, tq * 512:(tq + 1) * 512],
                           s=(ci == 0), pp=(ci == 7): e.matmul(o, l, r, start=s, stop=pp),
                           reads=(l1a_wtok[off], lctok), writes=(f"PB{4 + tq}",))
            for tq in range(4):
                dst = bf_view(ctx.slab(qksl), (0 if which == "q" else 2048) + tq * 512, 512)
                b.emit("dve" if tq % 2 == 0 else "act",
                       lambda e, o=dst, i_=ctx.pb[4 + tq][:], m=tq % 2: (e.tensor_copy(o, i_) if m == 0 else e.copy(o, i_)),
                       reads=(f"PB{4 + tq}",), writes=(ctx.stok(qksl, which),))
        for ti in range(NT):
            pbv = ctx.pb[7]
            for ci in range(8):
                lch, lctok = ctx.ln_chunk(ci)
                wv_ = bf_view(ctx.slab(wsl), 2048 + ci * 128, 128)
                b.emit("pe", lambda e, o=pbv[:, 0:128], l=lch[:, ti * 128:(ti + 1) * 128], r=wv_,
                       s=(ci == 0), pp=(ci == 7): e.matmul(o, l, r, start=s, stop=pp),
                       reads=(l1a_wtok[2048], lctok), writes=("PB7",))
            b.emit("dve", lambda e, o=bf_view(ctx.slab(vsl_z), ti * 128, 128), i_=pbv[:, 0:128]:
                   e.tensor_copy(o, i_),
                   reads=("PB7",), writes=(ctx.stok(vsl_z, "v"),))

        for hh in range(2):
            h = 2 * g + hh
            hp = h % 2
            rbuf, pbuf = rbufs[hp], pbufs[hp]
            qv_all = bf_view(ctx.slab(qksl), 0, 2048)[hh * 64:hh * 64 + 64, :]
            kv_all = bf_view(ctx.slab(qksl), 2048, 2048)[hh * 64:hh * 64 + 64, :]
            qtok, ktok = ctx.stok(qksl, "q"), ctx.stok(qksl, "k")
            tidx = 0

            def em_transpose(dst_sv, dst_tok, src_ap, read_toks):
                nonlocal tidx
                eng = "act" if tidx % 2 == 0 else "sp"
                ch = "trp" if tidx % 2 == 0 else "trps"
                tidx += 1
                b.emit(eng, lambda e, o=dst_sv, i_=src_ap: e.dma_start_transpose(o, i_),
                       reads=read_toks, writes=(dst_tok,), dma=ch)

            def phase_R(qt):
                q0 = qt * 128
                lhsT = qv_all[:, q0:q0 + 128]
                for part, (c0, ncol, pbi) in enumerate(((0, 512, 5), (512, 512, 6), (1024, 1, 4))):
                    pbr = ctx.pb[pbi]
                    b.emit("pe", lambda e, o=pbr[:, 0:ncol], l=lhsT,
                           r=pek[hh * 64:hh * 64 + 64, c0:c0 + ncol]:
                           e.matmul(o, l, r, start=True, stop=True),
                           reads=(qtok, pktok), writes=(f"PB{pbi}",))
                r0c = r0all[:, qt:qt + 1]
                b.emit("dve", lambda e, o=r0c, i_=ctx.pb[5][:, 0:1]: e.tensor_copy(o, i_),
                       reads=("PB5",), writes=(f"l1a_r0.{qt}",))
                rpar = qt % 2
                rview = bf_view(ctx.slab(rbsl), rpar * 1026, 1025)
                rtok = ctx.stok(rbsl, f"r{rpar}")
                for part, (c0, ncol, pbi) in enumerate(((0, 512, 5), (512, 512, 6), (1024, 1, 4))):
                    b.emit("dve", lambda e, o=rview[:, c0:c0 + ncol], i_=ctx.pb[pbi][:, 0:ncol], s=r0c:
                           e.tensor_scalar(o, i_, s, None, ALU.subtract),
                           reads=(f"PB{pbi}", f"l1a_r0.{qt}"), writes=(rtok,))
                b.emit("pool", lambda e, o=AP(rbuf, q0 * RSTR + 1024, [[RSTR, 128], [1, 1025]]),
                       i_=rview:
                       e.dma_start(out=o, in_=i_), reads=(rtok,), writes=(f"rbR{hp}.{qt}",), dma="skwp")

            def phase_SP(qt):
                q0 = qt * 128
                nck = (q0 + 128 + 511) // 512
                klen = nck * 512
                bsb = bf_view(ctx.slab(rbsl), 2048, klen)
                btok = ctx.stok(rbsl, "b")
                b.emit("sp", lambda e, o=bsb, i_=AP(rbuf, q0 * PSTR + 2048, [[PSTR, 128], [1, klen]]):
                       e.dma_start(out=o, in_=i_),
                       reads=(f"rbR{hp}.{qt}", f"rpad{hp}"), writes=(btok,), dma="skw")
                spar = qt % 2
                stok_s = ctx.stok(ssl[spar], "s")
                for kc in range(nck):
                    pbS = ctx.pb[kc % 2]
                    pbSt = f"PB{kc % 2}"
                    b.emit("pe", lambda e, o=pbS[:], l=qv_all[:, q0:q0 + 128],
                           r=kv_all[:, kc * 512:(kc + 1) * 512]:
                           e.matmul(o, l, r, start=True, stop=True),
                           reads=(qtok, ktok), writes=(pbSt,))
                    sdst = f_view(ctx.slab(ssl[spar]), kc * 512, 512)
                    b.emit("dve", lambda e, o=sdst, i_=pbS[:], s=r0all[:, qt:qt + 1],
                           i2=bsb[:, kc * 512:(kc + 1) * 512]:
                           e.scalar_tensor_tensor(o, i_, s, i2, ALU.add, ALU.add),
                           reads=(pbSt, f"l1a_r0.{qt}", btok), writes=(stok_s,))
                dn = dencol[:, spar * 2:spar * 2 + 1]
                pp = ppsl[spar]
                pe_sb = bf_view(ctx.slab(pp), 0, klen)
                b.emit("act", lambda e, o=pe_sb, i_=f_view(ctx.slab(ssl[spar]), 0, klen), a=dn:
                       e.activation(o, i_, AF.Exp, bias=bi_col[:], scale=sc_col[:], accum_out=a),
                       reads=(stok_s, "attn_scale", "attn_bias"),
                       writes=(ctx.stok(pp, "e"), f"l1a_den{spar}"))
                rd = dencol[:, spar * 2 + 1:spar * 2 + 2]
                b.emit("dve", lambda e, o=rd, i_=dn: e.reciprocal(o, i_),
                       reads=(f"l1a_den{spar}",), writes=(f"l1a_denr{spar}",))
                pn_sb = bf_view(ctx.slab(pp), 2048, klen)
                b.emit("dve", lambda e, o=pn_sb, i_=pe_sb, s=rd:
                       e.tensor_scalar(o, i_, s, None, ALU.mult),
                       reads=(ctx.stok(pp, "e"), f"l1a_denr{spar}"), writes=(ctx.stok(pp, "n"),))
                b.emit("pool", lambda e, o=AP(pbuf, q0 * PSTR + 1024, [[PSTR, 128], [1, klen]]), i_=pn_sb:
                       e.dma_start(out=o, in_=i_),
                       reads=(ctx.stok(pp, "n"),), writes=(f"pbR{hp}.{qt}",), dma="skwp")

            def phase_AV(qc):
                q0 = qc * 512
                nkc = 4 * (qc + 1)
                pbZ = ctx.pb[2]
                pbZt = "PB2"
                ptoks = tuple(f"pbR{hp}.{4 * qc + j}" for j in range(4)) + (f"ppad{hp}",)

                def em_pev(jc, s_, pp_):
                    j0 = 1 + jc * 128
                    sv = bf_view(ctx.slab(stsl), (jc % 4) * 512, 512)
                    stok_ = ctx.stok(stsl, f"t{jc % 4}")
                    em_transpose(sv, stok_, AP(pbuf, q0 * RSTR + j0, [[RSTR, 512], [1, 128]]), ptoks)
                    b.emit("pe", lambda e, o=pbZ[0:65, :], l=peva[:, jc * 65:(jc + 1) * 65], r=sv,
                           s=s_, pp=pp_:
                           e.matmul(o, l, r, start=s_, stop=pp_),
                           reads=(pvtok, stok_), writes=(pbZt,))
                em_pev(0, True, False)
                for kc in range(nkc):
                    sv = bf_view(ctx.slab(stsl), (kc % 4) * 512, 512)
                    stok_ = ctx.stok(stsl, f"t{kc % 4}")
                    em_transpose(sv, stok_, AP(pbuf, q0 * PSTR + 1024 + kc * 128, [[PSTR, 512], [1, 128]]), ptoks)
                    vv = bf_view(ctx.slab(vsl_z), kc * 128, 128)[:, hh * 64:hh * 64 + 64]
                    b.emit("pe", lambda e, o=pbZ[0:64, :], l=vv, r=sv:
                           e.matmul(o, l, r, start=False, stop=False),
                           reads=(ctx.stok(vsl_z, "v"), stok_), writes=(pbZt,))
                for jc in range(1, 8):
                    em_pev(jc, False, jc == 7)
                w0r = ctx.small("l1a_w0", 1, 512, BF16)
                b.emit("dve", lambda e, o=w0r[:], i_=pbZ[64:65, :]:
                       e.tensor_scalar(o, i_, -1.0, 1.0, ALU.mult, ALU.add),
                       reads=(pbZt,), writes=("l1a_w0",))
                pbB = ctx.pb[3]
                b.emit("pe", lambda e, o=pbB[0:64, :], l=pev0[:], r=w0r[:]:
                       e.matmul(o, l, r, start=True, stop=True),
                       reads=("l1a_pev0", "l1a_w0"), writes=("PB3",))
                zdst = bf_view(ctx.slab(vsl_z), 2048 + q0, 512)[hh * 64:hh * 64 + 64, :]
                ztmp = bf_view(ctx.slab(stsl), 2048, 512)[0:64, :]
                b.emit("dve", lambda e, o=ztmp, i_=pbZ[0:64, :]: e.tensor_copy(o, i_),
                       reads=(pbZt,), writes=(ctx.stok(stsl, "zt"),))
                b.emit("dve", lambda e, o=zdst, a=ztmp, c=pbB[0:64, :]: e.tensor_add(o, a, c),
                       reads=(ctx.stok(stsl, "zt"), "PB3"), writes=(ctx.stok(vsl_z, "z"),))

            for qt in range(NT + 2):
                if qt < NT:
                    phase_R(qt)
                if qt >= 2:
                    phase_SP(qt - 2)
                    if (qt - 2) % 4 == 3:
                        phase_AV((qt - 2) // 4)
        for ti in range(NT):
            xv, xtok = ctx.x_tile(ti)
            lhsT = bf_view(ctx.slab(vsl_z), 2048 + ti * 128, 128)
            for half in range(2):
                b.emit("pe", lambda e, o=ctx.pb[5 + half][:], l=lhsT,
                       r=bf_view(ctx.slab(wsl), 3072 + half * 512, 512):
                       e.matmul(o, l, r, start=True, stop=True),
                       reads=(ctx.stok(vsl_z, "z"), ctx.stok(wsl, "wo")), writes=(f"PB{5 + half}",))
            for half in range(2):
                dst = xv[:, half * 512:half * 512 + 512]
                b.emit("dve", lambda e, o=dst, a=dst, c=ctx.pb[5 + half][:]: e.tensor_add(o, a, c),
                       reads=(f"PB{5 + half}", xtok), writes=(xtok,), tag="resid")
        for s in (wsl, qksl, vsl_z, ssl[0], ssl[1], rbsl, ppsl[0], ppsl[1], stsl):
            ctx.free_slab(s)
    ctx.free_slab(l1csl)


def wunit_load(ctx, dram3, ee, r0, c0, nrows, ncols, sl_pair, sem):
    acc0 = big_wload(ctx, dram3, sl_pair[0], sem, r0=r0, c0=c0, nch=4, ncols=ncols, ee=ee)
    acc1 = big_wload(ctx, dram3, sl_pair[1], sem, r0=r0 + 512, c0=c0, nch=4, ncols=ncols, ee=ee)

    def acc(ci):
        return (acc0 if ci < 4 else acc1)(ci % 4)
    return acc


def emit_cem(ctx, cten, ctok, E, cem, cemtok):
    b = ctx.bld
    identf = ctx.small("identf32", 128, 128, F32)
    for ti in range(NT):
        b.emit("pe", lambda e, o=ctx.pb[7][0:E, 0:128], i_=cten[:, ti * E:(ti + 1) * E], idm=identf[:]:
               e.transpose(o, i_, idm),
               reads=(ctok, "identf32"), writes=("PB7",), tag="ctr")
        b.emit("dve", lambda e, o=cem[:, ti * 128:(ti + 1) * 128] if cem.shape[0] == E else cem[0:E, ti * 128:(ti + 1) * 128], i_=ctx.pb[7][0:E, 0:128]:
               e.tensor_copy(o, i_),
               reads=("PB7",), writes=(cemtok,))


def emit_bcast_row(ctx, src_row, src_tok, dst, dst_tok, n):
    b = ctx.bld
    ones1 = ctx.small("ones1bf", 1, 128, BF16)
    if "ones1bf" not in ctx.bld.res:
        b.emit("dve", lambda e: e.memset(ones1[:], 1.0), reads=(), writes=("ones1bf",))
    b.emit("pe", lambda e, o=ctx.pb[6][:, 0:n], l=ones1[:], r=src_row:
           e.matmul(o, l, r, start=True, stop=True),
           reads=("ones1bf", src_tok), writes=("PB6",))
    b.emit("dve", lambda e, o=dst, i_=ctx.pb[6][:, 0:n]: e.tensor_copy(o, i_),
           reads=("PB6",), writes=(dst_tok,))


def emit_l1_ffn(ctx, W):
    b = ctx.bld
    emit_ln(ctx, "ln_l1f")
    E = 16
    cstage = ctx.alloc_slab()
    fcsl = ctx.alloc_slab()
    rw = ctx.small("l1f_rw", 128, 128, BF16)
    for ci in range(8):
        cload(ctx, W["router_w"][ci * 128:(ci + 1) * 128, :], rw[:, ci * 16:(ci + 1) * 16],
              "l1f_rw", 128, 16, cstage, stage_off=16 * (ci % 4))
    rbv = ctx.small("l1f_rb", 1, 16, BF16)
    cload(ctx, W["router_b"], rbv[:], "l1f_rb", 1, 16, cstage, stage_off=80)
    rb_bc = ctx.small("l1f_rbbc", 128, 16, F32)
    emit_bcast_row(ctx, rbv[:], "l1f_rb", rb_bc[:], "l1f_rbbc", 16)
    cten = ctx.small("l1f_c", 128, NT * E, F32)
    rcol = ctx.small("l1f_rcol", 128, 8, F32)
    top8 = ctx.small("l1f_top8", 128, 8, F32)
    lsc = ctx.small("l1f_lsc", 128, 2 * E, F32)
    for ti in range(NT):
        pbr = ctx.pb[6]
        for ci in range(8):
            lch, lctok = ctx.ln_chunk(ci)
            b.emit("pe", lambda e, o=pbr[:, 0:16], l=lch[:, ti * 128:(ti + 1) * 128],
                   r=rw[:, ci * 16:(ci + 1) * 16], s=(ci == 0), pp=(ci == 7):
                   e.matmul(o, l, r, start=s, stop=pp),
                   reads=(lctok, "l1f_rw"), writes=("PB6",))
        lg = lsc[:, 0:16]; ex = lsc[:, 16:32]
        ct = "l1f_lsc"
        b.emit("dve", lambda e, o=lg, a=pbr[:, 0:16], c=rb_bc[:]: e.tensor_add(o, a, c),
               reads=("PB6", "l1f_rbbc"), writes=(ct,))
        nm = rcol[:, 0:1]; sm = rcol[:, 1:2]; rs = rcol[:, 2:3]; th = rcol[:, 3:4]
        b.emit("dve", lambda e, o=nm, i_=lg: e.reduce_max(o, i_, axis=AX.X, negate=True),
               reads=(ct,), writes=("l1f_rcol_n",))
        b.emit("act", lambda e, o=ex, i_=lg, bi=nm: e.activation(o, i_, AF.Exp, bias=bi),
               reads=(ct, "l1f_rcol_n"), writes=(ct + "e",))
        b.emit("dve", lambda e, o=top8[:], i_=ex: e.max(out=o, in_=i_),
               reads=(ct + "e",), writes=("l1f_top8",))
        b.emit("dve", lambda e, o=th, i_=top8[:, 3:4]: e.tensor_copy(o, i_),
               reads=("l1f_top8",), writes=("l1f_rcol_t",))
        cv = cten[:, ti * E:(ti + 1) * E]
        b.emit("dve", lambda e, o=cv, i_=ex, s=th: e.tensor_scalar(o, i_, s, None, ALU.is_ge),
               reads=(ct + "e", "l1f_rcol_t"), writes=("l1f_c",))
        b.emit("dve", lambda e, o=cv, a=cv, c=ex: e.tensor_mul(o, a, c),
               reads=("l1f_c", ct + "e"), writes=("l1f_c",))
        b.emit("dve", lambda e, o=sm, i_=cv: e.reduce_sum(o, i_, axis=AX.X),
               reads=("l1f_c",), writes=("l1f_rcol_s",))
        b.emit("dve", lambda e, o=rs, i_=sm: e.reciprocal(o, i_),
               reads=("l1f_rcol_s",), writes=("l1f_rcol_r",))
        b.emit("dve", lambda e, o=cv, i_=cv, s=rs: e.tensor_scalar(o, i_, s, None, ALU.mult),
               reads=("l1f_c", "l1f_rcol_r"), writes=("l1f_c",))
    cem = bf_view(ctx.slab(fcsl), 0, 2048)[0:E, :]
    cemtok = ctx.stok(fcsl, "cem")
    emit_cem(ctx, cten, "l1f_c", E, cem, cemtok)
    b1t = ctx.small("l1f_b1", 128, E * 16, F32)
    for ee in range(E):
        b.emit("sp", lambda e, o=b1t[:, ee * 16:(ee + 1) * 16], i_=W["b1p"][ee]:
               e.dma_start(out=o, in_=i_), reads=(), writes=("l1f_b1",), dma="cis")
    b2t = bf_view(ctx.slab(fcsl), 2048, 1024)[0:E, :]
    b2tok = ctx.stok(fcsl, "b2")
    cload(ctx, W["b2"], b2t, b2tok, E, 1024, cstage, stage_off=512)
    ctx.free_slab(cstage)

    HT = 1024
    usl1 = [[ctx.alloc_slab(), ctx.alloc_slab()] for _ in range(2)]
    usl2 = [[ctx.alloc_slab(), ctx.alloc_slab()] for _ in range(2)]
    hsl = [ctx.alloc_slab(), ctx.alloc_slab()]
    for th in range(2):
        t0 = th * HT
        for ee in range(E):
            for ih in range(2):
                par = (ee * 2 + ih) % 2
                w1acc = wunit_load(ctx, W["w1"], ee, 0, ih * 1024, 1024, 1024, usl1[par], "wmo")
                w2acc = wunit_load(ctx, W["w2"], ee, ih * 1024, 0, 1024, 1024, usl2[par], "wmo")
                for icp in range(4):
                    for ci in range(8):
                        lch, lctok = ctx.ln_chunk(ci)
                        for ics in range(2):
                            ic = icp * 2 + ics
                            wv_, wtok = w1acc(ci)
                            for tc2 in range(2):
                                pbm = ctx.pb[ics * 2 + tc2]
                                b.emit("pe", lambda e, o=pbm[:], l=wv_[:, ic * 128:ic * 128 + 128],
                                       r=lch[:, t0 + tc2 * 512:t0 + tc2 * 512 + 512],
                                       s=(ci == 0), pp=(ci == 7):
                                       e.matmul(o, l, r, start=s, stop=pp),
                                       reads=(wtok, lctok), writes=(f"PB{ics * 2 + tc2}",))
                    for ics in range(2):
                        ic = icp * 2 + ics
                        iglob = ih * 8 + ic
                        for tc2 in range(2):
                            hv = bf_view(ctx.slab(hsl[ic // 4]), (ic % 4) * 1024 + tc2 * 512, 512)
                            htok = ctx.stok(hsl[ic // 4], f"h{ic % 4}")
                            b.emit("act", lambda e, o=hv, i_=ctx.pb[ics * 2 + tc2][:],
                                   bi=b1t[:, ee * 16 + iglob:ee * 16 + iglob + 1]:
                                   e.activation(o, i_, AF.Gelu, bias=bi),
                                   reads=(f"PB{ics * 2 + tc2}", "l1f_b1"), writes=(htok,))
                for tt in range(8):
                    ti = (t0 + tt * 128) // 128
                    xv, xtok = ctx.x_tile(ti)
                    bb0 = 4 + (tt % 2) * 2
                    for ic in range(8):
                        wv_, wtok = w2acc(ic)
                        hv = bf_view(ctx.slab(hsl[ic // 4]), (ic % 4) * 1024 + tt * 128, 128)
                        htok = ctx.stok(hsl[ic // 4], f"h{ic % 4}")
                        for hh in range(2):
                            pbo = ctx.pb[bb0 + hh]
                            b.emit("pe", lambda e, o=pbo[:], l=hv, r=wv_[:, hh * 512:hh * 512 + 512],
                                   s=(ic == 0), pp=(ic == 7):
                                   e.matmul(o, l, r, start=s, stop=pp),
                                   reads=(wtok, htok), writes=(f"PB{bb0 + hh}",))
                    for hh in range(2):
                        ccol = cten[:, ti * E + ee:ti * E + ee + 1]
                        dst = xv[:, hh * 512:hh * 512 + 512]
                        b.emit("dve", lambda e, o=dst, i_=ctx.pb[bb0 + hh][:], s=ccol, i2=dst:
                               e.scalar_tensor_tensor(o, i_, s, i2, ALU.mult, ALU.add),
                               reads=(f"PB{bb0 + hh}", "l1f_c", xtok), writes=(xtok,), tag="moeacc")
    for ti in range(NT):
        xv, xtok = ctx.x_tile(ti)
        for hh in range(2):
            pbo = ctx.pb[4 + hh % 2]
            pbot = f"PB{4 + hh % 2}"
            b.emit("pe", lambda e, o=pbo[:], l=cem[:, ti * 128:(ti + 1) * 128],
                   r=b2t[:, hh * 512:hh * 512 + 512]:
                   e.matmul(o, l, r, start=True, stop=True),
                   reads=(cemtok, b2tok), writes=(pbot,))
            dst = xv[:, hh * 512:hh * 512 + 512]
            b.emit("dve", lambda e, o=dst, a=dst, c=pbo[:]: e.tensor_add(o, a, c),
                   reads=(pbot, xtok), writes=(xtok,), tag="resid")
    for pair in usl1 + usl2:
        ctx.free_slab(pair[0]); ctx.free_slab(pair[1])
    ctx.free_slab(hsl[0]); ctx.free_slab(hsl[1])
    ctx.free_slab(fcsl)


def emit_l2_ffn(ctx, W):
    b = ctx.bld
    emit_ln(ctx, "ln_l2f")
    E = 8
    cstage = ctx.alloc_slab()
    fcsl = ctx.alloc_slab()
    rw = ctx.small("l2f_rw", 128, 8 * 9, BF16)
    for ci in range(8):
        cload(ctx, W["wgt"][ci * 128:(ci + 1) * 128, :], rw[:, ci * 9:(ci + 1) * 9],
              "l2f_rw", 128, 9, cstage, stage_off=16 * (ci % 4))
    rbv = ctx.small("l2f_rb", 1, 9, BF16)
    cload(ctx, W["bgt"], rbv[:], "l2f_rb", 1, 9, cstage, stage_off=80)
    rb_bc = ctx.small("l2f_rbbc", 128, 9, F32)
    emit_bcast_row(ctx, rbv[:], "l2f_rb", rb_bc[:], "l2f_rbbc", 9)
    cten = ctx.small("l2f_c", 128, NT * E, F32)
    rcol = ctx.small("l2f_rcol", 128, 8, F32)
    lsc = ctx.small("l2f_lsc", 128, 32, F32)
    for ti in range(NT):
        pbr = ctx.pb[6]
        for ci in range(8):
            lch, lctok = ctx.ln_chunk(ci)
            b.emit("pe", lambda e, o=pbr[:, 0:9], l=lch[:, ti * 128:(ti + 1) * 128],
                   r=rw[:, ci * 9:(ci + 1) * 9], s=(ci == 0), pp=(ci == 7):
                   e.matmul(o, l, r, start=s, stop=pp),
                   reads=(lctok, "l2f_rw"), writes=("PB6",))
        lg = lsc[:, 0:9]; ex = lsc[:, 16:24]
        ct = "l2f_lsc"
        b.emit("dve", lambda e, o=lg, a=pbr[:, 0:9], c=rb_bc[:]: e.tensor_add(o, a, c),
               reads=("PB6", "l2f_rbbc"), writes=(ct,))
        nm = rcol[:, 0:1]; sm = rcol[:, 1:2]; rs = rcol[:, 2:3]
        thr = rcol[:, 3:4]; eq = rcol[:, 4:5]; es = rcol[:, 5:6]
        b.emit("dve", lambda e, o=nm, i_=lg[:, 0:8]: e.reduce_max(o, i_, axis=AX.X, negate=True),
               reads=(ct,), writes=(ct + "n",))
        b.emit("act", lambda e, o=ex, i_=lg[:, 0:8], bi=nm, a=es:
               e.activation(o, i_, AF.Exp, bias=bi, accum_out=a),
               reads=(ct, ct + "n"), writes=(ct + "e", ct + "s"))
        b.emit("dve", lambda e, o=rs, i_=es: e.reciprocal(o, i_),
               reads=(ct + "s",), writes=(ct + "r",))
        b.emit("act", lambda e, o=thr, i_=lg[:, 8:9]: e.activation(o, i_, AF.Sigmoid),
               reads=(ct,), writes=(ct + "t",))
        cv = cten[:, ti * E:(ti + 1) * E]
        b.emit("dve", lambda e, o=thr, i_=thr: e.tensor_scalar_mul(o, i_, 0.125),
               reads=(ct + "t",), writes=(ct + "t",))
        b.emit("dve", lambda e, o=cv, i_=ex, s=rs, s2=thr:
               e.tensor_scalar(o, i_, s, s2, ALU.mult, ALU.subtract),
               reads=(ct + "e", ct + "r", ct + "t"), writes=("l2f_c",))
        b.emit("dve", lambda e, o=cv, i_=cv: e.tensor_relu(o, i_),
               reads=("l2f_c",), writes=("l2f_c",))
        b.emit("dve", lambda e, o=sm, i_=cv: e.reduce_sum(o, i_, axis=AX.X),
               reads=("l2f_c",), writes=(ct + "m",))
        b.emit("dve", lambda e, o=eq, i_=sm: e.tensor_scalar(o, i_, 0.0, None, ALU.is_equal),
               reads=(ct + "m",), writes=(ct + "q",))
        b.emit("dve", lambda e, o=sm, a=sm, c=eq: e.tensor_add(o, a, c),
               reads=(ct + "m", ct + "q"), writes=(ct + "m",))
        b.emit("dve", lambda e, o=rs, i_=sm: e.reciprocal(o, i_),
               reads=(ct + "m",), writes=(ct + "r",))
        b.emit("dve", lambda e, o=cv, i_=cv, s=rs: e.tensor_scalar(o, i_, s, None, ALU.mult),
               reads=("l2f_c", ct + "r"), writes=("l2f_c",))
    cem = bf_view(ctx.slab(fcsl), 0, 2048)[0:E, :]
    cemtok = ctx.stok(fcsl, "cem")
    emit_cem(ctx, cten, "l2f_c", E, cem, cemtok)
    b1t = ctx.small("l2f_b1", 128, E * 32, F32)
    for ee in range(E):
        b.emit("sp", lambda e, o=b1t[:, ee * 32:(ee + 1) * 32], i_=W["b1p"][ee]:
               e.dma_start(out=o, in_=i_), reads=(), writes=("l2f_b1",), dma="cis")
    b2t = bf_view(ctx.slab(fcsl), 2048, 1024)[0:E, :]
    b2tok = ctx.stok(fcsl, "b2")
    cload(ctx, W["b2"], b2t, b2tok, E, 1024, cstage, stage_off=512)
    ctx.free_slab(cstage)

    HT = 1024
    u1 = [[ctx.alloc_slab(), ctx.alloc_slab()] for _ in range(2)]
    u2 = [[ctx.alloc_slab(), ctx.alloc_slab()] for _ in range(2)]
    hsl = [ctx.alloc_slab(), ctx.alloc_slab()]
    for th in range(2):
        t0 = th * HT
        for ee in range(E):
            for iq in range(4):
                par = (ee * 4 + iq) % 2
                w1u = wunit_load(ctx, W["w1"], ee, 0, iq * 1024, 1024, 1024, u1[par], "wmo")
                w2u = wunit_load(ctx, W["w2"], ee, iq * 1024, 0, 1024, 1024, u2[par], "wmo")
                for icp in range(4):
                    for ci in range(8):
                        lch, lctok = ctx.ln_chunk(ci)
                        for ics in range(2):
                            ic = icp * 2 + ics
                            wv_, wtok = w1u(ci)
                            for tc2 in range(2):
                                pbm = ctx.pb[ics * 2 + tc2]
                                b.emit("pe", lambda e, o=pbm[:], l=wv_[:, ic * 128:ic * 128 + 128],
                                       r=lch[:, t0 + tc2 * 512:t0 + tc2 * 512 + 512],
                                       s=(ci == 0), pp=(ci == 7):
                                       e.matmul(o, l, r, start=s, stop=pp),
                                       reads=(wtok, lctok), writes=(f"PB{ics * 2 + tc2}",))
                    for ics in range(2):
                        ic = icp * 2 + ics
                        bi_ = b1t[:, ee * 32 + iq * 8 + ic:ee * 32 + iq * 8 + ic + 1]
                        for tc2 in range(2):
                            hv = bf_view(ctx.slab(hsl[ic // 4]), (ic % 4) * 1024 + tc2 * 512, 512)
                            htok = ctx.stok(hsl[ic // 4], f"h{ic % 4}")
                            b.emit("act", lambda e, o=hv, i_=ctx.pb[ics * 2 + tc2][:], bi=bi_:
                                   e.activation(o, i_, AF.Gelu_apprx_tanh, bias=bi),
                                   reads=(f"PB{ics * 2 + tc2}", "l2f_b1"), writes=(htok,))
                for tt in range(8):
                    ti = (t0 + tt * 128) // 128
                    xv, xtok = ctx.x_tile(ti)
                    bb0 = 4 + (tt % 2) * 2
                    for ic in range(8):
                        wv_, wtok = w2u(ic)
                        hv = bf_view(ctx.slab(hsl[ic // 4]), (ic % 4) * 1024 + tt * 128, 128)
                        htok = ctx.stok(hsl[ic // 4], f"h{ic % 4}")
                        for hh in range(2):
                            pbo = ctx.pb[bb0 + hh]
                            b.emit("pe", lambda e, o=pbo[:], l=hv, r=wv_[:, hh * 512:hh * 512 + 512],
                                   s=(ic == 0), pp=(ic == 7):
                                   e.matmul(o, l, r, start=s, stop=pp),
                                   reads=(wtok, htok), writes=(f"PB{bb0 + hh}",))
                    for hh in range(2):
                        ccol = cten[:, ti * E + ee:ti * E + ee + 1]
                        dst = xv[:, hh * 512:hh * 512 + 512]
                        b.emit("dve", lambda e, o=dst, i_=ctx.pb[bb0 + hh][:], s=ccol, i2=dst:
                               e.scalar_tensor_tensor(o, i_, s, i2, ALU.mult, ALU.add),
                               reads=(f"PB{bb0 + hh}", "l2f_c", xtok), writes=(xtok,), tag="moeacc")
    for ti in range(NT):
        xv, xtok = ctx.x_tile(ti)
        for hh in range(2):
            pbo = ctx.pb[4 + hh % 2]
            pbot = f"PB{4 + hh % 2}"
            b.emit("pe", lambda e, o=pbo[:], l=cem[:, ti * 128:(ti + 1) * 128],
                   r=b2t[:, hh * 512:hh * 512 + 512]:
                   e.matmul(o, l, r, start=True, stop=True),
                   reads=(cemtok, b2tok), writes=(pbot,))
            dst = xv[:, hh * 512:hh * 512 + 512]
            b.emit("dve", lambda e, o=dst, a=dst, c=pbo[:]: e.tensor_add(o, a, c),
                   reads=(pbot, xtok), writes=(xtok,), tag="resid")
    for s in u1[0] + u1[1] + u2[0] + u2[1] + hsl + [fcsl]:
        ctx.free_slab(s)


def emit_l3_ffn(ctx, W):
    b = ctx.bld
    emit_ln(ctx, "ln_l3f")
    E = 32
    cstage = ctx.alloc_slab()
    rw = ctx.small("l3f_rw", 128, 8 * 32, BF16)
    for ci in range(8):
        cload(ctx, W["rwT"][ci * 128:(ci + 1) * 128, :], rw[:, ci * 32:(ci + 1) * 32],
              "l3f_rw", 128, 32, cstage, stage_off=32 * (ci % 4))
    rbv = ctx.small("l3f_rb", 1, 32, BF16)
    cload(ctx, W["router_bias"], rbv[:], "l3f_rb", 1, 32, cstage, stage_off=256)
    ctx.free_slab(cstage)
    rb_bc = ctx.small("l3f_rbbc", 128, 32, F32)
    emit_bcast_row(ctx, rbv[:], "l3f_rb", rb_bc[:], "l3f_rbbc", 32)
    cten = ctx.small("l3f_c", 128, NT * E, F32)
    rcol = ctx.small("l3f_rcol", 128, 8, F32)
    top8 = ctx.small("l3f_top8", 128, 8, F32)
    lsc = ctx.small("l3f_lsc", 128, 160, F32)
    for ti in range(NT):
        pbr = ctx.pb[6]
        for ci in range(8):
            lch, lctok = ctx.ln_chunk(ci)
            b.emit("pe", lambda e, o=pbr[:, 0:32], l=lch[:, ti * 128:(ti + 1) * 128],
                   r=rw[:, ci * 32:(ci + 1) * 32], s=(ci == 0), pp=(ci == 7):
                   e.matmul(o, l, r, start=s, stop=pp),
                   reads=(lctok, "l3f_rw"), writes=("PB6",))
        sc = lsc[:, 0:32]; sfc = lsc[:, 32:64]; tmp = lsc[:, 64:96]; gsx = lsc[:, 96:104]
        gm = lsc[:, 104:108]; sk = lsc[:, 108:140]
        ct = "l3f_lsc"
        b.emit("act", lambda e, o=sc, i_=pbr[:, 0:32]: e.activation(o, i_, AF.Sigmoid),
               reads=("PB6",), writes=(ct,))
        b.emit("dve", lambda e, o=sfc, a=sc, c=rb_bc[:]: e.tensor_add(o, a, c),
               reads=(ct, "l3f_rbbc"), writes=(ct + "f",))
        for gix in range(4):
            b.emit("dve", lambda e, o=top8[:], i_=sfc[:, gix * 8:(gix + 1) * 8]: e.max(out=o, in_=i_),
                   reads=(ct + "f",), writes=("l3f_top8",))
            b.emit("dve", lambda e, o=gsx[:, gix:gix + 1], a=top8[:, 0:1], c=top8[:, 1:2]:
                   e.tensor_add(o, a, c),
                   reads=("l3f_top8",), writes=(ct + "g",))
        b.emit("dve", lambda e, o=gsx[:, 4:8]: e.memset(o, -1e30), reads=(), writes=(ct + "g",))
        b.emit("dve", lambda e, o=top8[:], i_=gsx[:]: e.max(out=o, in_=i_),
               reads=(ct + "g",), writes=("l3f_top8",))
        b.emit("dve", lambda e, o=gm, i_=gsx[:, 0:4], s=top8[:, 1:2]:
               e.tensor_scalar(o, i_, s, None, ALU.is_ge),
               reads=(ct + "g", "l3f_top8"), writes=(ct + "gm",))
        for gix in range(4):
            b.emit("dve", lambda e, o=sk[:, gix * 8:(gix + 1) * 8],
                   i_=gm[:, gix:gix + 1].to_broadcast([128, 8]):
                   e.tensor_copy(o, i_),
                   reads=(ct + "gm",), writes=(ct + "sk",))
        b.emit("dve", lambda e, o=tmp, a=sfc, c=sk[:, 0:32]: e.tensor_mul(o, a, c),
               reads=(ct + "f", ct + "sk"), writes=(ct + "tm",))
        b.emit("dve", lambda e, o=sk[:, 0:32], i_=sk[:, 0:32]:
               e.tensor_scalar(o, i_, 1e30, -1e30, ALU.mult, ALU.add),
               reads=(ct + "sk",), writes=(ct + "sk",))
        b.emit("dve", lambda e, o=tmp, a=tmp, c=sk[:, 0:32]: e.tensor_add(o, a, c),
               reads=(ct + "tm", ct + "sk"), writes=(ct + "tm",))
        b.emit("dve", lambda e, o=top8[:], i_=tmp: e.max(out=o, in_=i_),
               reads=(ct + "tm",), writes=("l3f_top8",))
        cv = cten[:, ti * E:(ti + 1) * E]
        b.emit("dve", lambda e, o=cv, i_=tmp, s=top8[:, 3:4]:
               e.tensor_scalar(o, i_, s, None, ALU.is_ge),
               reads=(ct + "tm", "l3f_top8"), writes=("l3f_c",))
        b.emit("dve", lambda e, o=cv, a=cv, c=sc: e.tensor_mul(o, a, c),
               reads=("l3f_c", ct), writes=("l3f_c",))
        nm = rcol[:, 0:1]; rs = rcol[:, 1:2]
        b.emit("dve", lambda e, o=nm, i_=cv: e.reduce_sum(o, i_, axis=AX.X),
               reads=("l3f_c",), writes=(ct + "su",))
        b.emit("dve", lambda e, o=nm, i_=nm: e.tensor_scalar_add(o, i_, 1e-20),
               reads=(ct + "su",), writes=(ct + "su",))
        b.emit("dve", lambda e, o=rs, i_=nm: e.reciprocal(o, i_),
               reads=(ct + "su",), writes=(ct + "sr",))
        b.emit("dve", lambda e, o=cv, i_=cv, s=rs, s2=2.5:
               e.tensor_scalar(o, i_, s, s2, ALU.mult, ALU.mult),
               reads=("l3f_c", ct + "sr"), writes=("l3f_c",))

    ug = [ctx.alloc_slab(), ctx.alloc_slab()]
    uu = [ctx.alloc_slab(), ctx.alloc_slab()]
    ud = [ctx.alloc_slab(), ctx.alloc_slab()]
    hsl = [ctx.alloc_slab(), ctx.alloc_slab()]
    gsl = [ctx.alloc_slab(), ctx.alloc_slab()]

    def l3_unit(dram3, ee, sl, sem):
        return big_wload(ctx, dram3, sl, sem, nch=8, ncols=512, ee=ee)

    def l3_dunit(dram3, ee, sl, sem):
        return big_wload(ctx, dram3, sl, sem, nch=4, ncols=1024, ee=ee)

    for ee in range(E):
        gacc = l3_unit(W["gate_wT"], ee, ug[ee % 2], "wmo")
        uacc = l3_unit(W["up_wT"], ee, uu[ee % 2], "wmo")
        dacc = l3_dunit(W["down_wT"], ee, ud[ee % 2], "wmo")
        for ic in range(4):
            for ci in range(8):
                wv_, wtok = gacc(ci)
                lch, lctok = ctx.ln_chunk(ci)
                for tq in range(4):
                    b.emit("pe", lambda e, o=ctx.pb[tq][:], l=wv_[:, ic * 128:ic * 128 + 128],
                           r=lch[:, tq * 512:(tq + 1) * 512], s=(ci == 0), pp=(ci == 7):
                           e.matmul(o, l, r, start=s, stop=pp),
                           reads=(wtok, lctok), writes=(f"PB{tq}",))
            for ci in range(8):
                wv_, wtok = uacc(ci)
                lch, lctok = ctx.ln_chunk(ci)
                for tq in range(4):
                    b.emit("pe", lambda e, o=ctx.pb[4 + tq][:], l=wv_[:, ic * 128:ic * 128 + 128],
                           r=lch[:, tq * 512:(tq + 1) * 512], s=(ci == 0), pp=(ci == 7):
                           e.matmul(o, l, r, start=s, stop=pp),
                           reads=(wtok, lctok), writes=(f"PB{4 + tq}",))
            for tq in range(4):
                gv = bf_view(ctx.slab(gsl[ic // 2]), (ic % 2) * 2048 + tq * 512, 512)
                gtok = ctx.stok(gsl[ic // 2], f"g{ic % 2}")
                b.emit("act", lambda e, o=gv, i_=ctx.pb[tq][:]: e.activation(o, i_, AF.Silu),
                       reads=(f"PB{tq}",), writes=(gtok,))
                hv = bf_view(ctx.slab(hsl[ic // 2]), (ic % 2) * 2048 + tq * 512, 512)
                htok = ctx.stok(hsl[ic // 2], f"h{ic % 2}")
                b.emit("dve", lambda e, o=hv, a=ctx.pb[4 + tq][:], c=gv: e.tensor_mul(o, a, c),
                       reads=(f"PB{4 + tq}", gtok), writes=(htok,))
        for tt in range(NT):
            xv, xtok = ctx.x_tile(tt)
            bb0 = (tt % 2) * 2
            for ic in range(4):
                wv_, wtok = dacc(ic)
                hv = bf_view(ctx.slab(hsl[ic // 2]), (ic % 2) * 2048 + tt * 128, 128)
                htok = ctx.stok(hsl[ic // 2], f"h{ic % 2}")
                for hh in range(2):
                    b.emit("pe", lambda e, o=ctx.pb[bb0 + hh][:], l=hv, r=wv_[:, hh * 512:hh * 512 + 512],
                           s=(ic == 0), pp=(ic == 3):
                           e.matmul(o, l, r, start=s, stop=pp),
                           reads=(wtok, htok), writes=(f"PB{bb0 + hh}",))
            for hh in range(2):
                ccol = cten[:, tt * E + ee:tt * E + ee + 1]
                dst = xv[:, hh * 512:hh * 512 + 512]
                b.emit("dve", lambda e, o=dst, i_=ctx.pb[bb0 + hh][:], s=ccol, i2=dst:
                       e.scalar_tensor_tensor(o, i_, s, i2, ALU.mult, ALU.add),
                       reads=(f"PB{bb0 + hh}", "l3f_c", xtok), writes=(xtok,), tag="moeacc")
    for s_ in (ug[0], ug[1], uu[0], uu[1], ud[0], ud[1]):
        ctx.free_slab(s_)
    sg_u = [ctx.alloc_slab(), ctx.alloc_slab()]
    su_u = [ctx.alloc_slab(), ctx.alloc_slab()]
    sd_u = [ctx.alloc_slab(), ctx.alloc_slab()]
    for th in range(2):
        t0 = th * 1024
        sgacc = wunit_load(ctx, W["sgwT"], None, 0, 0, 1024, 1024, sg_u, "wmo")
        suacc = wunit_load(ctx, W["suwT"], None, 0, 0, 1024, 1024, su_u, "wmo")
        sdacc = wunit_load(ctx, W["sdwT"], None, 0, 0, 1024, 1024, sd_u, "wmo")
        for ic in range(8):
            for ci in range(8):
                wv_, wtok = sgacc(ci)
                lch, lctok = ctx.ln_chunk(ci)
                for tq in range(2):
                    b.emit("pe", lambda e, o=ctx.pb[tq][:], l=wv_[:, ic * 128:ic * 128 + 128],
                           r=lch[:, t0 + tq * 512:t0 + tq * 512 + 512], s=(ci == 0), pp=(ci == 7):
                           e.matmul(o, l, r, start=s, stop=pp),
                           reads=(wtok, lctok), writes=(f"PB{tq}",))
            for ci in range(8):
                wv_, wtok = suacc(ci)
                lch, lctok = ctx.ln_chunk(ci)
                for tq in range(2):
                    b.emit("pe", lambda e, o=ctx.pb[2 + tq][:], l=wv_[:, ic * 128:ic * 128 + 128],
                           r=lch[:, t0 + tq * 512:t0 + tq * 512 + 512], s=(ci == 0), pp=(ci == 7):
                           e.matmul(o, l, r, start=s, stop=pp),
                           reads=(wtok, lctok), writes=(f"PB{2 + tq}",))
            for tq in range(2):
                gv = bf_view(ctx.slab(gsl[ic // 4]), (ic % 4) * 1024 + tq * 512, 512)
                gtok = ctx.stok(gsl[ic // 4], f"sg{ic % 4}")
                b.emit("act", lambda e, o=gv, i_=ctx.pb[tq][:]: e.activation(o, i_, AF.Silu),
                       reads=(f"PB{tq}",), writes=(gtok,))
                hv = bf_view(ctx.slab(hsl[ic // 4]), (ic % 4) * 1024 + tq * 512, 512)
                htok = ctx.stok(hsl[ic // 4], f"sh{ic % 4}")
                b.emit("dve", lambda e, o=hv, a=ctx.pb[2 + tq][:], c=gv: e.tensor_mul(o, a, c),
                       reads=(f"PB{2 + tq}", gtok), writes=(htok,))
        for tt in range(8):
            ti = (t0 + tt * 128) // 128
            xv, xtok = ctx.x_tile(ti)
            bb0 = 4 + (tt % 2) * 2
            for ic in range(8):
                wv_, wtok = sdacc(ic)
                hv = bf_view(ctx.slab(hsl[ic // 4]), (ic % 4) * 1024 + tt * 128, 128)
                htok = ctx.stok(hsl[ic // 4], f"sh{ic % 4}")
                for hh in range(2):
                    b.emit("pe", lambda e, o=ctx.pb[bb0 + hh][:], l=hv, r=wv_[:, hh * 512:hh * 512 + 512],
                           s=(ic == 0), pp=(ic == 7):
                           e.matmul(o, l, r, start=s, stop=pp),
                           reads=(wtok, htok), writes=(f"PB{bb0 + hh}",))
            for hh in range(2):
                dst = xv[:, hh * 512:hh * 512 + 512]
                b.emit("dve", lambda e, o=dst, a=dst, c=ctx.pb[bb0 + hh][:]: e.tensor_add(o, a, c),
                       reads=(f"PB{bb0 + hh}", xtok), writes=(xtok,), tag="resid")
    for s in [hsl[0], hsl[1], gsl[0], gsl[1]] + sg_u + su_u + sd_u:
        ctx.free_slab(s)


def _colprep(v):
    return np.ascontiguousarray(np.asarray(v, dtype=np.float32).reshape(8, 128).T)


def host_prep(inputs):
    P = {}
    g = lambda k: np.ascontiguousarray(np.asarray(inputs[k], dtype=np.float32))
    tr = lambda k: np.ascontiguousarray(np.asarray(inputs[k], dtype=np.float32).T)
    P["l0a"] = {
        "w": _colprep(inputs["L0_attn_w"]), "u": _colprep(inputs["L0_attn_u"]),
        "mk": _colprep(inputs["L0_attn_mix_k"]), "mv": _colprep(inputs["L0_attn_mix_v"]),
        "mr": _colprep(inputs["L0_attn_mix_r"]),
        "wkT": tr("L0_attn_Wk"), "wvT": tr("L0_attn_Wv"),
        "wrT": tr("L0_attn_Wr"), "woT": tr("L0_attn_Wo"),
    }
    def headmaj_T(k):
        w = np.asarray(inputs[k], dtype=np.float32).reshape(1024, 1024)
        return np.ascontiguousarray(w.T)
    pek_ = np.asarray(inputs["L1_attn_pek"], np.float32)
    pev_ = np.asarray(inputs["L1_attn_pev"], np.float32)
    P["l1a"] = {
        "wqT": tr("L1_attn_Wq"), "wkT": tr("L1_attn_Wk"),
        "wvT": tr("L1_attn_Wv"), "woT": tr("L1_attn_Wo"),
        "pekT": np.ascontiguousarray(pek_[0:1025].T),
        "pev_aug": np.ascontiguousarray(np.concatenate(
            [pev_[1:1025], np.ones((1024, 1), np.float32)], axis=1)),
        "pev0": np.ascontiguousarray(pev_[0:1]),
    }
    P["l2a"] = {
        "wqT": headmaj_T("L2_attn_W_Q"), "wkT": headmaj_T("L2_attn_W_K"),
        "wvT": headmaj_T("L2_attn_W_V"), "woT": tr("L2_attn_W_O"),
    }
    perm = []
    for p_ in range(8):
        for h_ in (2 * p_, 2 * p_ + 1):
            perm.extend([64 * h_ + 2 * i_ for i_ in range(32)])
        for h_ in (2 * p_, 2 * p_ + 1):
            perm.extend([64 * h_ + 2 * i_ + 1 for i_ in range(32)])
    perm = np.asarray(perm)
    P["l3a"] = {
        "wqT": np.ascontiguousarray(np.asarray(inputs["L3_attn_Wq"], np.float32).T[:, perm]),
        "wkT": np.ascontiguousarray(np.asarray(inputs["L3_attn_Wk"], np.float32).T[:, perm]),
        "wvT": tr("L3_attn_Wv"), "woT": tr("L3_attn_Wo"),
    }
    def _b1prep(b1):
        b1 = np.asarray(b1, np.float32)
        E_, I_ = b1.shape
        return np.ascontiguousarray(b1.reshape(E_, I_ // 128, 128).transpose(0, 2, 1))
    P["l1f"] = {
        "router_w": g("L1_ffn_router_w"),
        "router_b": np.asarray(inputs["L1_ffn_router_b"], np.float32).reshape(1, 16),
        "w1": g("L1_ffn_w1"), "b1p": _b1prep(inputs["L1_ffn_b1"]),
        "w2": g("L1_ffn_w2"), "b2": g("L1_ffn_b2"),
    }
    P["l2f"] = {
        "wgt": np.ascontiguousarray(np.concatenate(
            [np.asarray(inputs["L2_ffn_Wg"], np.float32),
             np.asarray(inputs["L2_ffn_Wt"], np.float32)], axis=1)),
        "bgt": np.ascontiguousarray(np.concatenate(
            [np.asarray(inputs["L2_ffn_bg"], np.float32),
             np.asarray(inputs["L2_ffn_bt"], np.float32)]).reshape(1, 9)),
        "w1": g("L2_ffn_W1"), "b1p": _b1prep(inputs["L2_ffn_b1"]),
        "w2": g("L2_ffn_W2"), "b2": g("L2_ffn_b2"),
    }
    P["l3f"] = {
        "rwT": tr("L3_ffn_router_weight"),
        "router_bias": np.asarray(inputs["L3_ffn_router_bias"], np.float32).reshape(1, 32),
        "gate_wT": np.ascontiguousarray(np.asarray(inputs["L3_ffn_gate_w"], np.float32).transpose(0, 2, 1)),
        "up_wT": np.ascontiguousarray(np.asarray(inputs["L3_ffn_up_w"], np.float32).transpose(0, 2, 1)),
        "down_wT": np.ascontiguousarray(np.asarray(inputs["L3_ffn_down_w"], np.float32).transpose(0, 2, 1)),
        "sgwT": tr("L3_ffn_shared_gate_w"), "suwT": tr("L3_ffn_shared_up_w"),
        "sdwT": tr("L3_ffn_shared_down_w"),
    }
    P["l0f"] = {
        "mur": _colprep(inputs["L0_ffn_mu_r"]), "muk": _colprep(inputs["L0_ffn_mu_k"]),
        "wrT": tr("L0_ffn_W_r"), "wkT": tr("L0_ffn_W_k"), "wvT": tr("L0_ffn_W_v"),
    }
    return P


def declare_weights(nc, wd, prefix):
    aps, inmap = {}, {}
    for k, v in wd.items():
        t = nc.dram_tensor(f"{prefix}_{k}", list(v.shape), F32, kind="ExternalInput")
        aps[k] = t.ap()
        inmap[f"{prefix}_{k}"] = v
    return aps, inmap


POOL_N = {"l0a": 12, "l0f": 12}


def build_block(emit_fns, wds, prefixes):
    nc = bass.Bass("TRN2", target_bir_lowering=False, debug=False)
    x_in = nc.dram_tensor("x_in", [T, C], F32, kind="ExternalInput").ap()
    x_out = nc.dram_tensor("x_out", [T, C], F32, kind="ExternalOutput").ap()
    inmap_all = {}
    wapss = []
    for wd, prefix in zip(wds, prefixes):
        waps, inmap_w = declare_weights(nc, wd, prefix)
        inmap_all.update(inmap_w)
        wapss.append(waps)
    bld = Bld(nc)
    if len(prefixes) > 1:
        npool = 12
    else:
        npool = min(POOL_N.get(p, 11) for p in prefixes)
    ctx = Ctx(nc, bld, n_pool=npool)
    identbf = nc.inline_tensor(np.eye(128, dtype=np.float32).astype(ml_dtypes.bfloat16), "identbf_c")
    idt = ctx.small("identbf", 128, 128, BF16)
    bld.emit("sp", lambda e: e.dma_start(out=idt[:], in_=identbf.ap()),
             reads=(), writes=("identbf",), dma="cis")
    identfc = nc.inline_tensor(np.eye(128, dtype=np.float32), "identf32_c")
    idtf = ctx.small("identf32", 128, 128, F32)
    bld.emit("sp", lambda e: e.dma_start(out=idtf[:], in_=identfc.ap()),
             reads=(), writes=("identf32",), dma="cis")
    emit_load_x(ctx, x_in)
    for emit_fn, waps in zip(emit_fns, wapss):
        emit_fn(ctx, waps)
    emit_store_x(ctx, x_out)
    bld.finalize()
    return nc, inmap_all


BLOCKS = [
    ("l0a", emit_l0_attn),
    ("l0f", emit_l0_ffn),
    ("l1a", emit_l1_attn),
    ("l1f", emit_l1_ffn),
    ("l2a", emit_l2_attn),
    ("l2f", emit_l2_ffn),
    ("l3a", emit_l3_attn),
    ("l3f", emit_l3_ffn),
]


def run_blocks(xs, P, blocks):
    for name, emit_fn in blocks:
        nc, inmap_all = build_block([emit_fn], [P[name]], [name])
        in_maps = [dict(inmap_all, x_in=xs[c]) for c in range(8)]
        res = run_bass_kernel_spmd(nc, in_maps, core_ids=list(range(8)))
        xs = [np.asarray(res.results[c]["x_out"], dtype=np.float32) for c in range(8)]
    return xs


def kernel(**inputs):
    import os
    P = host_prep(inputs)
    x_full = np.asarray(inputs["L0_attn_x"], dtype=np.float32)
    xs = [np.ascontiguousarray(x_full[bb]) for bb in range(8)]
    if os.environ.get("UNFUSED") == "1":
        xs = run_blocks(xs, P, BLOCKS)
        return np.stack(xs, axis=0)
    names = [n for n, _ in BLOCKS]
    fns = [f for _, f in BLOCKS]
    nc, inmap_all = build_block(fns, [P[n] for n in names], names)
    in_maps = [dict(inmap_all, x_in=xs[c]) for c in range(8)]
    res = run_bass_kernel_spmd(nc, in_maps, core_ids=list(range(8)))
    return np.stack([np.asarray(res.results[c]["x_out"], dtype=np.float32) for c in range(8)], axis=0)
